# version 03
# baseline (speedup 1.0000x reference)
import contextlib
import numpy as np

import concourse.bass as bass
import concourse.mybir as mybir
from concourse.bass_utils import run_bass_kernel_spmd

F32 = mybir.dt.float32
BF16 = mybir.dt.bfloat16
I32 = mybir.dt.int32
AF = mybir.ActivationFunctionType
ALU = mybir.AluOpType
AX = mybir.AxisListType

N_CORES = 8
N_SEQ = 4
S = 2048
D = 1024
NT = 16
KT = 8
M32C = 12582912.0
M64C = 1.5 * 2.0 ** 52
EPS = 1e-5
SCALE_A = 0.03125
SCALE_B = 0.125
SUBLAYERS = ("a", "f1", "b", "f2")

ENGINES = ("sync", "tensor", "vector", "scalar", "gpsimd")


class Prog:

    def __init__(self):
        self.q = {e: [] for e in ENGINES}
        self.cnt = {}

    def op(self, eng, fn, wait=(), inc=None, amt=1):
        tok = None
        incs = ()
        if inc is not None:
            self.cnt[inc] = self.cnt.get(inc, 0) + amt
            tok = (inc, self.cnt[inc])
            incs = ((inc, self.cnt[inc], amt),)
        self.q[eng].append((tuple(w for w in wait if w is not None), fn, incs))
        return tok

    def barrier(self):
        self.op("sync", lambda E: E.nop(),
                wait=(("dIn", self.cnt.get("dIn", 0)),))
        self.op("gpsimd", lambda E: E.nop(),
                wait=(("dOut", self.cnt.get("dOut", 0)),))
        toks = {}
        for e in ENGINES:
            toks[e] = self.op(e, lambda E: E.nop(), inc="m_" + e)
        for e in ENGINES:
            waits = tuple(toks[o] for o in ENGINES if o != e)
            self.op(e, lambda E: E.nop(), wait=waits)

    def replay(self, block, sems):
        for eng_name in ENGINES:
            items = self.q[eng_name]

            def body(eng, items=items):
                waited = {}
                for waits, fn, incs in items:
                    for s_, v in waits:
                        if waited.get(s_, -1) < v:
                            eng.wait_ge(sems[s_], v)
                            waited[s_] = v
                    inst = fn(eng)
                    for s_, _v, amt in incs:
                        inst.then_inc(sems[s_], amt)
            getattr(block, eng_name)(body)


def build_core_program(sublayers=SUBLAYERS, n_seq=N_SEQ, debug_ffn=False, debug_a=False):
    nc = bass.Bass("TRN2", target_bir_lowering=False,
                   detect_race_conditions=False)
    P = Prog()

    x_d = nc.dram_tensor("x", [n_seq * S, D], F32, kind="ExternalInput").ap()
    y_d = nc.dram_tensor("y", [n_seq * S, D], F32, kind="ExternalOutput").ap()
    W_NAMES = ("wq_a", "wk_a", "wv_a", "wo_a", "wq_b", "wk_b", "wv_b", "wo_b")
    W_IDX = {w: i for i, w in enumerate(W_NAMES)}
    win = {w: nc.dram_tensor(w, [D, D], F32, kind="ExternalInput").ap()
           for w in W_NAMES}
    B_NAMES = ("bq_a", "bk_a", "bv_a", "bo_a", "bq_b", "bk_b", "bv_b", "bo_b", "fb")
    bin_ = {b: nc.dram_tensor(b, [1, D], F32, kind="ExternalInput").ap()
            for b in B_NAMES}
    tw_d = nc.dram_tensor("tw", [D, D], I32, kind="ExternalInput").ap()
    scal_d = nc.dram_tensor("ascale", [1, 1], F32, kind="ExternalInput").ap()
    decay_d = nc.dram_tensor("decay", [1, 16], F32, kind="ExternalInput").ap()

    dbg = {}
    if debug_ffn:
        dbg["xit"] = nc.dram_tensor("dbg_xit", [D, S], BF16,
                                    kind="ExternalOutput").ap()
        dbg["mc"] = nc.dram_tensor("dbg_mc", [128, NT], F32,
                                   kind="ExternalOutput").ap()
        dbg["rc"] = nc.dram_tensor("dbg_rc", [128, NT], F32,
                                   kind="ExternalOutput").ap()
        dbg["sc"] = nc.dram_tensor("dbg_sc", [2, 128], F32,
                                   kind="ExternalOutput").ap()
        dbg["wt"] = nc.dram_tensor("dbg_wt", [D, D], BF16,
                                   kind="ExternalOutput").ap()
        dbg["fbs"] = nc.dram_tensor("dbg_fbs", [1, D], F32,
                                    kind="ExternalOutput").ap()
        dbg["fs"] = nc.dram_tensor("dbg_fs", [4 * 128, D], F32,
                                   kind="ExternalOutput").ap()
        dbg["xqb"] = nc.dram_tensor("dbg_xqb", [128, D], BF16,
                                    kind="ExternalOutput").ap()
        dbg["qf"] = nc.dram_tensor("dbg_qf", [128, 1], F32,
                                   kind="ExternalOutput").ap()
        dbg["cur"] = nc.dram_tensor("dbg_cur", [3, 128], F32,
                                    kind="ExternalOutput").ap()
        dbg["qm1"] = nc.dram_tensor("dbg_qm1", [128, 1], F32,
                                    kind="ExternalOutput").ap()
        dbg["xin1"] = nc.dram_tensor("dbg_xin1", [128, D], F32,
                                     kind="ExternalOutput").ap()
    if debug_a:
        for nm, shp, dt_ in (("a_lth", [D, S], BF16), ("a_ltl", [D, S], BF16),
                             ("a_kh", [D, S], BF16), ("a_kl", [D, S], BF16),
                             ("a_qh", [D, S], BF16), ("a_ql", [D, S], BF16),
                             ("a_at", [2048, 256], F32),
                             ("a_e0", [128, S], F32), ("a_e1", [128, S], F32),
                             ("a_r8", [2, 1024], F32),
                             ("a_cbh", [D, S], BF16), ("a_cbl", [D, S], BF16)):
            dbg[nm] = nc.dram_tensor("dbg_" + nm, shp, dt_,
                                     kind="ExternalOutput").ap()
    scr_w = nc.dram_tensor("scr_w", [16 * D, D], BF16).ap()
    scr_wt = nc.dram_tensor("scr_wt", [D, D], BF16).ap()
    scr_v = nc.dram_tensor("scr_v", [S, D], F32).ap()
    scr_x = [nc.dram_tensor(f"scr_x{i}", [S, D], F32).ap() for i in range(3)]

    def sb(name, shape, dtype, off):
        return nc.alloc_sbuf_tensor_at(name, shape, dtype, offset=off)

    def sbm(name, shape, dtype):
        return nc.alloc_sbuf_tensor(name, shape, dtype)

    ones_row = sbm("ones_row", [1, 128], F32)
    I_f = sbm("I_f", [128, 128], F32)
    I_b = sbm("I_b", [128, 128], BF16)
    iocol = sbm("iocol", [128, 1], F32)
    iof = sbm("iof", [128, 128], F32)
    qflag = sbm("qflag", [128, 1], F32)
    qm1 = sbm("qm1", [128, 1], F32)
    zcol = sbm("zcol", [128, 8], F32)
    zcol_b = sbm("zcol_b", [128, 8], BF16)
    sdf = sbm("sdf", [128, 16], F32)
    bcols = {b: sbm("c_" + b, [128, KT], F32)
             for b in ("bq_a", "bk_a", "bq_b", "bk_b")}
    brows = {b: sbm("r_" + b, [1, D], F32)
             for b in ("bv_a", "bo_a", "bv_b", "bo_b", "fb")}
    brows["fbs"] = brows["fb"]
    s_bc = sbm("s_bc", [128, 1], F32)
    srec_bc = sbm("srec_bc", [128, 1], F32)
    sums_c = sbm("sums_c", [128, NT], F32)
    sumsq_c = sbm("sumsq_c", [128, NT], F32)
    m_c = sbm("m_c", [128, NT], F32)
    r_c = sbm("r_c", [128, NT], F32)
    rs_c = sbm("rs_c", [128, NT], F32)
    var_c = sbm("var_c", [128, NT], F32)
    CURS = ("m", "r", "rs", "bias", "negM", "rv", "sd", "b")
    cur = {k: sbm("cur_" + k, [128, 1], F32) for k in CURS}
    sums8 = sbm("sums8", [128, 8], F32)
    tmp8 = sbm("tmp8", [128, 8], F32)
    rinv8 = sbm("rinv8", [128, 8], F32)
    rinv8x2 = sbm("rinv8x2", [128, 8], F32)
    ln2c = sbm("ln2c", [128, 1], F32)
    sums8b = sbm("sums8b", [128, 8], F32)
    rinv8b = sbm("rinv8b", [128, 8], F32)
    Dbuf = [sbm(f"Dbuf{i}", [128, 128], F32) for i in range(2)]
    Abuf = sbm("Abuf", [128, 132], F32)
    ATb = sbm("ATb", [128, 132], F32)
    ATp = sbm("ATp", [1, 132], F32)
    kbound_h = sbm("kbound_h", [128, KT], BF16)
    kbound_l = sbm("kbound_l", [128, KT], BF16)
    vbound = sbm("vbound", [1, D], F32)
    lnd_bc = sbm("lnd_bc", [128, 16], F32)
    assert nc.SBUF_PARTITION_SIZE_BYTES - nc.sbuf_bytes_remaining // 1 >= 0

    OFF_XIN = 49120
    OFF_XOUT = 57312
    OFF_WA = 65504
    OFF_BIG1 = 98272
    OFF_BIG2 = 163808
    assert OFF_BIG2 + 64 * 1024 <= 229344

    xin = sb("xin", [128, 2, D], F32, OFF_XIN)
    xout = sb("xout", [128, 2, D], F32, OFF_XOUT)
    STS0 = sb("STS0", [128, D], F32, OFF_WA)
    STS1 = sb("STS1", [128, D], F32, OFF_WA + 4 * 1024)
    FS = [sb(f"FS{i}", [128, D], F32, OFF_WA + i * 4 * 1024) for i in range(4)]
    LNh = sb("LNh", [128, 2, D], BF16, OFF_WA + 8 * 1024)
    LNl = sb("LNl", [128, 2, D], BF16, OFF_WA + 12 * 1024)
    xqb = sb("xqb", [128, D], BF16, OFF_WA)
    wa_h = sb("wa_h", [128, KT, D], BF16, OFF_WA)
    wa_l = sb("wa_l", [128, KT, D], BF16, OFF_WA + 16 * 1024)
    wt_f = sb("wt_f", [128, KT, D], BF16, OFF_WA + 16 * 1024)
    Ebuf = [sb("Ebuf0", [128, S], F32, OFF_WA),
            sb("Ebuf1", [128, S], F32, OFF_WA + 8 * 1024)]
    ATa = sb("ATa", [128, 16, 256], F32, OFF_WA + 16 * 1024)

    LTh = sb("LTh", [128, KT, S], BF16, OFF_BIG1)
    LTl = sb("LTl", [128, KT, S], BF16, OFF_BIG1 + 32 * 1024)
    QSh = sb("QSh", [128, KT, 512], BF16, OFF_XIN)
    QSl = sb("QSl", [128, KT, 512], BF16, OFF_XOUT)
    XIT = sb("XIT", [128, KT, S], BF16, OFF_BIG1)
    LTbh = sb("LTbh", [128, KT, 512], BF16, OFF_BIG1)
    LTbl = sb("LTbl", [128, KT, 512], BF16, OFF_BIG1 + 8 * 1024)
    QSbh = sb("QSbh", [128, KT, 512], BF16, OFF_BIG1 + 16 * 1024)
    QSbl = sb("QSbl", [128, KT, 512], BF16, OFF_BIG1 + 24 * 1024)
    Eall = sb("Eall", [128, 4, 132], F32, OFF_BIG1 + 32 * 1024)
    vprev = sb("vprev", [1, 4, D], F32, OFF_BIG1 + 35 * 1024)

    Kh = sb("Kh", [128, KT, S], BF16, OFF_BIG2)
    Kl = sb("Kl", [128, KT, S], BF16, OFF_BIG2 + 32 * 1024)
    CBh = sb("CBh", [128, KT, S], BF16, OFF_BIG2)
    CBl = sb("CBl", [128, KT, S], BF16, OFF_BIG2 + 32 * 1024)
    kbh = sb("kbh", [128, KT, 512], BF16, OFF_BIG2)
    kbl = sb("kbl", [128, KT, 512], BF16, OFF_BIG2 + 8 * 1024)
    cbh_b = sb("cbh_b", [128, KT, 512], BF16, OFF_BIG2 + 16 * 1024)
    cbl_b = sb("cbl_b", [128, KT, 512], BF16, OFF_BIG2 + 24 * 1024)
    vb = [sb("vb0", [128, 2, D], F32, OFF_XIN),
          sb("vb1", [128, 2, D], F32, OFF_XOUT)]
    WS32 = sb("WS32", [128, KT, D], F32, OFF_BIG1)
    WSI = sb("WSI", [128, KT, D], I32, OFF_BIG1 + 32 * 1024)
    WSH = sb("WSH", [128, KT, D], BF16, OFF_BIG2)
    WSL = sb("WSL", [128, KT, D], BF16, OFF_BIG2 + 16 * 1024)

    PSA = nc.alloc_psum_tensor("PSA", [128, 2048], F32)
    PSB = nc.alloc_psum_tensor("PSB", [128, 512], F32)
    PSC = nc.alloc_psum_tensor("PSC", [128, 512], F32)
    PEA = nc.alloc_psum_tensor("PEA", [128, 1024], BF16)
    PEB = nc.alloc_psum_tensor("PEB", [128, 1024], BF16)

    def ld(out_ap, in_ap, wait=()):
        return P.op("sync", lambda E, o=out_ap, i=in_ap:
                    E.dma_start(out=o, in_=i), wait, inc="dIn", amt=16)

    def st(out_ap, in_ap, wait=()):
        return P.op("gpsimd", lambda E, o=out_ap, i=in_ap:
                    E.dma_start(out=o, in_=i), wait, inc="dOut", amt=16)

    def mm(ps, lhsT, rhs, start, stop, wait=(), inc=None, tp=None, tr=False):
        def f(E, o=ps, l_=lhsT, r_=rhs, st_=start, sp=stop, tp_=tp, t_=tr):
            if t_:
                return E.matmul(out=o, lhsT=l_, rhs=r_, start=st_, stop=sp,
                                tile_position=tp_, is_transpose=True)
            return E.matmul(out=o, lhsT=l_, rhs=r_, start=st_, stop=sp,
                            tile_position=tp_)
        return P.op("tensor", f, wait, inc)

    def v_op(fn, wait=(), inc=None):
        return P.op("vector", fn, wait, inc)

    def s_op(fn, wait=(), inc=None):
        return P.op("scalar", fn, wait, inc)

    def vcopy(dst, src, wait=(), inc=None):
        return v_op(lambda E, d=dst, s_=src: E.tensor_copy(out=d, in_=s_),
                    wait, inc)

    def vts(dst, src, scalar, opc, wait=(), inc=None):
        return v_op(lambda E, d=dst, s_=src, sc=scalar, o=opc:
                    E.tensor_single_scalar(out=d, in_=s_, scalar=sc, op=o),
                    wait, inc)

    def vtt(dst, a, b, opc, wait=(), inc=None):
        return v_op(lambda E, d=dst, a_=a, b_=b, o=opc:
                    E.tensor_tensor(out=d, in0=a_, in1=b_, op=o), wait, inc)

    def vstt(dst, in0, scalar, in1, op0, op1, wait=(), inc=None):
        return v_op(lambda E, d=dst, i0=in0, sc=scalar, i1=in1, o0=op0, o1=op1:
                    E.scalar_tensor_tensor(out=d, in0=i0, scalar=sc, in1=i1,
                                           op0=o0, op1=o1), wait, inc)

    def scopy(dst, src, wait=(), inc=None):
        return s_op(lambda E, d=dst, s_=src: E.copy(out=d, in_=s_), wait, inc)

    def sact(dst, src, func, bias=0.0, scale=1.0, accum=None, wait=(), inc=None):
        return s_op(lambda E, d=dst, s_=src, f=func, b=bias, sc=scale, a=accum:
                    E.activation(out=d, in_=s_, func=f, bias=b, scale=sc,
                                 accum_out=a), wait, inc)

    def w_ap(idx, hilo):
        return scr_w[(2 * idx + hilo) * D:(2 * idx + hilo + 1) * D, :].rearrange(
            "(kt p) d -> p kt d", p=128)

    def load_weight(wname):
        idx = W_IDX[wname]
        t1 = ld(wa_h[:], w_ap(idx, 0))
        t2 = ld(wa_l[:], w_ap(idx, 1))
        return (t1, t2)

    def emit_init():
        P.op("gpsimd", lambda E: E.iota(out=iocol[:], pattern=[[0, 1]], base=0,
                                        channel_multiplier=1,
                                        allow_small_or_imprecise_dtypes=True))
        tg = P.op("gpsimd", lambda E: E.iota(out=iof[:], pattern=[[1, 128]],
                                             base=0, channel_multiplier=0,
                                             allow_small_or_imprecise_dtypes=True),
                  inc="g")
        v_op(lambda E: E.memset(ones_row[:], 1.0))
        v_op(lambda E: E.memset(zcol[:], 0.0))
        tz = v_op(lambda E: E.memset(zcol_b[:], 0.0), inc="v")
        vts(qflag[:], zcol[:, 0:1], 1.0, ALU.add, wait=(tz,))
        vts(ln2c[:], zcol[:, 0:1], 0.6931471805599453, ALU.add)
        v_op(lambda E: E.tensor_scalar(out=qflag[:], in0=qflag[:], scalar1=M64C,
                                       scalar2=-M64C, op0=ALU.add, op1=ALU.add))
        vts(I_f[:], iof[:], iocol[:, 0:1], ALU.is_equal, wait=(tg,))
        vcopy(I_b[:], I_f[:], inc="v")
        v_op(lambda E: E.tensor_scalar(out=qm1[:], in0=qflag[:], scalar1=-1.0,
                                       scalar2=1.0, op0=ALU.mult, op1=ALU.add))

        tb = []
        for b in ("bv_a", "bo_a", "bv_b", "bo_b", "fb"):
            tb.append(ld(brows[b][:], bin_[b][:]))
        prev_ev = None
        for b in bcols:
            trow = ld(vbound[0:1, :], bin_[b][:],
                      wait=(prev_ev,) if prev_ev else ())
            tg2 = None
            for mt in range(KT):
                tg2 = mm(PSC[:, mt:mt + 1], vbound[0:1, mt * 128:(mt + 1) * 128],
                         ones_row[0:1, 0:1], mt == 0, mt == KT - 1,
                         wait=(trow,) if mt == 0 else (),
                         inc="t" if mt == KT - 1 else None)
            prev_ev = scopy(bcols[b][:], PSC[:, 0:KT], wait=(tg2,), inc="s")
        tb.append(prev_ev)
        t_sc = ld(cur["m"][0:1, 0:1], scal_d[:])
        t_dec = ld(lnd_bc[0:1, 0:16], decay_d[:])

        tv = vts(cur["m"][0:1, 0:1], cur["m"][0:1, 0:1], 1e-5, ALU.max,
                 wait=(t_sc,), inc="v")
        tmm = mm(PSB[:, 0:1], ones_row[:, 0:128], cur["m"][0:1, 0:1], True, True,
                 wait=(tv,), inc="t")
        ts1 = scopy(s_bc[:], PSB[:, 0:1], wait=(tmm,), inc="s")
        ts1b = sact(srec_bc[:], s_bc[:], AF.Ln, wait=(ts1,), inc="s")
        ts2 = sact(srec_bc[:], srec_bc[:], AF.Exp, scale=-1.0, wait=(ts1b,),
                   inc="s")
        vts(brows["fb"][:], brows["fb"][:], srec_bc[0:1, 0:1], ALU.mult,
            wait=tuple(tb) + (ts2,), inc="v")

        tl = sact(lnd_bc[0:1, 0:16], lnd_bc[0:1, 0:16], AF.Ln, wait=(t_dec,),
                  inc="s")
        tmm2 = mm(PSB[:, 8:24], ones_row[:, 0:128], lnd_bc[0:1, 0:16], True, True,
                  wait=(tl, ts1), inc="t")
        tl2 = scopy(lnd_bc[:], PSB[:, 8:24], wait=(tmm2,), inc="s")
        tl3 = sact(sdf[:], lnd_bc[:], AF.Exp, scale=iocol[:, 0:1], wait=(tl2,),
                   inc="s")
        vts(sdf[:], sdf[:], SCALE_B, ALU.mult, wait=(tl3,), inc="v")
        P.barrier()

        tld = ld(WSI[:], tw_d.rearrange("(kt p) d -> p kt d", p=128))
        tv1 = vcopy(WS32[:], WSI[:], wait=(tld,), inc="v")
        tv2 = vts(WSH[:], WS32[:], -1.0, ALU.add, wait=(tv1,), inc="v")
        ev_toks = []
        pe_ev = {}
        for okt in range(KT):
            for g in range(2):
                pe = PEA if g == 0 else PEB
                key = (okt * 2 + g) % 2
                tts = None
                for j in range(4):
                    ikt = g * 4 + j
                    w = (tv2,)
                    if okt * 2 + g >= 2:
                        w = (tv2, pe_ev.get(okt * 2 + g - 2))
                    tts = mm(pe[:, j * 128:(j + 1) * 128],
                             WSH[:, okt, ikt * 128:(ikt + 1) * 128], I_b[:],
                             j == 0, j == 3, wait=w,
                             inc="t" if j == 3 else None, tr=True)
                dst = WSL[:, g * 4:(g + 1) * 4, okt * 128:okt * 128 + 128]
                src = pe[:, 0:512].rearrange("p (a b) -> p a b", a=4)
                ev = v_op(lambda E, d=dst, s_=src:
                          E.tensor_copy(out=d, in_=s_), wait=(tts,), inc="v")
                pe_ev[okt * 2 + g] = ev
                ev_toks.append(ev)
        st(scr_wt.rearrange("(kt p) d -> p kt d", p=128), WSL[:],
           wait=tuple(ev_toks))
        P.barrier()

        for w, idx in W_IDX.items():
            tldw = ld(WS32[:], win[w].rearrange("(kt p) d -> p kt d", p=128))
            th = vcopy(WSH[:], WS32[:], wait=(tldw,), inc="v")
            tl_ = v_op(lambda E: E.tensor_sub(out=WSL[:], in0=WS32[:],
                                              in1=WSH[:]), wait=(th,), inc="v")
            sh = st(w_ap(idx, 0), WSH[:], wait=(th,))
            sl = st(w_ap(idx, 1), WSL[:], wait=(tl_,))
            P.op("sync", lambda E: E.nop(), wait=(sh, sl))
        P.barrier()

    def emit_stats(src_ap, row0=0):
        vt = [None] * NT
        stk = [None] * NT
        for t in range(NT):
            slot = t % 2
            w = []
            if t >= 2:
                w = [vt[t - 2], stk[t - 2]]
            l_ = ld(xin[:, slot, :],
                    src_ap[row0 + t * 128:row0 + (t + 1) * 128, :], wait=tuple(w))
            vt[t] = v_op(lambda E, sl=slot, tt=t: E.tensor_scalar(
                out=STS0[:], in0=xin[:, sl, :], scalar1=1.0, scalar2=0.0,
                op0=ALU.mult, op1=ALU.add,
                accum_out=sums_c[:, tt:tt + 1]), wait=(l_,), inc="v")
            stk[t] = s_op(lambda E, sl=slot, tt=t: E.activation(
                out=STS1[:], in_=xin[:, sl, :], func=AF.Square,
                accum_out=sumsq_c[:, tt:tt + 1]), wait=(l_,), inc="s")
        tv1 = vts(m_c[:], sums_c[:], 1.0 / D, ALU.mult,
                  wait=(vt[NT - 1], stk[NT - 1]), inc="v")
        tv2 = vtt(var_c[:], m_c[:], m_c[:], ALU.mult, wait=(tv1,), inc="v")
        tv3 = vstt(var_c[:], sumsq_c[:], 1.0 / D, var_c[:], ALU.mult,
                   ALU.subtract, wait=(tv2,), inc="v")
        tv4 = vts(var_c[:], var_c[:], EPS, ALU.add, wait=(tv3,), inc="v")
        s1 = sact(r_c[:], var_c[:], AF.Ln, wait=(tv4,), inc="s")
        s2 = sact(r_c[:], r_c[:], AF.Exp, scale=-0.5, wait=(s1,), inc="s")
        return s2

    def emit_ln_fm(src_ap, row0, ntiles, tile0, dst_h, dst_l, col0, wtok):
        ev_h = {}
        ev_l = {}
        vtok = {}
        for t in range(ntiles):
            slot = t % 2
            buf = t % 2
            gt = tile0 + t
            w = [wtok]
            if t >= 2:
                w.append(vtok[t - 2])
            l_ = ld(xin[:, slot, :],
                    src_ap[row0 + t * 128:row0 + (t + 1) * 128, :], wait=tuple(w))
            wv = [l_]
            if t >= 2:
                wv += [ev_h[t - 2], ev_l[t - 2]]
            vts(FS[buf][:], xin[:, slot, :], m_c[:, gt:gt + 1], ALU.subtract,
                wait=tuple(wv))
            th = vts(LNh[:, buf, :], FS[buf][:], r_c[:, gt:gt + 1], ALU.mult,
                     inc="v")
            tl_ = vstt(LNl[:, buf, :], FS[buf][:], r_c[:, gt:gt + 1],
                       LNh[:, buf, :], ALU.mult, ALU.subtract, inc="v")
            vtok[t] = tl_
            col = col0 + t * 128
            for (pe, lnbuf, wt, evd, dst) in ((PEA, LNh, th, ev_h, dst_h),
                                              (PEB, LNl, tl_, ev_l, dst_l)):
                tt = None
                for kt in range(KT):
                    w2 = [wt]
                    if t >= 1:
                        w2.append(evd.get(t - 1))
                    tt = mm(pe[:, kt * 128:(kt + 1) * 128],
                            lnbuf[:, buf, kt * 128:(kt + 1) * 128], I_b[:],
                            kt == 0, kt == KT - 1, wait=tuple(x for x in w2 if x),
                            inc="t" if kt == KT - 1 else None, tr=True)
                dstap = dst[:, :, col:col + 128]
                srcap = pe[:, 0:1024].rearrange("p (a b) -> p a b", a=KT)
                if pe is PEA:
                    evd[t] = scopy(dstap, srcap, wait=(tt,), inc="s")
                else:
                    evd[t] = v_op(lambda E, d=dstap, s_=srcap:
                                  E.tensor_copy(out=d, in_=s_), wait=(tt,),
                                  inc="v")
        return (ev_h[ntiles - 1], ev_l[ntiles - 1])

    def proj_fm(wtoks, mov_h, mov_l, ncols, dst_fn, bname, col0=0):
        evt = {}
        nunit = 0
        for ntc in range(ncols // 512):
            for mt in range(KT):
                bank = (PSB, PSC)[nunit % 2]
                w0 = list(wtoks)
                if nunit >= 2:
                    w0 += list(evt[nunit - 2])
                tokmm = None
                for p, (wsel, msel) in enumerate(((wa_h, mov_h), (wa_l, mov_h),
                                                  (wa_h, mov_l))):
                    for kt in range(KT):
                        last = p == 2 and kt == KT - 1
                        tokmm = mm(bank[:, 0:512],
                                   wsel[:, kt, mt * 128:(mt + 1) * 128],
                                   msel[:, kt, ntc * 512:(ntc + 1) * 512],
                                   p == 0 and kt == 0, last,
                                   wait=tuple(w0) if (p == 0 and kt == 0) else (),
                                   inc="t" if last else None)
                hi_ap, lo_ap = dst_fn(mt, col0 + ntc * 512, 512)
                te1 = sact(hi_ap, bank[:, 0:512], AF.Identity,
                           bias=bcols[bname][:, mt:mt + 1], wait=(tokmm,),
                           inc="s")
                te2 = vstt(lo_ap, bank[:, 0:512], bcols[bname][:, mt:mt + 1],
                           hi_ap, ALU.add, ALU.subtract, wait=(tokmm, te1),
                           inc="v")
                evt[nunit] = (te1, te2)
                nunit += 1
        return evt[nunit - 1]

    def proj_tm(first_w, act_fn, wmov, ntiles, brow, evac, passes=3):
        evt = {}
        nunit = 0
        for tt in range(ntiles):
            for ntc in range(2):
                bank = (PSB, PSC)[nunit % 2]
                w0 = list(first_w)
                if nunit >= 2:
                    w0 += list(evt[nunit - 2])
                psel = (((0, 0), (1, 0), (0, 1)) if passes == 3 else ((0, 0),))
                tokmm = None
                first = True
                for (ah, wl_) in psel:
                    for kt in range(KT):
                        tokmm = mm(bank[:, 0:512], act_fn(ah, kt, tt),
                                   wmov[wl_][:, kt, ntc * 512:(ntc + 1) * 512],
                                   first, False,
                                   wait=tuple(w0) if first else ())
                        first = False
                tokmm = mm(bank[:, 0:512], ones_row[:, 0:128],
                           brow[0:1, ntc * 512:(ntc + 1) * 512], False, True,
                           inc="t")
                evt[nunit] = evac(tt, ntc, bank, tokmm)
                nunit += 1
        return evt

    def emit_attn_a(src_ap, dst_ap):
        stok = emit_stats(src_ap)
        P.barrier()
        emit_ln_fm(src_ap, 0, NT, 0, LTh, LTl, 0, stok)
        P.barrier()
        if debug_a:
            st(dbg["a_lth"].rearrange("(kt p) t -> p kt t", p=128), LTh[:])
            st(dbg["a_ltl"].rearrange("(kt p) t -> p kt t", p=128), LTl[:])
            P.barrier()

        wv = load_weight("wv_a")
        spill_toks = {}

        def evac_v(tt, ntc, bank, tokmm):
            wv2 = []
            if tt >= 2:
                wv2 = [spill_toks[tt - 2]]
            te = scopy(xout[:, tt % 2, ntc * 512:(ntc + 1) * 512], bank[:, 0:512],
                       wait=(tokmm,) + tuple(wv2), inc="s")
            if ntc == 1:
                spill_toks[tt] = st(scr_v[tt * 128:(tt + 1) * 128, :],
                                    xout[:, tt % 2, :], wait=(te,))
            return (te,)

        proj_tm(wv, lambda h, kt, tt: (LTh if h == 0 else LTl)[:, kt,
                                                              tt * 128:(tt + 1) * 128],
                (wa_h, wa_l), NT, brows["bv_a"], evac_v, passes=3)
        P.barrier()

        wk = load_weight("wk_a")
        proj_fm(wk, LTh, LTl, S,
                lambda mt, c0, w: (Kh[:, mt, c0:c0 + w], Kl[:, mt, c0:c0 + w]),
                "bk_a")
        P.barrier()
        if debug_a:
            st(dbg["a_kh"].rearrange("(kt p) t -> p kt t", p=128), Kh[:])
            st(dbg["a_kl"].rearrange("(kt p) t -> p kt t", p=128), Kl[:])
            P.barrier()

        wq = load_weight("wq_a")

        def qdst(mt, c0, w):
            if c0 < 512:
                return (QSh[:, mt, c0:c0 + w], QSl[:, mt, c0:c0 + w])
            return (LTh[:, mt, c0 - 512:c0 - 512 + w],
                    LTl[:, mt, c0 - 512:c0 - 512 + w])

        proj_fm(wq, LTh, LTl, S, qdst, "bq_a")
        P.barrier()
        if debug_a:
            st(dbg["a_qh"][:, 0:512].rearrange("(kt p) t -> p kt t", p=128),
               QSh[:])
            st(dbg["a_ql"][:, 0:512].rearrange("(kt p) t -> p kt t", p=128),
               QSl[:])
            st(dbg["a_qh"][:, 512:S].rearrange("(kt p) t -> p kt t", p=128),
               LTh[:, :, 0:S - 512])
            st(dbg["a_ql"][:, 512:S].rearrange("(kt p) t -> p kt t", p=128),
               LTl[:, :, 0:S - 512])
            P.barrier()

        def q_ap(hilo, kt, c0):
            if c0 < 512:
                t_ = (QSh, QSl)[hilo]
                return t_[:, kt, c0:c0 + 128]
            t_ = (LTh, LTl)[hilo]
            return t_[:, kt, c0 - 512:c0 - 512 + 128]

        prev_read = None
        at_ev = {}
        dg_tok = {}
        for idx in range(16):
            n, qt = idx // 2, idx % 2
            E = Ebuf[idx % 2]
            c0 = n * 256 + qt * 128
            tokmm = None
            for m in range(8):
                for p, hl in enumerate(((0, 0), (1, 0), (0, 1))):
                    qh, kh = hl
                    for kt in range(KT):
                        start = m % 2 == 0 and p == 0 and kt == 0
                        stop = m % 2 == 1 and p == 2 and kt == KT - 1
                        w0 = ()
                        if start and prev_read is not None:
                            w0 = tuple(prev_read)
                        tokmm = mm(PSA[:, m * 256:(m + 1) * 256],
                                   q_ap(qh, kt, c0),
                                   (Kh, Kl)[kh][:, kt, m * 256:(m + 1) * 256],
                                   start, stop, wait=w0,
                                   inc="t" if (m == 7 and p == 2 and
                                               kt == KT - 1) else None)
            tmx = v_op(lambda E_: E_.reduce_max(out=cur["negM"][:],
                                                in_=PSA[:, 0:2048], axis=AX.X,
                                                negate=True), wait=(tokmm,))
            tb_ = vts(cur["bias"][:], cur["negM"][:], SCALE_A, ALU.mult, inc="v")
            texp = sact(E[:, 0:2048], PSA[:, 0:2048], AF.Exp,
                        bias=cur["bias"][:, 0:1], scale=SCALE_A, wait=(tb_,),
                        inc="s")
            prev_read = (tb_, texp)
            tsum = v_op(lambda E_, Ex=E: E_.reduce_sum(
                out=sums8[:], in_=Ex[:, 0:2048].rearrange("p (a b) -> p a b", a=8),
                axis=AX.X), wait=(texp,), inc="v")
            tln = sact(tmp8[:], sums8[:], AF.Ln, wait=(tsum,), inc="s")
            trv = sact(rinv8[:], tmp8[:], AF.Exp, scale=-1.0, wait=(tln,),
                       inc="s")
            trv2 = sact(rinv8x2[:], tmp8[:], AF.Exp, scale=-1.0,
                        bias=ln2c[:, 0:1], wait=(tln,), inc="s")
            for m in range(8):
                wv2 = [trv, trv2]
                if m >= 2:
                    wv2.append(dg_tok[(idx, m - 2)])
                elif idx > 0:
                    wv2.append(dg_tok[(idx - 1, m + 6)])
                rsrc = rinv8x2 if m == n else rinv8
                tD = vts(Dbuf[m % 2][:], I_f[:], rsrc[:, m:m + 1], ALU.mult,
                         wait=tuple(wv2), inc="v")
                wmm = [tD]
                if m == 0 and idx > 0:
                    wmm += list(at_ev[idx - 1])
                mm(PSB[:, 0:128], E[:, m * 256:m * 256 + 128], Dbuf[m % 2][:],
                   m == 0, m == 7, wait=tuple(wmm))
                dg_tok[(idx, m)] = mm(PSC[:, 0:128],
                                      E[:, m * 256 + 128:m * 256 + 256],
                                      Dbuf[m % 2][:], m == 0, m == 7, inc="t")
            ta1 = scopy(ATa[:, n * 2, qt * 128:qt * 128 + 128], PSB[:, 0:128],
                        wait=(dg_tok[(idx, 7)],), inc="s")
            ta2 = scopy(ATa[:, n * 2 + 1, qt * 128:qt * 128 + 128], PSC[:, 0:128],
                        wait=(dg_tok[(idx, 7)],), inc="s")
            at_ev[idx] = (ta1, ta2)
        P.barrier()
        if debug_a:
            st(dbg["a_at"].rearrange("(a p) q -> p a q", p=128), ATa[:])
            st(dbg["a_e0"][:], Ebuf[0][:])
            st(dbg["a_e1"][:], Ebuf[1][:])
            st(dbg["a_r8"][0:1, :].rearrange("one (p w) -> (one p) w", w=8),
               rinv8[:])
            st(dbg["a_r8"][1:2, :].rearrange("one (p w) -> (one p) w", w=8),
               sums8[:])
            P.barrier()

        vload = {}
        avtok = {}
        evtok = {}
        nunit = 0
        for n in range(8):
            vrb = (xin, xout)[n % 2]
            w0 = []
            if n >= 2:
                w0 = [avtok[n - 2]]
            vload[n] = (
                ld(vrb[:, 0, :], scr_v[n * 256:n * 256 + 128, :],
                   wait=tuple(w0)),
                ld(vrb[:, 1, :], scr_v[n * 256 + 128:n * 256 + 256, :],
                   wait=tuple(w0)))
            for et in range(KT):
                bank = (PSB, PSC)[nunit % 2]
                w1 = list(vload[n])
                if nunit >= 2:
                    w1 += list(evtok[nunit - 2])
                mm(bank[:, 0:256], vrb[:, 0, et * 128:(et + 1) * 128],
                   ATa[:, n * 2, 0:256], True, False, wait=tuple(w1))
                tokav = mm(bank[:, 0:256], vrb[:, 1, et * 128:(et + 1) * 128],
                           ATa[:, n * 2 + 1, 0:256], False, True, inc="t")
                if et == KT - 1:
                    avtok[n] = tokav
                th = sact(CBh[:, et, n * 256:(n + 1) * 256], bank[:, 0:256],
                          AF.Identity, wait=(tokav,), inc="s")
                tl_ = vtt(CBl[:, et, n * 256:(n + 1) * 256], bank[:, 0:256],
                          CBh[:, et, n * 256:(n + 1) * 256], ALU.subtract,
                          wait=(tokav, th), inc="v")
                evtok[nunit] = (th, tl_)
                nunit += 1
        P.barrier()
        if debug_a:
            st(dbg["a_cbh"].rearrange("(kt p) t -> p kt t", p=128), CBh[:])
            st(dbg["a_cbl"].rearrange("(kt p) t -> p kt t", p=128), CBl[:])
            P.barrier()

        wo = load_weight("wo_a")
        emit_out_proj(wo, lambda h, kt, tt: (CBh if h == 0 else CBl)[
            :, kt, tt * 128:(tt + 1) * 128], (wa_h, wa_l), NT, 0,
            brows["bo_a"], src_ap, 0, dst_ap, 0, passes=3, ffn_scale=False)
        P.barrier()

    def emit_out_proj(first_w, act_fn, wmov, ntiles, tile0, brow,
                      res_ap, res_row0, dst_ap, dst_row0, passes, ffn_scale):
        res_ld = {}
        st_tok = {}

        def evac(tt, ntc, bank, tokmm):
            if ntc == 0 and tt not in res_ld:
                w = []
                if tt >= 2:
                    w = [st_tok[tt - 2]]
                res_ld[tt] = ld(
                    xin[:, tt % 2, :],
                    res_ap[res_row0 + (tile0 + tt) * 128:
                           res_row0 + (tile0 + tt + 1) * 128, :], wait=tuple(w))
            sl = tt % 2
            wst = (st_tok.get(tt - 2),) if tt >= 2 else ()
            if ffn_scale:
                te = vstt(xout[:, sl, ntc * 512:(ntc + 1) * 512], bank[:, 0:512],
                          s_bc[:, 0:1], xin[:, sl, ntc * 512:(ntc + 1) * 512],
                          ALU.mult, ALU.add,
                          wait=(tokmm, res_ld[tt]) + wst, inc="v")
            else:
                te = vtt(xout[:, sl, ntc * 512:(ntc + 1) * 512], bank[:, 0:512],
                         xin[:, sl, ntc * 512:(ntc + 1) * 512], ALU.add,
                         wait=(tokmm, res_ld[tt]) + wst, inc="v")
            if ntc == 1:
                st_tok[tt] = st(
                    dst_ap[dst_row0 + (tile0 + tt) * 128:
                           dst_row0 + (tile0 + tt + 1) * 128, :],
                    xout[:, sl, :], wait=(te,))
            return (te,)

        proj_tm(first_w, act_fn, wmov, ntiles, brow, evac, passes=passes)

    def emit_ffn(src_ap, dst_ap):
        stok = emit_stats(src_ap)
        twt = ld(wt_f[:], scr_wt.rearrange("(kt p) d -> p kt d", p=128))
        stok = vts(rs_c[:], r_c[:], srec_bc[:, 0:1], ALU.mult, wait=(stok,),
                   inc="v")
        P.barrier()
        pe_tok = {}
        ev_tok = {}
        for t in range(NT):
            slot = t % 2
            w = [stok]
            if t >= 2:
                w.append(pe_tok[t - 2])
            l_ = ld(xin[:, slot, :], src_ap[t * 128:(t + 1) * 128, :],
                    wait=tuple(w))
            wv = [l_]
            if t >= 1:
                wv.append(pe_tok[t - 1])
            vts(FS[0][:], xin[:, slot, :], m_c[:, t:t + 1], ALU.subtract,
                wait=tuple(wv))
            vts(FS[1][:], FS[0][:], rs_c[:, t:t + 1], ALU.mult)
            v_op(lambda E: E.tensor_scalar(out=FS[2][:], in0=FS[1][:],
                                           scalar1=M32C, scalar2=-M32C,
                                           op0=ALU.add, op1=ALU.add))
            v_op(lambda E: E.tensor_scalar(out=FS[3][:], in0=FS[1][:],
                                           scalar1=M64C, scalar2=-M64C,
                                           op0=ALU.add, op1=ALU.add))
            vts(FS[0][:], FS[2][:], qm1[:, 0:1], ALU.mult)
            vtt(FS[1][:], FS[0][:], FS[3][:], ALU.add)
            tq = v_op(lambda E: E.tensor_scalar(out=xqb[:],
                                                in0=FS[1][:], scalar1=127.0,
                                                scalar2=-127.0, op0=ALU.min,
                                                op1=ALU.max), inc="v")
            pe = (PEA, PEB)[t % 2]
            tt_ = None
            for kt in range(KT):
                w2 = [tq]
                if t >= 2:
                    w2.append(ev_tok[t - 2])
                tt_ = mm(pe[:, kt * 128:(kt + 1) * 128],
                         xqb[:, kt * 128:(kt + 1) * 128], I_b[:],
                         kt == 0, kt == KT - 1, wait=tuple(w2),
                         inc="t" if kt == KT - 1 else None, tr=True)
            pe_tok[t] = tt_
            ev_tok[t] = scopy(XIT[:, :, t * 128:(t + 1) * 128],
                              pe[:, 0:1024].rearrange("p (a b) -> p a b", a=KT),
                              wait=(tt_,), inc="s")
        P.barrier()
        if debug_ffn:
            st(dbg["xit"].rearrange("(kt p) t -> p kt t", p=128), XIT[:])
            st(dbg["mc"][:], m_c[:])
            st(dbg["rc"][:], r_c[:])
            st(dbg["sc"][0:1, :].rearrange("one (p w) -> (one p) w", w=1),
               s_bc[:])
            st(dbg["sc"][1:2, :].rearrange("one (p w) -> (one p) w", w=1),
               srec_bc[:])
            st(dbg["wt"].rearrange("(kt p) t -> p kt t", p=128), wt_f[:])
            st(dbg["fbs"][:], brows["fbs"][:])
            for i_ in range(4):
                st(dbg["fs"][i_ * 128:(i_ + 1) * 128, :], FS[i_][:])
            st(dbg["xqb"][:], xqb[:])
            st(dbg["qf"][:], qflag[:])
            st(dbg["qm1"][:], qm1[:])
            st(dbg["cur"][0:1, :].rearrange("one (p w) -> (one p) w", w=1),
               cur["m"][:])
            st(dbg["cur"][1:2, :].rearrange("one (p w) -> (one p) w", w=1),
               cur["r"][:])
            st(dbg["cur"][2:3, :].rearrange("one (p w) -> (one p) w", w=1),
               cur["rs"][:])
            st(dbg["xin1"][:], xin[:, 1, :])
            P.barrier()
        emit_out_proj((twt, ev_tok[NT - 1]),
                      lambda h, kt, tt: XIT[:, kt, tt * 128:(tt + 1) * 128],
                      (wt_f, wt_f), NT, 0, brows["fbs"], src_ap, 0, dst_ap, 0,
                      passes=1, ffn_scale=True)
        P.barrier()

    def emit_attn_b(src_ap, dst_ap):
        stok = emit_stats(src_ap)
        P.barrier()
        for quarter in range(4):
            row0 = quarter * 512
            emit_ln_fm(src_ap, row0, 4, quarter * 4, LTbh, LTbl, 0, stok)
            P.barrier()

            wv = load_weight("wv_b")
            vp_tok = {}

            def evac_vb(tt, ntc, bank, tokmm):
                dst = vb[0][:, tt, :] if tt < 2 else vb[1][:, tt - 2, :]
                te = scopy(dst[:, ntc * 512:(ntc + 1) * 512], bank[:, 0:512],
                           wait=(tokmm,), inc="s")
                if ntc == 1 and tt < 3:
                    src_t = vb[0] if tt < 2 else vb[1]
                    si = tt if tt < 2 else tt - 2
                    vp_tok[tt + 1] = st(vprev[0:1, tt + 1, :],
                                        src_t[127:128, si, :], wait=(te,))
                return (te,)

            proj_tm(wv, lambda h, kt, tt: (LTbh if h == 0 else LTbl)[
                :, kt, tt * 128:(tt + 1) * 128], (wa_h, wa_l), 4,
                brows["bv_b"], evac_vb, passes=3)
            if quarter == 0:
                vp_tok[0] = v_op(lambda E: E.memset(vprev[0:1, 0, :], 0.0),
                                 inc="v")
            else:
                vp_tok[0] = v_op(lambda E: E.tensor_copy(out=vprev[0:1, 0, :],
                                                         in_=vbound[0:1, :]),
                                 inc="v")
            P.barrier()

            wk = load_weight("wk_b")
            proj_fm(wk, LTbh, LTbl, 512,
                    lambda mt, c0, w: (kbh[:, mt, c0:c0 + w],
                                       kbl[:, mt, c0:c0 + w]), "bk_b")
            P.barrier()

            wq = load_weight("wq_b")
            proj_fm(wq, LTbh, LTbl, 512,
                    lambda mt, c0, w: (QSbh[:, mt, c0:c0 + w],
                                       QSbl[:, mt, c0:c0 + w]), "bq_b")
            P.barrier()

            def qb_ap(hilo, h, c):
                t_ = (QSbh, QSbl)[hilo]
                p0 = (h % 2) * 64
                return t_[p0:p0 + 64, h // 2, c * 128:(c + 1) * 128]

            def kb_ap(hilo, h, c0, c1):
                t_ = (kbh, kbl)[hilo]
                p0 = (h % 2) * 64
                return t_[p0:p0 + 64, h // 2, c0:c1]

            exp_tok = {}
            av_ev = {}
            tr_ev_all = {}
            for h in range(16):
                tp = (64, 0) if h % 2 == 1 else None
                wsd = [exp_tok.get((h - 1, 3))] if h > 0 else []
                tsd = vcopy(cur["sd"][:], sdf[:, h:h + 1], wait=tuple(wsd),
                            inc="v")
                for c in range(4):
                    bank = (PSB, PSC)[c % 2]
                    w0 = [tsd]
                    if (h, c - 2) in exp_tok:
                        w0.append(exp_tok[(h, c - 2)])
                    elif h > 0 and c < 2:
                        w0.append(exp_tok[(h - 1, c + 2)])
                    first = True
                    for p, hl in enumerate(((0, 0), (1, 0), (0, 1))):
                        qh, kh = hl
                        if c == 0 and quarter == 0:
                            kprev = zcol_b[(h % 2) * 64:(h % 2) * 64 + 64, 0:1]
                        elif c == 0:
                            kprev = (kbound_h, kbound_l)[kh][
                                (h % 2) * 64:(h % 2) * 64 + 64,
                                h // 2:h // 2 + 1]
                        else:
                            kprev = kb_ap(kh, h, c * 128 - 1, c * 128)
                        mm(bank[:, 0:1], qb_ap(qh, h, c), kprev,
                           first, False, wait=tuple(w0) if first else (), tp=tp)
                        first = False
                    tokmm = None
                    for p, hl in enumerate(((0, 0), (1, 0), (0, 1))):
                        qh, kh = hl
                        tokmm = mm(bank[:, 1:129], qb_ap(qh, h, c),
                                   kb_ap(kh, h, c * 128, (c + 1) * 128),
                                   False, p == 2, tp=tp,
                                   inc="t" if p == 2 else None)
                    v_op(lambda E, b=bank: E.reduce_max(
                        out=cur["negM"][:], in_=b[:, 0:129], axis=AX.X,
                        negate=True), wait=(tokmm,))
                    tb_ = vts(cur["bias"][:], cur["negM"][:], sdf[:, h:h + 1],
                              ALU.mult, inc="v")
                    texp = sact(Eall[:, c, 0:129], bank[:, 0:129], AF.Exp,
                                bias=cur["bias"][:, 0:1],
                                scale=cur["sd"][:, 0:1], wait=(tb_,), inc="s")
                    exp_tok[(h, c)] = texp
                    v_op(lambda E, c_=c: E.reduce_sum(
                        out=sums8b[:, c_:c_ + 1], in_=Eall[:, c_, 0:129],
                        axis=AX.X), wait=(texp,),
                        inc="v" if c == 3 else None)
                tln = sact(rinv8b[:, 0:4], sums8b[:, 0:4], AF.Ln,
                           wait=(("v", P.cnt["v"]),), inc="s")
                trv = sact(rinv8b[:, 0:4], rinv8b[:, 0:4], AF.Exp, scale=-1.0,
                           wait=(tln,), inc="s")
                tr_tok = {}
                for c in range(4):
                    w0 = [trv]
                    if c >= 1:
                        w0.append(tr_tok[c - 1])
                    tA = vts(Abuf[:, 0:129], Eall[:, c, 0:129],
                             rinv8b[:, c:c + 1], ALU.mult, wait=tuple(w0),
                             inc="v")
                    sl = (c % 2) * 512
                    w1 = [tA]
                    if (h, c - 2) in tr_ev_all:
                        w1 += list(tr_ev_all[(h, c - 2)])
                    elif h > 0 and (h - 1, c + 2) in tr_ev_all:
                        w1 += list(tr_ev_all[(h - 1, c + 2)])
                    mm(PSA[:, sl:sl + 128], Abuf[:, 1:129], I_f[:], True, True,
                       wait=tuple(x for x in w1 if x), tr=True)
                    trt = mm(PSA[0:1, sl + 128:sl + 256], Abuf[:, 0:1], I_f[:],
                             True, True, inc="t", tr=True)
                    tr_tok[c] = trt
                    ta1 = scopy(ATb[:, 0:128], PSA[:, sl:sl + 128], wait=(trt,),
                                inc="s")
                    ta2 = scopy(ATp[0:1, 0:128], PSA[0:1, sl + 128:sl + 256],
                                wait=(trt,), inc="s")
                    tr_ev_all[(h, c)] = (ta1, ta2)
                    sa = 1024 + (c % 2) * 512
                    vt_ = vb[0] if c < 2 else vb[1]
                    vi = c if c < 2 else c - 2
                    w2 = [ta1, ta2, vp_tok[c]]
                    if (h, c - 2) in av_ev:
                        w2 += list(av_ev[(h, c - 2)])
                    elif h > 0 and (h - 1, c + 2) in av_ev:
                        w2 += list(av_ev[(h - 1, c + 2)])
                    mm(PSA[0:64, sa:sa + 128], vt_[:, vi, h * 64:h * 64 + 64],
                       ATb[:, 0:128], True, False, wait=tuple(w2))
                    tokav = mm(PSA[0:64, sa:sa + 128],
                               vprev[0:1, c, h * 64:h * 64 + 64],
                               ATp[0:1, 0:128], False, True, inc="t")
                    p0 = (h % 2) * 64
                    th_ = s_op(lambda E, p_=p0, h_=h, c_=c, sa_=sa:
                               E.activation(
                                   out=cbh_b[p_:p_ + 64, h_ // 2,
                                             c_ * 128:(c_ + 1) * 128],
                                   in_=PSA[0:64, sa_:sa_ + 128],
                                   func=AF.Identity), wait=(tokav,), inc="s")
                    tl_ = v_op(lambda E, p_=p0, h_=h, c_=c, sa_=sa:
                               E.tensor_tensor(
                                   out=cbl_b[p_:p_ + 64, h_ // 2,
                                             c_ * 128:(c_ + 1) * 128],
                                   in0=PSA[0:64, sa_:sa_ + 128],
                                   in1=cbh_b[p_:p_ + 64, h_ // 2,
                                             c_ * 128:(c_ + 1) * 128],
                                   op=ALU.subtract), wait=(tokav, th_), inc="v")
                    av_ev[(h, c)] = (th_, tl_)
            P.barrier()

            if quarter < 3:
                vcopy(kbound_h[:], kbh[:, :, 511:512].rearrange(
                    "p a b -> p (a b)"), inc="v")
                vcopy(kbound_l[:], kbl[:, :, 511:512].rearrange(
                    "p a b -> p (a b)"), inc="v")
                st(vbound[0:1, :], vb[1][127:128, 1, :])
                P.barrier()

            wo = load_weight("wo_b")
            emit_out_proj(wo, lambda hl, kt, tt: (cbh_b if hl == 0 else cbl_b)[
                :, kt, tt * 128:(tt + 1) * 128], (wa_h, wa_l), 4, quarter * 4,
                brows["bo_b"], src_ap, 0, dst_ap, 0, passes=3, ffn_scale=False)
            P.barrier()

    emit_init()
    for s_i in range(n_seq):
        xsrc = x_d[s_i * S:(s_i + 1) * S, :]
        chain = []
        cur_src = xsrc
        for j, sl in enumerate(sublayers):
            is_last = j == len(sublayers) - 1
            dst = y_d[s_i * S:(s_i + 1) * S, :] if is_last else scr_x[j % 3][:, :]
            chain.append((sl, cur_src, dst))
            cur_src = dst
        for sl, src, dst in chain:
            if sl == "a":
                emit_attn_a(src, dst)
            elif sl == "b":
                emit_attn_b(src, dst)
            else:
                emit_ffn(src, dst)
    P.barrier()

    with contextlib.ExitStack() as stack:
        sems = {}
        for name in ("dIn", "dOut", "t", "v", "s", "g",
                     "m_sync", "m_tensor", "m_vector", "m_scalar", "m_gpsimd"):
            sems[name] = stack.enter_context(nc.semaphore(name))
        block = stack.enter_context(nc.Block())
        P.replay(block, sems)
    assert nc.sbuf_bytes_remaining >= 229344 - 49120, nc.sbuf_bytes_remaining
    return nc


_NC_CACHE = {}


def _get_program():
    key = (SUBLAYERS, N_SEQ)
    if key not in _NC_CACHE:
        _NC_CACHE[key] = build_core_program()
    return _NC_CACHE[key]


def kernel(L0_attn_a_x, L0_attn_a_Wq, L0_attn_a_bq, L0_attn_a_Wk, L0_attn_a_bk,
           L0_attn_a_Wv, L0_attn_a_bv, L0_attn_a_Wo, L0_attn_a_bo,
           L0_attn_b_Wq, L0_attn_b_bq, L0_attn_b_Wk, L0_attn_b_bk,
           L0_attn_b_Wv, L0_attn_b_bv, L0_attn_b_Wo, L0_attn_b_bo,
           L0_attn_b_decay, L0_ffn_ternary_weight, L0_ffn_bias,
           L0_ffn_act_scale):
    nc = _get_program()
    x = np.ascontiguousarray(np.asarray(L0_attn_a_x, dtype=np.float32))
    common = {
        "wq_a": np.ascontiguousarray(L0_attn_a_Wq),
        "wk_a": np.ascontiguousarray(L0_attn_a_Wk),
        "wv_a": np.ascontiguousarray(L0_attn_a_Wv),
        "wo_a": np.ascontiguousarray(L0_attn_a_Wo),
        "wq_b": np.ascontiguousarray(L0_attn_b_Wq),
        "wk_b": np.ascontiguousarray(L0_attn_b_Wk),
        "wv_b": np.ascontiguousarray(L0_attn_b_Wv),
        "wo_b": np.ascontiguousarray(L0_attn_b_Wo),
        "bq_a": np.ascontiguousarray(L0_attn_a_bq.reshape(1, D)),
        "bk_a": np.ascontiguousarray(L0_attn_a_bk.reshape(1, D)),
        "bv_a": np.ascontiguousarray(L0_attn_a_bv.reshape(1, D)),
        "bo_a": np.ascontiguousarray(L0_attn_a_bo.reshape(1, D)),
        "bq_b": np.ascontiguousarray(L0_attn_b_bq.reshape(1, D)),
        "bk_b": np.ascontiguousarray(L0_attn_b_bk.reshape(1, D)),
        "bv_b": np.ascontiguousarray(L0_attn_b_bv.reshape(1, D)),
        "bo_b": np.ascontiguousarray(L0_attn_b_bo.reshape(1, D)),
        "fb": np.ascontiguousarray(L0_ffn_bias.reshape(1, D)),
        "tw": np.ascontiguousarray(L0_ffn_ternary_weight),
        "ascale": np.ascontiguousarray(
            np.asarray(L0_ffn_act_scale, dtype=np.float32).reshape(1, 1)),
        "decay": np.ascontiguousarray(L0_attn_b_decay.reshape(1, 16)),
    }
    in_maps = []
    for c in range(N_CORES):
        m = dict(common)
        m["x"] = np.ascontiguousarray(
            x[c * N_SEQ:(c + 1) * N_SEQ].reshape(N_SEQ * S, D))
        in_maps.append(m)
    res = run_bass_kernel_spmd(nc, in_maps, core_ids=list(range(N_CORES)))
    out = np.concatenate(
        [r["y"].reshape(N_SEQ, S, D) for r in res.results], axis=0)
    return out


# version 05
# speedup vs baseline: 1.0010x; 1.0010x over previous
import contextlib
import numpy as np

import concourse.bass as bass
import concourse.mybir as mybir
import concourse.bass_utils as _bass_utils
from concourse.bass_utils import run_bass_kernel_spmd

_orig_run_command = _bass_utils.run_command


def _run_command_ldwopt(argv, **kwargs):
    argv = [a.replace("--enable-ldw-opt=false", "--enable-ldw-opt=true")
            if isinstance(a, str) else a for a in argv]
    return _orig_run_command(argv, **kwargs)


_bass_utils.run_command = _run_command_ldwopt

F32 = mybir.dt.float32
BF16 = mybir.dt.bfloat16
I32 = mybir.dt.int32
AF = mybir.ActivationFunctionType
ALU = mybir.AluOpType
AX = mybir.AxisListType

N_CORES = 8
N_SEQ = 4
S = 2048
D = 1024
NT = 16
KT = 8
M32C = 12582912.0
M64C = 1.5 * 2.0 ** 52
EPS = 1e-5
SCALE_A = 0.03125
SCALE_B = 0.125
SUBLAYERS = ("a", "f1", "b", "f2")

ENGINES = ("sync", "tensor", "vector", "scalar", "gpsimd")


class Prog:

    def __init__(self):
        self.q = {e: [] for e in ENGINES}
        self.cnt = {}

    def op(self, eng, fn, wait=(), inc=None, amt=1):
        tok = None
        incs = ()
        if inc is not None:
            self.cnt[inc] = self.cnt.get(inc, 0) + amt
            tok = (inc, self.cnt[inc])
            incs = ((inc, self.cnt[inc], amt),)
        self.q[eng].append((tuple(w for w in wait if w is not None), fn, incs))
        return tok

    def barrier(self):
        self.op("sync", lambda E: E.nop(),
                wait=(("dIn", self.cnt.get("dIn", 0)),))
        self.op("gpsimd", lambda E: E.nop(),
                wait=(("dOut", self.cnt.get("dOut", 0)),))
        toks = {}
        for e in ENGINES:
            toks[e] = self.op(e, lambda E: E.nop(), inc="m_" + e)
        for e in ENGINES:
            waits = tuple(toks[o] for o in ENGINES if o != e)
            self.op(e, lambda E: E.nop(), wait=waits)

    def replay(self, block, sems):
        for eng_name in ENGINES:
            items = self.q[eng_name]

            def body(eng, items=items):
                waited = {}
                for waits, fn, incs in items:
                    for s_, v in waits:
                        if waited.get(s_, -1) < v:
                            eng.wait_ge(sems[s_], v)
                            waited[s_] = v
                    inst = fn(eng)
                    for s_, _v, amt in incs:
                        inst.then_inc(sems[s_], amt)
            getattr(block, eng_name)(body)


def build_core_program(sublayers=SUBLAYERS, n_seq=N_SEQ, debug_ffn=False, debug_a=False):
    nc = bass.Bass("TRN2", target_bir_lowering=False,
                   detect_race_conditions=False)
    P = Prog()

    x_d = nc.dram_tensor("x", [n_seq * S, D], F32, kind="ExternalInput").ap()
    y_d = nc.dram_tensor("y", [n_seq * S, D], F32, kind="ExternalOutput").ap()
    W_NAMES = ("wq_a", "wk_a", "wv_a", "wo_a", "wq_b", "wk_b", "wv_b", "wo_b")
    W_IDX = {w: i for i, w in enumerate(W_NAMES)}
    win = {w: nc.dram_tensor(w, [D, D], F32, kind="ExternalInput").ap()
           for w in W_NAMES}
    B_NAMES = ("bq_a", "bk_a", "bv_a", "bo_a", "bq_b", "bk_b", "bv_b", "bo_b", "fb")
    bin_ = {b: nc.dram_tensor(b, [1, D], F32, kind="ExternalInput").ap()
            for b in B_NAMES}
    tw_d = nc.dram_tensor("tw", [D, D], I32, kind="ExternalInput").ap()
    scal_d = nc.dram_tensor("ascale", [1, 1], F32, kind="ExternalInput").ap()
    decay_d = nc.dram_tensor("decay", [1, 16], F32, kind="ExternalInput").ap()

    dbg = {}
    if debug_ffn:
        dbg["xit"] = nc.dram_tensor("dbg_xit", [D, S], BF16,
                                    kind="ExternalOutput").ap()
        dbg["mc"] = nc.dram_tensor("dbg_mc", [128, NT], F32,
                                   kind="ExternalOutput").ap()
        dbg["rc"] = nc.dram_tensor("dbg_rc", [128, NT], F32,
                                   kind="ExternalOutput").ap()
        dbg["sc"] = nc.dram_tensor("dbg_sc", [2, 128], F32,
                                   kind="ExternalOutput").ap()
        dbg["wt"] = nc.dram_tensor("dbg_wt", [D, D], BF16,
                                   kind="ExternalOutput").ap()
        dbg["fbs"] = nc.dram_tensor("dbg_fbs", [1, D], F32,
                                    kind="ExternalOutput").ap()
        dbg["fs"] = nc.dram_tensor("dbg_fs", [4 * 128, D], F32,
                                   kind="ExternalOutput").ap()
        dbg["xqb"] = nc.dram_tensor("dbg_xqb", [128, D], BF16,
                                    kind="ExternalOutput").ap()
        dbg["qf"] = nc.dram_tensor("dbg_qf", [128, 1], F32,
                                   kind="ExternalOutput").ap()
        dbg["cur"] = nc.dram_tensor("dbg_cur", [3, 128], F32,
                                    kind="ExternalOutput").ap()
        dbg["qm1"] = nc.dram_tensor("dbg_qm1", [128, 1], F32,
                                    kind="ExternalOutput").ap()
        dbg["xin1"] = nc.dram_tensor("dbg_xin1", [128, D], F32,
                                     kind="ExternalOutput").ap()
    if debug_a:
        for nm, shp, dt_ in (("a_lth", [D, S], BF16), ("a_ltl", [D, S], BF16),
                             ("a_kh", [D, S], BF16), ("a_kl", [D, S], BF16),
                             ("a_qh", [D, S], BF16), ("a_ql", [D, S], BF16),
                             ("a_at", [2048, 256], F32),
                             ("a_e0", [128, S], F32), ("a_e1", [128, S], F32),
                             ("a_r8", [2, 1024], F32),
                             ("a_cbh", [D, S], BF16), ("a_cbl", [D, S], BF16)):
            dbg[nm] = nc.dram_tensor("dbg_" + nm, shp, dt_,
                                     kind="ExternalOutput").ap()
    scr_w = nc.dram_tensor("scr_w", [16 * D, D], BF16).ap()
    scr_wt = nc.dram_tensor("scr_wt", [D, D], BF16).ap()
    scr_v = nc.dram_tensor("scr_v", [S, D], F32).ap()
    scr_x = [nc.dram_tensor(f"scr_x{i}", [S, D], F32).ap() for i in range(3)]

    def sb(name, shape, dtype, off):
        return nc.alloc_sbuf_tensor_at(name, shape, dtype, offset=off)

    def sbm(name, shape, dtype):
        return nc.alloc_sbuf_tensor(name, shape, dtype)

    ones_row = sbm("ones_row", [1, 128], F32)
    I_f = sbm("I_f", [128, 128], F32)
    I_b = sbm("I_b", [128, 128], BF16)
    iocol = sbm("iocol", [128, 1], F32)
    iof = sbm("iof", [128, 128], F32)
    qflag = sbm("qflag", [128, 1], F32)
    qm1 = sbm("qm1", [128, 1], F32)
    zcol = sbm("zcol", [128, 8], F32)
    zcol_b = sbm("zcol_b", [128, 8], BF16)
    sdf = sbm("sdf", [128, 16], F32)
    bcols = {b: sbm("c_" + b, [128, KT], F32)
             for b in ("bq_a", "bk_a", "bq_b", "bk_b")}
    brows = {b: sbm("r_" + b, [1, D], F32)
             for b in ("bv_a", "bo_a", "bv_b", "bo_b", "fb")}
    brows["fbs"] = brows["fb"]
    s_bc = sbm("s_bc", [128, 1], F32)
    srec_bc = sbm("srec_bc", [128, 1], F32)
    sums_c = sbm("sums_c", [128, NT], F32)
    sumsq_c = sbm("sumsq_c", [128, NT], F32)
    m_c = sbm("m_c", [128, NT], F32)
    r_c = sbm("r_c", [128, NT], F32)
    rs_c = sbm("rs_c", [128, NT], F32)
    var_c = sbm("var_c", [128, NT], F32)
    CURS = ("m", "r", "rs", "bias", "negM", "rv", "sd", "b")
    cur = {k: sbm("cur_" + k, [128, 1], F32) for k in CURS}
    sums8 = sbm("sums8", [128, 8], F32)
    tmp8 = sbm("tmp8", [128, 8], F32)
    rinv8 = sbm("rinv8", [128, 8], F32)
    rinv8x2 = sbm("rinv8x2", [128, 8], F32)
    ln2c = sbm("ln2c", [128, 1], F32)
    sums8b = sbm("sums8b", [128, 8], F32)
    rinv8b = sbm("rinv8b", [128, 8], F32)
    Dbuf = [sbm(f"Dbuf{i}", [128, 128], F32) for i in range(2)]
    Abuf = sbm("Abuf", [128, 132], F32)
    ATb = sbm("ATb", [128, 132], F32)
    ATp = sbm("ATp", [1, 132], F32)
    kbound_h = sbm("kbound_h", [128, KT], BF16)
    kbound_l = sbm("kbound_l", [128, KT], BF16)
    vbound = sbm("vbound", [1, D], F32)
    lnd_bc = sbm("lnd_bc", [128, 16], F32)
    assert nc.SBUF_PARTITION_SIZE_BYTES - nc.sbuf_bytes_remaining // 1 >= 0

    OFF_XIN = 49120
    OFF_XOUT = 57312
    OFF_WA = 65504
    OFF_BIG1 = 98272
    OFF_BIG2 = 163808
    assert OFF_BIG2 + 64 * 1024 <= 229344

    xin = sb("xin", [128, 2, D], F32, OFF_XIN)
    xout = sb("xout", [128, 2, D], F32, OFF_XOUT)
    STS0 = sb("STS0", [128, D], F32, OFF_WA)
    STS1 = sb("STS1", [128, D], F32, OFF_WA + 4 * 1024)
    FS = [sb(f"FS{i}", [128, D], F32, OFF_WA + i * 4 * 1024) for i in range(4)]
    LNh = sb("LNh", [128, 2, D], BF16, OFF_WA + 8 * 1024)
    LNl = sb("LNl", [128, 2, D], BF16, OFF_WA + 12 * 1024)
    xqb = sb("xqb", [128, D], BF16, OFF_WA)
    wa_h = sb("wa_h", [128, KT, D], BF16, OFF_WA)
    wa_l = sb("wa_l", [128, KT, D], BF16, OFF_WA + 16 * 1024)
    wt_f = sb("wt_f", [128, KT, D], BF16, OFF_WA + 16 * 1024)
    Ebuf = [sb("Ebuf0", [128, S], F32, OFF_WA),
            sb("Ebuf1", [128, S], F32, OFF_WA + 8 * 1024)]
    ATa = sb("ATa", [128, 16, 256], F32, OFF_WA + 16 * 1024)

    LTh = sb("LTh", [128, KT, S], BF16, OFF_BIG1)
    LTl = sb("LTl", [128, KT, S], BF16, OFF_BIG1 + 32 * 1024)
    QSh = sb("QSh", [128, KT, 512], BF16, OFF_XIN)
    QSl = sb("QSl", [128, KT, 512], BF16, OFF_XOUT)
    XIT = sb("XIT", [128, KT, S], BF16, OFF_BIG1)
    LTbh = sb("LTbh", [128, KT, 512], BF16, OFF_BIG1)
    LTbl = sb("LTbl", [128, KT, 512], BF16, OFF_BIG1 + 8 * 1024)
    QSbh = sb("QSbh", [128, KT, 512], BF16, OFF_BIG1 + 16 * 1024)
    QSbl = sb("QSbl", [128, KT, 512], BF16, OFF_BIG1 + 24 * 1024)
    Eall = sb("Eall", [128, 4, 132], F32, OFF_BIG1 + 32 * 1024)
    vprev = sb("vprev", [1, 4, D], F32, OFF_BIG1 + 35 * 1024)

    Kh = sb("Kh", [128, KT, S], BF16, OFF_BIG2)
    Kl = sb("Kl", [128, KT, S], BF16, OFF_BIG2 + 32 * 1024)
    CBh = sb("CBh", [128, KT, S], BF16, OFF_BIG2)
    CBl = sb("CBl", [128, KT, S], BF16, OFF_BIG2 + 32 * 1024)
    kbh = sb("kbh", [128, KT, 512], BF16, OFF_BIG2)
    kbl = sb("kbl", [128, KT, 512], BF16, OFF_BIG2 + 8 * 1024)
    cbh_b = sb("cbh_b", [128, KT, 512], BF16, OFF_BIG2 + 16 * 1024)
    cbl_b = sb("cbl_b", [128, KT, 512], BF16, OFF_BIG2 + 24 * 1024)
    vb = [sb("vb0", [128, 2, D], F32, OFF_XIN),
          sb("vb1", [128, 2, D], F32, OFF_XOUT)]
    WS32 = sb("WS32", [128, KT, D], F32, OFF_BIG1)
    WSI = sb("WSI", [128, KT, D], I32, OFF_BIG1 + 32 * 1024)
    WSH = sb("WSH", [128, KT, D], BF16, OFF_BIG2)
    WSL = sb("WSL", [128, KT, D], BF16, OFF_BIG2 + 16 * 1024)

    PSA = nc.alloc_psum_tensor("PSA", [128, 2048], F32)
    PSB = nc.alloc_psum_tensor("PSB", [128, 512], F32)
    PSC = nc.alloc_psum_tensor("PSC", [128, 512], F32)
    PEA = nc.alloc_psum_tensor("PEA", [128, 1024], BF16)
    PEB = nc.alloc_psum_tensor("PEB", [128, 1024], BF16)

    def ld(out_ap, in_ap, wait=()):
        return P.op("sync", lambda E, o=out_ap, i=in_ap:
                    E.dma_start(out=o, in_=i), wait, inc="dIn", amt=16)

    def st(out_ap, in_ap, wait=()):
        return P.op("gpsimd", lambda E, o=out_ap, i=in_ap:
                    E.dma_start(out=o, in_=i), wait, inc="dOut", amt=16)

    def mm(ps, lhsT, rhs, start, stop, wait=(), inc=None, tp=None, tr=False):
        def f(E, o=ps, l_=lhsT, r_=rhs, st_=start, sp=stop, tp_=tp, t_=tr):
            if t_:
                return E.matmul(out=o, lhsT=l_, rhs=r_, start=st_, stop=sp,
                                tile_position=tp_, is_transpose=True)
            return E.matmul(out=o, lhsT=l_, rhs=r_, start=st_, stop=sp,
                            tile_position=tp_)
        return P.op("tensor", f, wait, inc)

    def v_op(fn, wait=(), inc=None):
        return P.op("vector", fn, wait, inc)

    def s_op(fn, wait=(), inc=None):
        return P.op("scalar", fn, wait, inc)

    def vcopy(dst, src, wait=(), inc=None):
        return v_op(lambda E, d=dst, s_=src: E.tensor_copy(out=d, in_=s_),
                    wait, inc)

    def vts(dst, src, scalar, opc, wait=(), inc=None):
        return v_op(lambda E, d=dst, s_=src, sc=scalar, o=opc:
                    E.tensor_single_scalar(out=d, in_=s_, scalar=sc, op=o),
                    wait, inc)

    def vtt(dst, a, b, opc, wait=(), inc=None):
        return v_op(lambda E, d=dst, a_=a, b_=b, o=opc:
                    E.tensor_tensor(out=d, in0=a_, in1=b_, op=o), wait, inc)

    def vstt(dst, in0, scalar, in1, op0, op1, wait=(), inc=None):
        return v_op(lambda E, d=dst, i0=in0, sc=scalar, i1=in1, o0=op0, o1=op1:
                    E.scalar_tensor_tensor(out=d, in0=i0, scalar=sc, in1=i1,
                                           op0=o0, op1=o1), wait, inc)

    def scopy(dst, src, wait=(), inc=None):
        return s_op(lambda E, d=dst, s_=src: E.copy(out=d, in_=s_), wait, inc)

    def sact(dst, src, func, bias=0.0, scale=1.0, accum=None, wait=(), inc=None):
        return s_op(lambda E, d=dst, s_=src, f=func, b=bias, sc=scale, a=accum:
                    E.activation(out=d, in_=s_, func=f, bias=b, scale=sc,
                                 accum_out=a), wait, inc)

    def w_ap(idx, hilo):
        return scr_w[(2 * idx + hilo) * D:(2 * idx + hilo + 1) * D, :].rearrange(
            "(kt p) d -> p kt d", p=128)

    def load_weight(wname):
        idx = W_IDX[wname]
        t1 = ld(wa_h[:], w_ap(idx, 0))
        t2 = ld(wa_l[:], w_ap(idx, 1))
        return (t1, t2)

    def emit_init():
        P.op("gpsimd", lambda E: E.iota(out=iocol[:], pattern=[[0, 1]], base=0,
                                        channel_multiplier=1,
                                        allow_small_or_imprecise_dtypes=True))
        tg = P.op("gpsimd", lambda E: E.iota(out=iof[:], pattern=[[1, 128]],
                                             base=0, channel_multiplier=0,
                                             allow_small_or_imprecise_dtypes=True),
                  inc="g")
        v_op(lambda E: E.memset(ones_row[:], 1.0))
        v_op(lambda E: E.memset(zcol[:], 0.0))
        tz = v_op(lambda E: E.memset(zcol_b[:], 0.0), inc="v")
        vts(qflag[:], zcol[:, 0:1], 1.0, ALU.add, wait=(tz,))
        vts(ln2c[:], zcol[:, 0:1], 0.6931471805599453, ALU.add)
        v_op(lambda E: E.tensor_scalar(out=qflag[:], in0=qflag[:], scalar1=M64C,
                                       scalar2=-M64C, op0=ALU.add, op1=ALU.add))
        vts(I_f[:], iof[:], iocol[:, 0:1], ALU.is_equal, wait=(tg,))
        vcopy(I_b[:], I_f[:], inc="v")
        v_op(lambda E: E.tensor_scalar(out=qm1[:], in0=qflag[:], scalar1=-1.0,
                                       scalar2=1.0, op0=ALU.mult, op1=ALU.add))

        tb = []
        for b in ("bv_a", "bo_a", "bv_b", "bo_b", "fb"):
            tb.append(ld(brows[b][:], bin_[b][:]))
        prev_ev = None
        for b in bcols:
            trow = ld(vbound[0:1, :], bin_[b][:],
                      wait=(prev_ev,) if prev_ev else ())
            tg2 = None
            for mt in range(KT):
                tg2 = mm(PSC[:, mt:mt + 1], vbound[0:1, mt * 128:(mt + 1) * 128],
                         ones_row[0:1, 0:1], mt == 0, mt == KT - 1,
                         wait=(trow,) if mt == 0 else (),
                         inc="t" if mt == KT - 1 else None)
            prev_ev = scopy(bcols[b][:], PSC[:, 0:KT], wait=(tg2,), inc="s")
        tb.append(prev_ev)
        t_sc = ld(cur["m"][0:1, 0:1], scal_d[:])
        t_dec = ld(lnd_bc[0:1, 0:16], decay_d[:])

        tv = vts(cur["m"][0:1, 0:1], cur["m"][0:1, 0:1], 1e-5, ALU.max,
                 wait=(t_sc,), inc="v")
        tmm = mm(PSB[:, 0:1], ones_row[:, 0:128], cur["m"][0:1, 0:1], True, True,
                 wait=(tv,), inc="t")
        ts1 = scopy(s_bc[:], PSB[:, 0:1], wait=(tmm,), inc="s")
        ts1b = sact(srec_bc[:], s_bc[:], AF.Ln, wait=(ts1,), inc="s")
        ts2 = sact(srec_bc[:], srec_bc[:], AF.Exp, scale=-1.0, wait=(ts1b,),
                   inc="s")
        vts(brows["fb"][:], brows["fb"][:], srec_bc[0:1, 0:1], ALU.mult,
            wait=tuple(tb) + (ts2,), inc="v")

        tl = sact(lnd_bc[0:1, 0:16], lnd_bc[0:1, 0:16], AF.Ln, wait=(t_dec,),
                  inc="s")
        tmm2 = mm(PSB[:, 8:24], ones_row[:, 0:128], lnd_bc[0:1, 0:16], True, True,
                  wait=(tl, ts1), inc="t")
        tl2 = scopy(lnd_bc[:], PSB[:, 8:24], wait=(tmm2,), inc="s")
        tl3 = sact(sdf[:], lnd_bc[:], AF.Exp, scale=iocol[:, 0:1], wait=(tl2,),
                   inc="s")
        vts(sdf[:], sdf[:], SCALE_B, ALU.mult, wait=(tl3,), inc="v")
        P.barrier()

        tld = ld(WSI[:], tw_d.rearrange("(kt p) d -> p kt d", p=128))
        tv1 = vcopy(WS32[:], WSI[:], wait=(tld,), inc="v")
        tv2 = vts(WSH[:], WS32[:], -1.0, ALU.add, wait=(tv1,), inc="v")
        ev_toks = []
        pe_ev = {}
        for okt in range(KT):
            for g in range(2):
                pe = PEA if g == 0 else PEB
                key = (okt * 2 + g) % 2
                tts = None
                for j in range(4):
                    ikt = g * 4 + j
                    w = (tv2,)
                    if okt * 2 + g >= 2:
                        w = (tv2, pe_ev.get(okt * 2 + g - 2))
                    tts = mm(pe[:, j * 128:(j + 1) * 128],
                             WSH[:, okt, ikt * 128:(ikt + 1) * 128], I_b[:],
                             j == 0, j == 3, wait=w,
                             inc="t" if j == 3 else None, tr=True)
                dst = WSL[:, g * 4:(g + 1) * 4, okt * 128:okt * 128 + 128]
                src = pe[:, 0:512].rearrange("p (a b) -> p a b", a=4)
                ev = v_op(lambda E, d=dst, s_=src:
                          E.tensor_copy(out=d, in_=s_), wait=(tts,), inc="v")
                pe_ev[okt * 2 + g] = ev
                ev_toks.append(ev)
        st(scr_wt.rearrange("(kt p) d -> p kt d", p=128), WSL[:],
           wait=tuple(ev_toks))
        P.barrier()

        for w, idx in W_IDX.items():
            tldw = ld(WS32[:], win[w].rearrange("(kt p) d -> p kt d", p=128))
            th = vcopy(WSH[:], WS32[:], wait=(tldw,), inc="v")
            tl_ = v_op(lambda E: E.tensor_sub(out=WSL[:], in0=WS32[:],
                                              in1=WSH[:]), wait=(th,), inc="v")
            sh = st(w_ap(idx, 0), WSH[:], wait=(th,))
            sl = st(w_ap(idx, 1), WSL[:], wait=(tl_,))
            P.op("sync", lambda E: E.nop(), wait=(sh, sl))
        P.barrier()

    def emit_stats(src_ap, row0=0):
        vt = [None] * NT
        stk = [None] * NT
        for t in range(NT):
            slot = t % 2
            w = []
            if t >= 2:
                w = [vt[t - 2], stk[t - 2]]
            l_ = ld(xin[:, slot, :],
                    src_ap[row0 + t * 128:row0 + (t + 1) * 128, :], wait=tuple(w))
            vt[t] = v_op(lambda E, sl=slot, tt=t: E.tensor_scalar(
                out=STS0[:], in0=xin[:, sl, :], scalar1=1.0, scalar2=0.0,
                op0=ALU.mult, op1=ALU.add,
                accum_out=sums_c[:, tt:tt + 1]), wait=(l_,), inc="v")
            stk[t] = s_op(lambda E, sl=slot, tt=t: E.activation(
                out=STS1[:], in_=xin[:, sl, :], func=AF.Square,
                accum_out=sumsq_c[:, tt:tt + 1]), wait=(l_,), inc="s")
        tv1 = vts(m_c[:], sums_c[:], 1.0 / D, ALU.mult,
                  wait=(vt[NT - 1], stk[NT - 1]), inc="v")
        tv2 = vtt(var_c[:], m_c[:], m_c[:], ALU.mult, wait=(tv1,), inc="v")
        tv3 = vstt(var_c[:], sumsq_c[:], 1.0 / D, var_c[:], ALU.mult,
                   ALU.subtract, wait=(tv2,), inc="v")
        tv4 = vts(var_c[:], var_c[:], EPS, ALU.add, wait=(tv3,), inc="v")
        s1 = sact(r_c[:], var_c[:], AF.Ln, wait=(tv4,), inc="s")
        s2 = sact(r_c[:], r_c[:], AF.Exp, scale=-0.5, wait=(s1,), inc="s")
        return s2

    def emit_ln_fm(src_ap, row0, ntiles, tile0, dst_h, dst_l, col0, wtok):
        ev_h = {}
        ev_l = {}
        vtok = {}
        for t in range(ntiles):
            slot = t % 2
            buf = t % 2
            gt = tile0 + t
            w = [wtok]
            if t >= 2:
                w.append(vtok[t - 2])
            l_ = ld(xin[:, slot, :],
                    src_ap[row0 + t * 128:row0 + (t + 1) * 128, :], wait=tuple(w))
            wv = [l_]
            if t >= 2:
                wv += [ev_h[t - 2], ev_l[t - 2]]
            vts(FS[buf][:], xin[:, slot, :], m_c[:, gt:gt + 1], ALU.subtract,
                wait=tuple(wv))
            th = vts(LNh[:, buf, :], FS[buf][:], r_c[:, gt:gt + 1], ALU.mult,
                     inc="v")
            tl_ = vstt(LNl[:, buf, :], FS[buf][:], r_c[:, gt:gt + 1],
                       LNh[:, buf, :], ALU.mult, ALU.subtract, inc="v")
            vtok[t] = tl_
            col = col0 + t * 128
            for (pe, lnbuf, wt, evd, dst) in ((PEA, LNh, th, ev_h, dst_h),
                                              (PEB, LNl, tl_, ev_l, dst_l)):
                tt = None
                for kt in range(KT):
                    w2 = [wt]
                    if t >= 1:
                        w2.append(evd.get(t - 1))
                    tt = mm(pe[:, kt * 128:(kt + 1) * 128],
                            lnbuf[:, buf, kt * 128:(kt + 1) * 128], I_b[:],
                            kt == 0, kt == KT - 1, wait=tuple(x for x in w2 if x),
                            inc="t" if kt == KT - 1 else None, tr=True)
                dstap = dst[:, :, col:col + 128]
                srcap = pe[:, 0:1024].rearrange("p (a b) -> p a b", a=KT)
                if pe is PEA:
                    evd[t] = scopy(dstap, srcap, wait=(tt,), inc="s")
                else:
                    evd[t] = v_op(lambda E, d=dstap, s_=srcap:
                                  E.tensor_copy(out=d, in_=s_), wait=(tt,),
                                  inc="v")
        return (ev_h[ntiles - 1], ev_l[ntiles - 1])

    def proj_fm(wtoks, mov_h, mov_l, ncols, dst_fn, bname, col0=0):
        evt = {}
        nunit = 0
        for ntc in range(ncols // 512):
            for mt in range(KT):
                bank = (PSB, PSC)[nunit % 2]
                w0 = list(wtoks)
                if nunit >= 2:
                    w0 += list(evt[nunit - 2])
                tokmm = None
                for p, (wsel, msel) in enumerate(((wa_h, mov_h), (wa_l, mov_h),
                                                  (wa_h, mov_l))):
                    for kt in range(KT):
                        last = p == 2 and kt == KT - 1
                        tokmm = mm(bank[:, 0:512],
                                   wsel[:, kt, mt * 128:(mt + 1) * 128],
                                   msel[:, kt, ntc * 512:(ntc + 1) * 512],
                                   p == 0 and kt == 0, last,
                                   wait=tuple(w0) if (p == 0 and kt == 0) else (),
                                   inc="t" if last else None)
                hi_ap, lo_ap = dst_fn(mt, col0 + ntc * 512, 512)
                te1 = sact(hi_ap, bank[:, 0:512], AF.Identity,
                           bias=bcols[bname][:, mt:mt + 1], wait=(tokmm,),
                           inc="s")
                te2 = vstt(lo_ap, bank[:, 0:512], bcols[bname][:, mt:mt + 1],
                           hi_ap, ALU.add, ALU.subtract, wait=(tokmm, te1),
                           inc="v")
                evt[nunit] = (te1, te2)
                nunit += 1
        return evt[nunit - 1]

    def proj_fm_pairs(wtoks, mov_h, mov_l, ncols, dst_fn, bname):
        evt = {}
        nunit = 0
        psel = ((wa_h, mov_h), (wa_l, mov_h), (wa_h, mov_l))
        for pair in range(ncols // 1024):
            for mt in range(KT):
                g = nunit % 2
                b0 = PSA[:, g * 1024:g * 1024 + 512]
                b1 = PSA[:, g * 1024 + 512:g * 1024 + 1024]
                w0 = list(wtoks)
                if nunit >= 2:
                    w0 += list(evt[nunit - 2])
                tokmm = None
                for p, (wsel, msel) in enumerate(psel):
                    for kt in range(KT):
                        lhs = wsel[:, kt, mt * 128:(mt + 1) * 128]
                        first = p == 0 and kt == 0
                        last = p == 2 and kt == KT - 1
                        mm(b0, lhs,
                           msel[:, kt, pair * 1024:pair * 1024 + 512],
                           first, last, wait=tuple(w0) if first else ())
                        tokmm = mm(b1, lhs,
                                   msel[:, kt, pair * 1024 + 512:
                                        pair * 1024 + 1024],
                                   first, last,
                                   inc="t" if last else None)
                toks = []
                for j, bk in enumerate((b0, b1)):
                    c0 = pair * 1024 + j * 512
                    hi_ap, lo_ap = dst_fn(mt, c0, 512)
                    te1 = sact(hi_ap, bk, AF.Identity,
                               bias=bcols[bname][:, mt:mt + 1], wait=(tokmm,),
                               inc="s")
                    te2 = vstt(lo_ap, bk, bcols[bname][:, mt:mt + 1],
                               hi_ap, ALU.add, ALU.subtract,
                               wait=(tokmm, te1), inc="v")
                    toks += [te1, te2]
                evt[nunit] = tuple(toks)
                nunit += 1
        return evt

    def proj_tm(first_w, act_fn, wmov, ntiles, brow, evac, passes=3):
        evt = {}
        for tt in range(ntiles):
            g = tt % 2
            b0 = PSA[:, g * 1024:g * 1024 + 512]
            b1 = PSA[:, g * 1024 + 512:g * 1024 + 1024]
            w0 = list(first_w)
            if tt >= 2:
                w0 += list(evt[tt - 2])
            psel = (((0, 0), (1, 0), (0, 1)) if passes == 3 else ((0, 0),))
            first = True
            for (ah, wl_) in psel:
                for kt in range(KT):
                    lhs = act_fn(ah, kt, tt)
                    mm(b0, lhs, wmov[wl_][:, kt, 0:512], first, False,
                       wait=tuple(w0) if first else ())
                    mm(b1, lhs, wmov[wl_][:, kt, 512:1024], first, False)
                    first = False
            mm(b0, ones_row[:, 0:128], brow[0:1, 0:512], False, True)
            tokmm = mm(b1, ones_row[:, 0:128], brow[0:1, 512:1024], False,
                       True, inc="t")
            evt[tt] = tuple(evac(tt, 0, b0, tokmm)) +                 tuple(evac(tt, 1, b1, tokmm))
        return evt

    def emit_attn_a(src_ap, dst_ap):
        stok = emit_stats(src_ap)
        P.barrier()
        emit_ln_fm(src_ap, 0, NT, 0, LTh, LTl, 0, stok)
        P.barrier()
        if debug_a:
            st(dbg["a_lth"].rearrange("(kt p) t -> p kt t", p=128), LTh[:])
            st(dbg["a_ltl"].rearrange("(kt p) t -> p kt t", p=128), LTl[:])
            P.barrier()

        wv = load_weight("wv_a")
        spill_toks = {}

        def evac_v(tt, ntc, bank, tokmm):
            wv2 = []
            if tt >= 2:
                wv2 = [spill_toks[tt - 2]]
            te = scopy(xout[:, tt % 2, ntc * 512:(ntc + 1) * 512], bank,
                       wait=(tokmm,) + tuple(wv2), inc="s")
            if ntc == 1:
                spill_toks[tt] = st(scr_v[tt * 128:(tt + 1) * 128, :],
                                    xout[:, tt % 2, :], wait=(te,))
            return (te,)

        proj_tm(wv, lambda h, kt, tt: (LTh if h == 0 else LTl)[:, kt,
                                                              tt * 128:(tt + 1) * 128],
                (wa_h, wa_l), NT, brows["bv_a"], evac_v, passes=3)
        P.barrier()

        wk = load_weight("wk_a")
        proj_fm_pairs(wk, LTh, LTl, S,
                      lambda mt, c0, w: (Kh[:, mt, c0:c0 + w],
                                         Kl[:, mt, c0:c0 + w]), "bk_a")
        P.barrier()
        if debug_a:
            st(dbg["a_kh"].rearrange("(kt p) t -> p kt t", p=128), Kh[:])
            st(dbg["a_kl"].rearrange("(kt p) t -> p kt t", p=128), Kl[:])
            P.barrier()

        wq = load_weight("wq_a")

        def qdst(mt, c0, w):
            if c0 < 512:
                return (QSh[:, mt, c0:c0 + w], QSl[:, mt, c0:c0 + w])
            return (LTh[:, mt, c0 - 512:c0 - 512 + w],
                    LTl[:, mt, c0 - 512:c0 - 512 + w])

        proj_fm(wq, LTh, LTl, S, qdst, "bq_a")
        P.barrier()
        if debug_a:
            st(dbg["a_qh"][:, 0:512].rearrange("(kt p) t -> p kt t", p=128),
               QSh[:])
            st(dbg["a_ql"][:, 0:512].rearrange("(kt p) t -> p kt t", p=128),
               QSl[:])
            st(dbg["a_qh"][:, 512:S].rearrange("(kt p) t -> p kt t", p=128),
               LTh[:, :, 0:S - 512])
            st(dbg["a_ql"][:, 512:S].rearrange("(kt p) t -> p kt t", p=128),
               LTl[:, :, 0:S - 512])
            P.barrier()

        def q_ap(hilo, kt, c0):
            if c0 < 512:
                t_ = (QSh, QSl)[hilo]
                return t_[:, kt, c0:c0 + 128]
            t_ = (LTh, LTl)[hilo]
            return t_[:, kt, c0 - 512:c0 - 512 + 128]

        prev_read = None
        at_ev = {}
        dg_tok = {}
        for idx in range(16):
            n, qt = idx // 2, idx % 2
            E = Ebuf[idx % 2]
            c0 = n * 256 + qt * 128
            tokmm = None
            for p, hl in enumerate(((0, 0), (1, 0), (0, 1))):
                qh, kh = hl
                for kt in range(KT):
                    lhs = q_ap(qh, kt, c0)
                    for m in range(8):
                        start = p == 0 and kt == 0 and m % 2 == 0
                        stop = p == 2 and kt == KT - 1 and m % 2 == 1
                        w0 = ()
                        if p == 0 and kt == 0 and m == 0 and                                 prev_read is not None:
                            w0 = tuple(prev_read)
                        tokmm = mm(PSA[:, m * 256:(m + 1) * 256], lhs,
                                   (Kh, Kl)[kh][:, kt, m * 256:(m + 1) * 256],
                                   start, stop, wait=w0,
                                   inc="t" if (m == 7 and p == 2 and
                                               kt == KT - 1) else None)
            tmx = v_op(lambda E_: E_.reduce_max(out=cur["negM"][:],
                                                in_=PSA[:, 0:2048], axis=AX.X,
                                                negate=True), wait=(tokmm,))
            tb_ = vts(cur["bias"][:], cur["negM"][:], SCALE_A, ALU.mult, inc="v")
            texp = sact(E[:, 0:2048], PSA[:, 0:2048], AF.Exp,
                        bias=cur["bias"][:, 0:1], scale=SCALE_A, wait=(tb_,),
                        inc="s")
            prev_read = (tb_, texp)
            tsum = v_op(lambda E_, Ex=E: E_.reduce_sum(
                out=sums8[:], in_=Ex[:, 0:2048].rearrange("p (a b) -> p a b", a=8),
                axis=AX.X), wait=(texp,), inc="v")
            tln = sact(tmp8[:], sums8[:], AF.Ln, wait=(tsum,), inc="s")
            trv = sact(rinv8[:], tmp8[:], AF.Exp, scale=-1.0, wait=(tln,),
                       inc="s")
            trv2 = sact(rinv8x2[:], tmp8[:], AF.Exp, scale=-1.0,
                        bias=ln2c[:, 0:1], wait=(tln,), inc="s")
            for m in range(8):
                wv2 = [trv, trv2]
                if m >= 2:
                    wv2.append(dg_tok[(idx, m - 2)])
                elif idx > 0:
                    wv2.append(dg_tok[(idx - 1, m + 6)])
                rsrc = rinv8x2 if m == n else rinv8
                tD = vts(Dbuf[m % 2][:], I_f[:], rsrc[:, m:m + 1], ALU.mult,
                         wait=tuple(wv2), inc="v")
                wmm = [tD]
                if m == 0 and idx > 0:
                    wmm += list(at_ev[idx - 1])
                mm(PSB[:, 0:128], E[:, m * 256:m * 256 + 128], Dbuf[m % 2][:],
                   m == 0, m == 7, wait=tuple(wmm))
                dg_tok[(idx, m)] = mm(PSC[:, 0:128],
                                      E[:, m * 256 + 128:m * 256 + 256],
                                      Dbuf[m % 2][:], m == 0, m == 7, inc="t")
            ta1 = scopy(ATa[:, n * 2, qt * 128:qt * 128 + 128], PSB[:, 0:128],
                        wait=(dg_tok[(idx, 7)],), inc="s")
            ta2 = scopy(ATa[:, n * 2 + 1, qt * 128:qt * 128 + 128], PSC[:, 0:128],
                        wait=(dg_tok[(idx, 7)],), inc="s")
            at_ev[idx] = (ta1, ta2)
        P.barrier()
        if debug_a:
            st(dbg["a_at"].rearrange("(a p) q -> p a q", p=128), ATa[:])
            st(dbg["a_e0"][:], Ebuf[0][:])
            st(dbg["a_e1"][:], Ebuf[1][:])
            st(dbg["a_r8"][0:1, :].rearrange("one (p w) -> (one p) w", w=8),
               rinv8[:])
            st(dbg["a_r8"][1:2, :].rearrange("one (p w) -> (one p) w", w=8),
               sums8[:])
            P.barrier()

        vload = {}
        avtok = {}
        evtok = {}
        nunit = 0
        for n in range(8):
            vrb = (xin, xout)[n % 2]
            w0 = []
            if n >= 2:
                w0 = [avtok[n - 2]]
            vload[n] = (
                ld(vrb[:, 0, :], scr_v[n * 256:n * 256 + 128, :],
                   wait=tuple(w0)),
                ld(vrb[:, 1, :], scr_v[n * 256 + 128:n * 256 + 256, :],
                   wait=tuple(w0)))
            for et in range(KT):
                bank = (PSB, PSC)[nunit % 2]
                w1 = list(vload[n])
                if nunit >= 2:
                    w1 += list(evtok[nunit - 2])
                mm(bank[:, 0:256], vrb[:, 0, et * 128:(et + 1) * 128],
                   ATa[:, n * 2, 0:256], True, False, wait=tuple(w1))
                tokav = mm(bank[:, 0:256], vrb[:, 1, et * 128:(et + 1) * 128],
                           ATa[:, n * 2 + 1, 0:256], False, True, inc="t")
                if et == KT - 1:
                    avtok[n] = tokav
                th = sact(CBh[:, et, n * 256:(n + 1) * 256], bank[:, 0:256],
                          AF.Identity, wait=(tokav,), inc="s")
                tl_ = vtt(CBl[:, et, n * 256:(n + 1) * 256], bank[:, 0:256],
                          CBh[:, et, n * 256:(n + 1) * 256], ALU.subtract,
                          wait=(tokav, th), inc="v")
                evtok[nunit] = (th, tl_)
                nunit += 1
        P.barrier()
        if debug_a:
            st(dbg["a_cbh"].rearrange("(kt p) t -> p kt t", p=128), CBh[:])
            st(dbg["a_cbl"].rearrange("(kt p) t -> p kt t", p=128), CBl[:])
            P.barrier()

        wo = load_weight("wo_a")
        emit_out_proj(wo, lambda h, kt, tt: (CBh if h == 0 else CBl)[
            :, kt, tt * 128:(tt + 1) * 128], (wa_h, wa_l), NT, 0,
            brows["bo_a"], src_ap, 0, dst_ap, 0, passes=3, ffn_scale=False)
        P.barrier()

    def emit_out_proj(first_w, act_fn, wmov, ntiles, tile0, brow,
                      res_ap, res_row0, dst_ap, dst_row0, passes, ffn_scale):
        res_ld = {}
        st_tok = {}

        def evac(tt, ntc, bank, tokmm):
            if ntc == 0 and tt not in res_ld:
                w = []
                if tt >= 2:
                    w = [st_tok[tt - 2]]
                res_ld[tt] = ld(
                    xin[:, tt % 2, :],
                    res_ap[res_row0 + (tile0 + tt) * 128:
                           res_row0 + (tile0 + tt + 1) * 128, :], wait=tuple(w))
            sl = tt % 2
            wst = (st_tok.get(tt - 2),) if tt >= 2 else ()
            if ffn_scale:
                te = vstt(xout[:, sl, ntc * 512:(ntc + 1) * 512], bank,
                          s_bc[:, 0:1], xin[:, sl, ntc * 512:(ntc + 1) * 512],
                          ALU.mult, ALU.add,
                          wait=(tokmm, res_ld[tt]) + wst, inc="v")
            else:
                te = vtt(xout[:, sl, ntc * 512:(ntc + 1) * 512], bank,
                         xin[:, sl, ntc * 512:(ntc + 1) * 512], ALU.add,
                         wait=(tokmm, res_ld[tt]) + wst, inc="v")
            if ntc == 1:
                st_tok[tt] = st(
                    dst_ap[dst_row0 + (tile0 + tt) * 128:
                           dst_row0 + (tile0 + tt + 1) * 128, :],
                    xout[:, sl, :], wait=(te,))
            return (te,)

        proj_tm(first_w, act_fn, wmov, ntiles, brow, evac, passes=passes)

    def emit_ffn(src_ap, dst_ap):
        stok = emit_stats(src_ap)
        twt = ld(wt_f[:], scr_wt.rearrange("(kt p) d -> p kt d", p=128))
        stok = vts(rs_c[:], r_c[:], srec_bc[:, 0:1], ALU.mult, wait=(stok,),
                   inc="v")
        P.barrier()
        pe_tok = {}
        ev_tok = {}
        for t in range(NT):
            slot = t % 2
            w = [stok]
            if t >= 2:
                w.append(pe_tok[t - 2])
            l_ = ld(xin[:, slot, :], src_ap[t * 128:(t + 1) * 128, :],
                    wait=tuple(w))
            wv = [l_]
            if t >= 1:
                wv.append(pe_tok[t - 1])
            vts(FS[0][:], xin[:, slot, :], m_c[:, t:t + 1], ALU.subtract,
                wait=tuple(wv))
            vts(FS[1][:], FS[0][:], rs_c[:, t:t + 1], ALU.mult)
            v_op(lambda E: E.tensor_scalar(out=FS[2][:], in0=FS[1][:],
                                           scalar1=M32C, scalar2=-M32C,
                                           op0=ALU.add, op1=ALU.add))
            v_op(lambda E: E.tensor_scalar(out=FS[3][:], in0=FS[1][:],
                                           scalar1=M64C, scalar2=-M64C,
                                           op0=ALU.add, op1=ALU.add))
            vts(FS[0][:], FS[2][:], qm1[:, 0:1], ALU.mult)
            vtt(FS[1][:], FS[0][:], FS[3][:], ALU.add)
            tq = v_op(lambda E: E.tensor_scalar(out=xqb[:],
                                                in0=FS[1][:], scalar1=127.0,
                                                scalar2=-127.0, op0=ALU.min,
                                                op1=ALU.max), inc="v")
            pe = (PEA, PEB)[t % 2]
            tt_ = None
            for kt in range(KT):
                w2 = [tq]
                if t >= 2:
                    w2.append(ev_tok[t - 2])
                tt_ = mm(pe[:, kt * 128:(kt + 1) * 128],
                         xqb[:, kt * 128:(kt + 1) * 128], I_b[:],
                         kt == 0, kt == KT - 1, wait=tuple(w2),
                         inc="t" if kt == KT - 1 else None, tr=True)
            pe_tok[t] = tt_
            ev_tok[t] = scopy(XIT[:, :, t * 128:(t + 1) * 128],
                              pe[:, 0:1024].rearrange("p (a b) -> p a b", a=KT),
                              wait=(tt_,), inc="s")
        P.barrier()
        if debug_ffn:
            st(dbg["xit"].rearrange("(kt p) t -> p kt t", p=128), XIT[:])
            st(dbg["mc"][:], m_c[:])
            st(dbg["rc"][:], r_c[:])
            st(dbg["sc"][0:1, :].rearrange("one (p w) -> (one p) w", w=1),
               s_bc[:])
            st(dbg["sc"][1:2, :].rearrange("one (p w) -> (one p) w", w=1),
               srec_bc[:])
            st(dbg["wt"].rearrange("(kt p) t -> p kt t", p=128), wt_f[:])
            st(dbg["fbs"][:], brows["fbs"][:])
            for i_ in range(4):
                st(dbg["fs"][i_ * 128:(i_ + 1) * 128, :], FS[i_][:])
            st(dbg["xqb"][:], xqb[:])
            st(dbg["qf"][:], qflag[:])
            st(dbg["qm1"][:], qm1[:])
            st(dbg["cur"][0:1, :].rearrange("one (p w) -> (one p) w", w=1),
               cur["m"][:])
            st(dbg["cur"][1:2, :].rearrange("one (p w) -> (one p) w", w=1),
               cur["r"][:])
            st(dbg["cur"][2:3, :].rearrange("one (p w) -> (one p) w", w=1),
               cur["rs"][:])
            st(dbg["xin1"][:], xin[:, 1, :])
            P.barrier()
        emit_out_proj((twt, ev_tok[NT - 1]),
                      lambda h, kt, tt: XIT[:, kt, tt * 128:(tt + 1) * 128],
                      (wt_f, wt_f), NT, 0, brows["fbs"], src_ap, 0, dst_ap, 0,
                      passes=1, ffn_scale=True)
        P.barrier()

    def emit_attn_b(src_ap, dst_ap):
        stok = emit_stats(src_ap)
        P.barrier()
        for quarter in range(4):
            row0 = quarter * 512
            emit_ln_fm(src_ap, row0, 4, quarter * 4, LTbh, LTbl, 0, stok)
            P.barrier()

            wv = load_weight("wv_b")
            vp_tok = {}

            def evac_vb(tt, ntc, bank, tokmm):
                dst = vb[0][:, tt, :] if tt < 2 else vb[1][:, tt - 2, :]
                te = scopy(dst[:, ntc * 512:(ntc + 1) * 512], bank,
                           wait=(tokmm,), inc="s")
                if ntc == 1 and tt < 3:
                    src_t = vb[0] if tt < 2 else vb[1]
                    si = tt if tt < 2 else tt - 2
                    vp_tok[tt + 1] = st(vprev[0:1, tt + 1, :],
                                        src_t[127:128, si, :], wait=(te,))
                return (te,)

            proj_tm(wv, lambda h, kt, tt: (LTbh if h == 0 else LTbl)[
                :, kt, tt * 128:(tt + 1) * 128], (wa_h, wa_l), 4,
                brows["bv_b"], evac_vb, passes=3)
            if quarter == 0:
                vp_tok[0] = v_op(lambda E: E.memset(vprev[0:1, 0, :], 0.0),
                                 inc="v")
            else:
                vp_tok[0] = v_op(lambda E: E.tensor_copy(out=vprev[0:1, 0, :],
                                                         in_=vbound[0:1, :]),
                                 inc="v")
            P.barrier()

            wk = load_weight("wk_b")
            proj_fm(wk, LTbh, LTbl, 512,
                    lambda mt, c0, w: (kbh[:, mt, c0:c0 + w],
                                       kbl[:, mt, c0:c0 + w]), "bk_b")
            P.barrier()

            wq = load_weight("wq_b")
            proj_fm(wq, LTbh, LTbl, 512,
                    lambda mt, c0, w: (QSbh[:, mt, c0:c0 + w],
                                       QSbl[:, mt, c0:c0 + w]), "bq_b")
            P.barrier()

            def qb_ap(hilo, h, c):
                t_ = (QSbh, QSbl)[hilo]
                p0 = (h % 2) * 64
                return t_[p0:p0 + 64, h // 2, c * 128:(c + 1) * 128]

            def kb_ap(hilo, h, c0, c1):
                t_ = (kbh, kbl)[hilo]
                p0 = (h % 2) * 64
                return t_[p0:p0 + 64, h // 2, c0:c1]

            exp_tok = {}
            av_ev = {}
            tr_ev_all = {}
            for h in range(16):
                tp = (64, 0) if h % 2 == 1 else None
                wsd = [exp_tok.get((h - 1, 3))] if h > 0 else []
                tsd = vcopy(cur["sd"][:], sdf[:, h:h + 1], wait=tuple(wsd),
                            inc="v")
                for c in range(4):
                    bank = (PSB, PSC)[c % 2]
                    w0 = [tsd]
                    if (h, c - 2) in exp_tok:
                        w0.append(exp_tok[(h, c - 2)])
                    elif h > 0 and c < 2:
                        w0.append(exp_tok[(h - 1, c + 2)])
                    first = True
                    for p, hl in enumerate(((0, 0), (1, 0), (0, 1))):
                        qh, kh = hl
                        if c == 0 and quarter == 0:
                            kprev = zcol_b[(h % 2) * 64:(h % 2) * 64 + 64, 0:1]
                        elif c == 0:
                            kprev = (kbound_h, kbound_l)[kh][
                                (h % 2) * 64:(h % 2) * 64 + 64,
                                h // 2:h // 2 + 1]
                        else:
                            kprev = kb_ap(kh, h, c * 128 - 1, c * 128)
                        mm(bank[:, 0:1], qb_ap(qh, h, c), kprev,
                           first, False, wait=tuple(w0) if first else (), tp=tp)
                        first = False
                    tokmm = None
                    for p, hl in enumerate(((0, 0), (1, 0), (0, 1))):
                        qh, kh = hl
                        tokmm = mm(bank[:, 1:129], qb_ap(qh, h, c),
                                   kb_ap(kh, h, c * 128, (c + 1) * 128),
                                   False, p == 2, tp=tp,
                                   inc="t" if p == 2 else None)
                    v_op(lambda E, b=bank: E.reduce_max(
                        out=cur["negM"][:], in_=b[:, 0:129], axis=AX.X,
                        negate=True), wait=(tokmm,))
                    tb_ = vts(cur["bias"][:], cur["negM"][:], sdf[:, h:h + 1],
                              ALU.mult, inc="v")
                    texp = sact(Eall[:, c, 0:129], bank[:, 0:129], AF.Exp,
                                bias=cur["bias"][:, 0:1],
                                scale=cur["sd"][:, 0:1], wait=(tb_,), inc="s")
                    exp_tok[(h, c)] = texp
                    v_op(lambda E, c_=c: E.reduce_sum(
                        out=sums8b[:, c_:c_ + 1], in_=Eall[:, c_, 0:129],
                        axis=AX.X), wait=(texp,),
                        inc="v" if c == 3 else None)
                tln = sact(rinv8b[:, 0:4], sums8b[:, 0:4], AF.Ln,
                           wait=(("v", P.cnt["v"]),), inc="s")
                trv = sact(rinv8b[:, 0:4], rinv8b[:, 0:4], AF.Exp, scale=-1.0,
                           wait=(tln,), inc="s")
                tr_tok = {}
                for c in range(4):
                    w0 = [trv]
                    if c >= 1:
                        w0.append(tr_tok[c - 1])
                    tA = vts(Abuf[:, 0:129], Eall[:, c, 0:129],
                             rinv8b[:, c:c + 1], ALU.mult, wait=tuple(w0),
                             inc="v")
                    sl = (c % 2) * 512
                    w1 = [tA]
                    if (h, c - 2) in tr_ev_all:
                        w1 += list(tr_ev_all[(h, c - 2)])
                    elif h > 0 and (h - 1, c + 2) in tr_ev_all:
                        w1 += list(tr_ev_all[(h - 1, c + 2)])
                    mm(PSA[:, sl:sl + 128], Abuf[:, 1:129], I_f[:], True, True,
                       wait=tuple(x for x in w1 if x), tr=True)
                    trt = mm(PSA[0:1, sl + 128:sl + 256], Abuf[:, 0:1], I_f[:],
                             True, True, inc="t", tr=True)
                    tr_tok[c] = trt
                    ta1 = scopy(ATb[:, 0:128], PSA[:, sl:sl + 128], wait=(trt,),
                                inc="s")
                    ta2 = scopy(ATp[0:1, 0:128], PSA[0:1, sl + 128:sl + 256],
                                wait=(trt,), inc="s")
                    tr_ev_all[(h, c)] = (ta1, ta2)
                    sa = 1024 + (c % 2) * 512
                    vt_ = vb[0] if c < 2 else vb[1]
                    vi = c if c < 2 else c - 2
                    w2 = [ta1, ta2, vp_tok[c]]
                    if (h, c - 2) in av_ev:
                        w2 += list(av_ev[(h, c - 2)])
                    elif h > 0 and (h - 1, c + 2) in av_ev:
                        w2 += list(av_ev[(h - 1, c + 2)])
                    mm(PSA[0:64, sa:sa + 128], vt_[:, vi, h * 64:h * 64 + 64],
                       ATb[:, 0:128], True, False, wait=tuple(w2))
                    tokav = mm(PSA[0:64, sa:sa + 128],
                               vprev[0:1, c, h * 64:h * 64 + 64],
                               ATp[0:1, 0:128], False, True, inc="t")
                    p0 = (h % 2) * 64
                    th_ = s_op(lambda E, p_=p0, h_=h, c_=c, sa_=sa:
                               E.activation(
                                   out=cbh_b[p_:p_ + 64, h_ // 2,
                                             c_ * 128:(c_ + 1) * 128],
                                   in_=PSA[0:64, sa_:sa_ + 128],
                                   func=AF.Identity), wait=(tokav,), inc="s")
                    tl_ = v_op(lambda E, p_=p0, h_=h, c_=c, sa_=sa:
                               E.tensor_tensor(
                                   out=cbl_b[p_:p_ + 64, h_ // 2,
                                             c_ * 128:(c_ + 1) * 128],
                                   in0=PSA[0:64, sa_:sa_ + 128],
                                   in1=cbh_b[p_:p_ + 64, h_ // 2,
                                             c_ * 128:(c_ + 1) * 128],
                                   op=ALU.subtract), wait=(tokav, th_), inc="v")
                    av_ev[(h, c)] = (th_, tl_)
            P.barrier()

            if quarter < 3:
                vcopy(kbound_h[:], kbh[:, :, 511:512].rearrange(
                    "p a b -> p (a b)"), inc="v")
                vcopy(kbound_l[:], kbl[:, :, 511:512].rearrange(
                    "p a b -> p (a b)"), inc="v")
                st(vbound[0:1, :], vb[1][127:128, 1, :])
                P.barrier()

            wo = load_weight("wo_b")
            emit_out_proj(wo, lambda hl, kt, tt: (cbh_b if hl == 0 else cbl_b)[
                :, kt, tt * 128:(tt + 1) * 128], (wa_h, wa_l), 4, quarter * 4,
                brows["bo_b"], src_ap, 0, dst_ap, 0, passes=3, ffn_scale=False)
            P.barrier()

    emit_init()
    for s_i in range(n_seq):
        xsrc = x_d[s_i * S:(s_i + 1) * S, :]
        chain = []
        cur_src = xsrc
        for j, sl in enumerate(sublayers):
            is_last = j == len(sublayers) - 1
            dst = y_d[s_i * S:(s_i + 1) * S, :] if is_last else scr_x[j % 3][:, :]
            chain.append((sl, cur_src, dst))
            cur_src = dst
        for sl, src, dst in chain:
            if sl == "a":
                emit_attn_a(src, dst)
            elif sl == "b":
                emit_attn_b(src, dst)
            else:
                emit_ffn(src, dst)
    P.barrier()

    with contextlib.ExitStack() as stack:
        sems = {}
        for name in ("dIn", "dOut", "t", "v", "s", "g",
                     "m_sync", "m_tensor", "m_vector", "m_scalar", "m_gpsimd"):
            sems[name] = stack.enter_context(nc.semaphore(name))
        block = stack.enter_context(nc.Block())
        P.replay(block, sems)
    assert nc.sbuf_bytes_remaining >= 229344 - 49120, nc.sbuf_bytes_remaining
    return nc


_NC_CACHE = {}


def _get_program():
    key = (SUBLAYERS, N_SEQ)
    if key not in _NC_CACHE:
        _NC_CACHE[key] = build_core_program()
    return _NC_CACHE[key]


def kernel(L0_attn_a_x, L0_attn_a_Wq, L0_attn_a_bq, L0_attn_a_Wk, L0_attn_a_bk,
           L0_attn_a_Wv, L0_attn_a_bv, L0_attn_a_Wo, L0_attn_a_bo,
           L0_attn_b_Wq, L0_attn_b_bq, L0_attn_b_Wk, L0_attn_b_bk,
           L0_attn_b_Wv, L0_attn_b_bv, L0_attn_b_Wo, L0_attn_b_bo,
           L0_attn_b_decay, L0_ffn_ternary_weight, L0_ffn_bias,
           L0_ffn_act_scale):
    nc = _get_program()
    x = np.ascontiguousarray(np.asarray(L0_attn_a_x, dtype=np.float32))
    common = {
        "wq_a": np.ascontiguousarray(L0_attn_a_Wq),
        "wk_a": np.ascontiguousarray(L0_attn_a_Wk),
        "wv_a": np.ascontiguousarray(L0_attn_a_Wv),
        "wo_a": np.ascontiguousarray(L0_attn_a_Wo),
        "wq_b": np.ascontiguousarray(L0_attn_b_Wq),
        "wk_b": np.ascontiguousarray(L0_attn_b_Wk),
        "wv_b": np.ascontiguousarray(L0_attn_b_Wv),
        "wo_b": np.ascontiguousarray(L0_attn_b_Wo),
        "bq_a": np.ascontiguousarray(L0_attn_a_bq.reshape(1, D)),
        "bk_a": np.ascontiguousarray(L0_attn_a_bk.reshape(1, D)),
        "bv_a": np.ascontiguousarray(L0_attn_a_bv.reshape(1, D)),
        "bo_a": np.ascontiguousarray(L0_attn_a_bo.reshape(1, D)),
        "bq_b": np.ascontiguousarray(L0_attn_b_bq.reshape(1, D)),
        "bk_b": np.ascontiguousarray(L0_attn_b_bk.reshape(1, D)),
        "bv_b": np.ascontiguousarray(L0_attn_b_bv.reshape(1, D)),
        "bo_b": np.ascontiguousarray(L0_attn_b_bo.reshape(1, D)),
        "fb": np.ascontiguousarray(L0_ffn_bias.reshape(1, D)),
        "tw": np.ascontiguousarray(L0_ffn_ternary_weight),
        "ascale": np.ascontiguousarray(
            np.asarray(L0_ffn_act_scale, dtype=np.float32).reshape(1, 1)),
        "decay": np.ascontiguousarray(L0_attn_b_decay.reshape(1, 16)),
    }
    in_maps = []
    for c in range(N_CORES):
        m = dict(common)
        m["x"] = np.ascontiguousarray(
            x[c * N_SEQ:(c + 1) * N_SEQ].reshape(N_SEQ * S, D))
        in_maps.append(m)
    res = run_bass_kernel_spmd(nc, in_maps, core_ids=list(range(N_CORES)))
    out = np.concatenate(
        [r["y"].reshape(N_SEQ, S, D) for r in res.results], axis=0)
    return out


# version 10
# speedup vs baseline: 1.0276x; 1.0266x over previous
import contextlib
import numpy as np

import concourse.bass as bass
import concourse.mybir as mybir
import concourse.bass_utils as _bass_utils
from concourse.bass_utils import run_bass_kernel_spmd

_orig_run_command = _bass_utils.run_command


def _run_command_ldwopt(argv, **kwargs):
    argv = [a.replace("--enable-ldw-opt=false", "--enable-ldw-opt=true")
            if isinstance(a, str) else a for a in argv]
    return _orig_run_command(argv, **kwargs)


_bass_utils.run_command = _run_command_ldwopt

F32 = mybir.dt.float32
BF16 = mybir.dt.bfloat16
I32 = mybir.dt.int32
AF = mybir.ActivationFunctionType
ALU = mybir.AluOpType
AX = mybir.AxisListType

N_CORES = 8
N_SEQ = 4
S = 2048
D = 1024
NT = 16
KT = 8
M32C = 12582912.0
M64C = 1.5 * 2.0 ** 52
EPS = 1e-5
SCALE_A = 0.03125
SCALE_B = 0.125
SUBLAYERS = ("a", "f1", "b", "f2")

ENGINES = ("sync", "tensor", "vector", "scalar", "gpsimd")


class Prog:

    def __init__(self):
        self.q = {e: [] for e in ENGINES}
        self.cnt = {}

    def op(self, eng, fn, wait=(), inc=None, amt=1):
        tok = None
        incs = ()
        if inc is not None:
            self.cnt[inc] = self.cnt.get(inc, 0) + amt
            tok = (inc, self.cnt[inc])
            incs = ((inc, self.cnt[inc], amt),)
        self.q[eng].append((tuple(w for w in wait if w is not None), fn, incs))
        return tok

    def barrier(self):
        self.op("sync", lambda E: E.nop(),
                wait=(("dIn", self.cnt.get("dIn", 0)),))
        self.op("gpsimd", lambda E: E.nop(),
                wait=(("dOut", self.cnt.get("dOut", 0)),))
        toks = {}
        for e in ENGINES:
            toks[e] = self.op(e, lambda E: E.nop(), inc="m_" + e)
        for e in ENGINES:
            waits = tuple(toks[o] for o in ENGINES if o != e)
            self.op(e, lambda E: E.nop(), wait=waits)

    def replay(self, block, sems):
        for eng_name in ENGINES:
            items = self.q[eng_name]

            def body(eng, items=items):
                waited = {}
                for waits, fn, incs in items:
                    for s_, v in waits:
                        if waited.get(s_, -1) < v:
                            eng.wait_ge(sems[s_], v)
                            waited[s_] = v
                    inst = fn(eng)
                    for s_, _v, amt in incs:
                        inst.then_inc(sems[s_], amt)
            getattr(block, eng_name)(body)


def build_core_program(sublayers=SUBLAYERS, n_seq=N_SEQ, debug_ffn=False, debug_a=False):
    nc = bass.Bass("TRN2", target_bir_lowering=False,
                   detect_race_conditions=False)
    P = Prog()

    x_d = nc.dram_tensor("x", [n_seq * S, D], F32, kind="ExternalInput").ap()
    y_d = nc.dram_tensor("y", [n_seq * S, D], F32, kind="ExternalOutput").ap()
    W_NAMES = ("wq_a", "wk_a", "wv_a", "wo_a", "wq_b", "wk_b", "wv_b", "wo_b")
    W_IDX = {w: i for i, w in enumerate(W_NAMES)}
    win = {w: nc.dram_tensor(w, [D, D], F32, kind="ExternalInput").ap()
           for w in W_NAMES}
    B_NAMES = ("bq_a", "bk_a", "bv_a", "bo_a", "bq_b", "bk_b", "bv_b", "bo_b", "fb")
    bin_ = {b: nc.dram_tensor(b, [1, D], F32, kind="ExternalInput").ap()
            for b in B_NAMES}
    tw_d = nc.dram_tensor("tw", [D, D], I32, kind="ExternalInput").ap()
    scal_d = nc.dram_tensor("ascale", [1, 1], F32, kind="ExternalInput").ap()
    decay_d = nc.dram_tensor("decay", [1, 16], F32, kind="ExternalInput").ap()

    dbg = {}
    if debug_ffn:
        dbg["xit"] = nc.dram_tensor("dbg_xit", [D, S], BF16,
                                    kind="ExternalOutput").ap()
        dbg["mc"] = nc.dram_tensor("dbg_mc", [128, NT], F32,
                                   kind="ExternalOutput").ap()
        dbg["rc"] = nc.dram_tensor("dbg_rc", [128, NT], F32,
                                   kind="ExternalOutput").ap()
        dbg["sc"] = nc.dram_tensor("dbg_sc", [2, 128], F32,
                                   kind="ExternalOutput").ap()
        dbg["wt"] = nc.dram_tensor("dbg_wt", [D, D], BF16,
                                   kind="ExternalOutput").ap()
        dbg["fbs"] = nc.dram_tensor("dbg_fbs", [1, D], F32,
                                    kind="ExternalOutput").ap()
        dbg["fs"] = nc.dram_tensor("dbg_fs", [4 * 128, D], F32,
                                   kind="ExternalOutput").ap()
        dbg["xqb"] = nc.dram_tensor("dbg_xqb", [128, D], BF16,
                                    kind="ExternalOutput").ap()
        dbg["qf"] = nc.dram_tensor("dbg_qf", [128, 1], F32,
                                   kind="ExternalOutput").ap()
        dbg["cur"] = nc.dram_tensor("dbg_cur", [3, 128], F32,
                                    kind="ExternalOutput").ap()
        dbg["qm1"] = nc.dram_tensor("dbg_qm1", [128, 1], F32,
                                    kind="ExternalOutput").ap()
        dbg["xin1"] = nc.dram_tensor("dbg_xin1", [128, D], F32,
                                     kind="ExternalOutput").ap()
    if debug_a:
        for nm, shp, dt_ in (("a_lth", [D, S], BF16), ("a_ltl", [D, S], BF16),
                             ("a_kh", [D, S], BF16), ("a_kl", [D, S], BF16),
                             ("a_qh", [D, S], BF16), ("a_ql", [D, S], BF16),
                             ("a_at", [2048, 256], F32),
                             ("a_e0", [128, S], F32), ("a_e1", [128, S], F32),
                             ("a_r8", [2, 1024], F32),
                             ("a_cbh", [D, S], BF16), ("a_cbl", [D, S], BF16)):
            dbg[nm] = nc.dram_tensor("dbg_" + nm, shp, dt_,
                                     kind="ExternalOutput").ap()
    scr_w = nc.dram_tensor("scr_w", [16 * D, D], BF16).ap()
    scr_wt = nc.dram_tensor("scr_wt", [D, D], BF16).ap()
    scr_v = nc.dram_tensor("scr_v", [S, D], F32).ap()
    scr_x = [nc.dram_tensor(f"scr_x{i}", [S, D], F32).ap() for i in range(3)]

    def sb(name, shape, dtype, off):
        return nc.alloc_sbuf_tensor_at(name, shape, dtype, offset=off)

    def sbm(name, shape, dtype):
        return nc.alloc_sbuf_tensor(name, shape, dtype)

    ones_row = sbm("ones_row", [1, 128], F32)
    I_f = sbm("I_f", [128, 128], F32)
    I_b = sbm("I_b", [128, 128], BF16)
    iocol = sbm("iocol", [128, 1], F32)
    iof = sbm("iof", [128, 128], F32)
    qflag = sbm("qflag", [128, 1], F32)
    qm1 = sbm("qm1", [128, 1], F32)
    zcol = sbm("zcol", [128, 8], F32)
    zcol_b = sbm("zcol_b", [128, 8], BF16)
    sdf = sbm("sdf", [128, 16], F32)
    bcols = {b: sbm("c_" + b, [128, KT], F32)
             for b in ("bq_a", "bk_a", "bq_b", "bk_b")}
    brows = {b: sbm("r_" + b, [1, D], F32)
             for b in ("bv_a", "bo_a", "bv_b", "bo_b", "fb")}
    brows["fbs"] = brows["fb"]
    s_bc = sbm("s_bc", [128, 1], F32)
    srec_bc = sbm("srec_bc", [128, 1], F32)
    sums_c = sbm("sums_c", [128, NT], F32)
    sumsq_c = sbm("sumsq_c", [128, NT], F32)
    m_c = sbm("m_c", [128, NT], F32)
    r_c = sbm("r_c", [128, NT], F32)
    rs_c = sbm("rs_c", [128, NT], F32)
    var_c = sbm("var_c", [128, NT], F32)
    CURS = ("m", "r", "rs", "bias", "negM", "rv", "sd", "b")
    cur = {k: sbm("cur_" + k, [128, 1], F32) for k in CURS}
    sums8 = sbm("sums8", [128, 8], F32)
    tmp8 = sbm("tmp8", [128, 8], F32)
    rinv8 = sbm("rinv8", [128, 8], F32)
    rinv8x2 = sbm("rinv8x2", [128, 8], F32)
    ln2c = sbm("ln2c", [128, 1], F32)
    sums16 = sbm("sums16", [128, 16], F32)
    tmp16 = sbm("tmp16", [128, 16], F32)
    rinv16 = sbm("rinv16", [128, 16], F32)
    rinv16x2 = sbm("rinv16x2", [128, 16], F32)
    negM8 = sbm("negM8", [128, 8], F32)
    bias8 = sbm("bias8", [128, 8], F32)
    sums8b = sbm("sums8b", [128, 8], F32)
    rinv8b = sbm("rinv8b", [128, 8], F32)
    Dbuf = [sbm(f"Dbuf{i}", [128, 128], F32) for i in range(2)]
    Abuf = sbm("Abuf", [128, 132], F32)
    ATb = sbm("ATb", [128, 132], F32)
    ATp = sbm("ATp", [1, 132], F32)
    kbound_h = sbm("kbound_h", [128, KT], BF16)
    kbound_l = sbm("kbound_l", [128, KT], BF16)
    vbound = sbm("vbound", [1, D], F32)
    lnd_bc = sbm("lnd_bc", [128, 16], F32)
    assert nc.SBUF_PARTITION_SIZE_BYTES - nc.sbuf_bytes_remaining // 1 >= 0

    OFF_XIN = 49120
    OFF_XOUT = 57312
    OFF_WA = 65504
    OFF_BIG1 = 98272
    OFF_BIG2 = 163808
    assert OFF_BIG2 + 64 * 1024 <= 229344

    xin = sb("xin", [128, 2, D], F32, OFF_XIN)
    xout = sb("xout", [128, 2, D], F32, OFF_XOUT)
    STS0 = sb("STS0", [128, D], F32, OFF_WA)
    STS1 = sb("STS1", [128, D], F32, OFF_WA + 4 * 1024)
    FS = [sb(f"FS{i}", [128, D], F32, OFF_WA + i * 4 * 1024) for i in range(4)]
    LNh = sb("LNh", [128, 2, D], BF16, OFF_WA + 8 * 1024)
    LNl = sb("LNl", [128, 2, D], BF16, OFF_WA + 12 * 1024)
    xqb = sb("xqb", [128, D], BF16, OFF_WA)
    wa_h = sb("wa_h", [128, KT, D], BF16, OFF_WA)
    wa_l = sb("wa_l", [128, KT, D], BF16, OFF_WA + 16 * 1024)
    wt_f = sb("wt_f", [128, KT, D], BF16, OFF_WA + 16 * 1024)
    Ebuf = [sb("Ebuf0", [128, S], F32, OFF_WA),
            sb("Ebuf1", [128, S], F32, OFF_WA + 8 * 1024)]
    ATa = sb("ATa", [128, 16, 256], F32, OFF_WA + 16 * 1024)

    LTh = sb("LTh", [128, KT, S], BF16, OFF_BIG1)
    LTl = sb("LTl", [128, KT, S], BF16, OFF_BIG1 + 32 * 1024)
    QSh = sb("QSh", [128, KT, 512], BF16, OFF_XIN)
    QSl = sb("QSl", [128, KT, 512], BF16, OFF_XOUT)
    XIT = sb("XIT", [128, KT, S], BF16, OFF_BIG1)
    LTbh = sb("LTbh", [128, KT, 512], BF16, OFF_BIG1)
    LTbl = sb("LTbl", [128, KT, 512], BF16, OFF_BIG1 + 8 * 1024)
    QSbh = sb("QSbh", [128, KT, 512], BF16, OFF_BIG1 + 16 * 1024)
    QSbl = sb("QSbl", [128, KT, 512], BF16, OFF_BIG1 + 24 * 1024)
    Eall = sb("Eall", [128, 8, 132], F32, OFF_BIG1 + 32 * 1024)
    vprev = sb("vprev", [1, 4, D], F32, OFF_BIG1 + 37 * 1024)

    Kh = sb("Kh", [128, KT, S], BF16, OFF_BIG2)
    Kl = sb("Kl", [128, KT, S], BF16, OFF_BIG2 + 32 * 1024)
    CBh = sb("CBh", [128, KT, S], BF16, OFF_BIG2)
    CBl = sb("CBl", [128, KT, S], BF16, OFF_BIG2 + 32 * 1024)
    kbh = sb("kbh", [128, KT, 512], BF16, OFF_BIG2)
    kbl = sb("kbl", [128, KT, 512], BF16, OFF_BIG2 + 8 * 1024)
    cbh_b = sb("cbh_b", [128, KT, 512], BF16, OFF_BIG2 + 16 * 1024)
    cbl_b = sb("cbl_b", [128, KT, 512], BF16, OFF_BIG2 + 24 * 1024)
    wb1_h = sb("wb1_h", [128, KT, D], BF16, OFF_BIG1)
    wb1_l = sb("wb1_l", [128, KT, D], BF16, OFF_BIG1 + 16 * 1024)
    wb2_h = sb("wb2_h", [128, KT, D], BF16, OFF_BIG2 + 32 * 1024)
    wb2_l = sb("wb2_l", [128, KT, D], BF16, OFF_BIG2 + 48 * 1024)
    FSa = [sb(f"FSa{i}", [128, D], F32, OFF_BIG2 + i * 4 * 1024)
           for i in range(2)]
    LNah = sb("LNah", [128, 2, D], BF16, OFF_BIG2 + 8 * 1024)
    LNal = sb("LNal", [128, 2, D], BF16, OFF_BIG2 + 12 * 1024)
    FSq = [sb(f"FSq{i}", [128, D], F32, OFF_BIG1 + 16 * 1024 + i * 4 * 1024)
           for i in range(2)]
    LNqh = sb("LNqh", [128, 2, D], BF16, OFF_BIG1 + 24 * 1024)
    LNql = sb("LNql", [128, 2, D], BF16, OFF_BIG1 + 28 * 1024)
    xfq = sb("xfq", [128, 2, D], F32, OFF_BIG2)
    vb = [sb("vb0", [128, 2, D], F32, OFF_XIN),
          sb("vb1", [128, 2, D], F32, OFF_XOUT)]
    WS32 = sb("WS32", [128, KT, D], F32, OFF_BIG1)
    WSI = sb("WSI", [128, KT, D], I32, OFF_BIG1 + 32 * 1024)
    WSH = sb("WSH", [128, KT, D], BF16, OFF_BIG2)
    WSL = sb("WSL", [128, KT, D], BF16, OFF_BIG2 + 16 * 1024)

    PSA = nc.alloc_psum_tensor("PSA", [128, 2048], F32)
    PSB = nc.alloc_psum_tensor("PSB", [128, 512], F32)
    PSC = nc.alloc_psum_tensor("PSC", [128, 512], F32)
    PEA = nc.alloc_psum_tensor("PEA", [128, 1024], BF16)
    PEB = nc.alloc_psum_tensor("PEB", [128, 1024], BF16)

    def ld(out_ap, in_ap, wait=(), sem="dIn"):
        return P.op("sync", lambda E, o=out_ap, i=in_ap:
                    E.dma_start(out=o, in_=i), wait, inc=sem, amt=16)

    def st(out_ap, in_ap, wait=()):
        return P.op("gpsimd", lambda E, o=out_ap, i=in_ap:
                    E.dma_start(out=o, in_=i), wait, inc="dOut", amt=16)

    def mm(ps, lhsT, rhs, start, stop, wait=(), inc=None, tp=None, tr=False):
        def f(E, o=ps, l_=lhsT, r_=rhs, st_=start, sp=stop, tp_=tp, t_=tr):
            if t_:
                return E.matmul(out=o, lhsT=l_, rhs=r_, start=st_, stop=sp,
                                tile_position=tp_, is_transpose=True)
            return E.matmul(out=o, lhsT=l_, rhs=r_, start=st_, stop=sp,
                            tile_position=tp_)
        return P.op("tensor", f, wait, inc)

    def v_op(fn, wait=(), inc=None):
        return P.op("vector", fn, wait, inc)

    def s_op(fn, wait=(), inc=None):
        return P.op("scalar", fn, wait, inc)

    def vcopy(dst, src, wait=(), inc=None):
        return v_op(lambda E, d=dst, s_=src: E.tensor_copy(out=d, in_=s_),
                    wait, inc)

    def vts(dst, src, scalar, opc, wait=(), inc=None):
        return v_op(lambda E, d=dst, s_=src, sc=scalar, o=opc:
                    E.tensor_single_scalar(out=d, in_=s_, scalar=sc, op=o),
                    wait, inc)

    def vtt(dst, a, b, opc, wait=(), inc=None):
        return v_op(lambda E, d=dst, a_=a, b_=b, o=opc:
                    E.tensor_tensor(out=d, in0=a_, in1=b_, op=o), wait, inc)

    def vstt(dst, in0, scalar, in1, op0, op1, wait=(), inc=None):
        return v_op(lambda E, d=dst, i0=in0, sc=scalar, i1=in1, o0=op0, o1=op1:
                    E.scalar_tensor_tensor(out=d, in0=i0, scalar=sc, in1=i1,
                                           op0=o0, op1=o1), wait, inc)

    def scopy(dst, src, wait=(), inc=None):
        return s_op(lambda E, d=dst, s_=src: E.copy(out=d, in_=s_), wait, inc)

    def sact(dst, src, func, bias=0.0, scale=1.0, accum=None, wait=(), inc=None):
        return s_op(lambda E, d=dst, s_=src, f=func, b=bias, sc=scale, a=accum:
                    E.activation(out=d, in_=s_, func=f, bias=b, scale=sc,
                                 accum_out=a), wait, inc)

    def w_ap(idx, hilo):
        return scr_w[(2 * idx + hilo) * D:(2 * idx + hilo + 1) * D, :].rearrange(
            "(kt p) d -> p kt d", p=128)

    def load_weight(wname, bh=None, bl=None, wait=()):
        idx = W_IDX[wname]
        bh = wa_h if bh is None else bh
        bl = wa_l if bl is None else bl
        t1 = ld(bh[:], w_ap(idx, 0), wait=wait, sem="dW")
        t2 = ld(bl[:], w_ap(idx, 1), wait=wait, sem="dW")
        return (t1, t2)

    def emit_init():
        P.op("gpsimd", lambda E: E.iota(out=iocol[:], pattern=[[0, 1]], base=0,
                                        channel_multiplier=1,
                                        allow_small_or_imprecise_dtypes=True))
        tg = P.op("gpsimd", lambda E: E.iota(out=iof[:], pattern=[[1, 128]],
                                             base=0, channel_multiplier=0,
                                             allow_small_or_imprecise_dtypes=True),
                  inc="g")
        v_op(lambda E: E.memset(ones_row[:], 1.0))
        v_op(lambda E: E.memset(zcol[:], 0.0))
        tz = v_op(lambda E: E.memset(zcol_b[:], 0.0), inc="v")
        vts(qflag[:], zcol[:, 0:1], 1.0, ALU.add, wait=(tz,))
        vts(ln2c[:], zcol[:, 0:1], 0.6931471805599453, ALU.add)
        v_op(lambda E: E.tensor_scalar(out=qflag[:], in0=qflag[:], scalar1=M64C,
                                       scalar2=-M64C, op0=ALU.add, op1=ALU.add))
        vts(I_f[:], iof[:], iocol[:, 0:1], ALU.is_equal, wait=(tg,))
        vcopy(I_b[:], I_f[:], inc="v")
        v_op(lambda E: E.tensor_scalar(out=qm1[:], in0=qflag[:], scalar1=-1.0,
                                       scalar2=1.0, op0=ALU.mult, op1=ALU.add))

        tb = []
        for b in ("bv_a", "bo_a", "bv_b", "bo_b", "fb"):
            tb.append(ld(brows[b][:], bin_[b][:]))
        prev_ev = None
        for b in bcols:
            trow = ld(vbound[0:1, :], bin_[b][:],
                      wait=(prev_ev,) if prev_ev else ())
            tg2 = None
            for mt in range(KT):
                tg2 = mm(PSC[:, mt:mt + 1], vbound[0:1, mt * 128:(mt + 1) * 128],
                         ones_row[0:1, 0:1], mt == 0, mt == KT - 1,
                         wait=(trow,) if mt == 0 else (),
                         inc="t" if mt == KT - 1 else None)
            prev_ev = scopy(bcols[b][:], PSC[:, 0:KT], wait=(tg2,), inc="s")
        tb.append(prev_ev)
        t_sc = ld(cur["m"][0:1, 0:1], scal_d[:])
        t_dec = ld(lnd_bc[0:1, 0:16], decay_d[:])

        tv = vts(cur["m"][0:1, 0:1], cur["m"][0:1, 0:1], 1e-5, ALU.max,
                 wait=(t_sc,), inc="v")
        tmm = mm(PSB[:, 0:1], ones_row[:, 0:128], cur["m"][0:1, 0:1], True, True,
                 wait=(tv,), inc="t")
        ts1 = scopy(s_bc[:], PSB[:, 0:1], wait=(tmm,), inc="s")
        ts1b = sact(srec_bc[:], s_bc[:], AF.Ln, wait=(ts1,), inc="s")
        ts2 = sact(srec_bc[:], srec_bc[:], AF.Exp, scale=-1.0, wait=(ts1b,),
                   inc="s")
        vts(brows["fb"][:], brows["fb"][:], srec_bc[0:1, 0:1], ALU.mult,
            wait=tuple(tb) + (ts2,), inc="v")

        tl = sact(lnd_bc[0:1, 0:16], lnd_bc[0:1, 0:16], AF.Ln, wait=(t_dec,),
                  inc="s")
        tmm2 = mm(PSB[:, 8:24], ones_row[:, 0:128], lnd_bc[0:1, 0:16], True, True,
                  wait=(tl, ts1), inc="t")
        tl2 = scopy(lnd_bc[:], PSB[:, 8:24], wait=(tmm2,), inc="s")
        tl3 = sact(sdf[:], lnd_bc[:], AF.Exp, scale=iocol[:, 0:1], wait=(tl2,),
                   inc="s")
        vts(sdf[:], sdf[:], SCALE_B, ALU.mult, wait=(tl3,), inc="v")
        P.barrier()

        tld = ld(WSI[:], tw_d.rearrange("(kt p) d -> p kt d", p=128))
        tv1 = vcopy(WS32[:], WSI[:], wait=(tld,), inc="v")
        tv2 = vts(WSH[:], WS32[:], -1.0, ALU.add, wait=(tv1,), inc="v")
        ev_toks = []
        pe_ev = {}
        for okt in range(KT):
            for g in range(2):
                pe = PEA if g == 0 else PEB
                key = (okt * 2 + g) % 2
                tts = None
                for j in range(4):
                    ikt = g * 4 + j
                    w = (tv2,)
                    if okt * 2 + g >= 2:
                        w = (tv2, pe_ev.get(okt * 2 + g - 2))
                    tts = mm(pe[:, j * 128:(j + 1) * 128],
                             WSH[:, okt, ikt * 128:(ikt + 1) * 128], I_b[:],
                             j == 0, j == 3, wait=w,
                             inc="t" if j == 3 else None, tr=True)
                dst = WSL[:, g * 4:(g + 1) * 4, okt * 128:okt * 128 + 128]
                src = pe[:, 0:512].rearrange("p (a b) -> p a b", a=4)
                ev = v_op(lambda E, d=dst, s_=src:
                          E.tensor_copy(out=d, in_=s_), wait=(tts,), inc="v")
                pe_ev[okt * 2 + g] = ev
                ev_toks.append(ev)
        st(scr_wt.rearrange("(kt p) d -> p kt d", p=128), WSL[:],
           wait=tuple(ev_toks))
        P.barrier()

        for w, idx in W_IDX.items():
            tldw = ld(WS32[:], win[w].rearrange("(kt p) d -> p kt d", p=128))
            th = vcopy(WSH[:], WS32[:], wait=(tldw,), inc="v")
            tl_ = v_op(lambda E: E.tensor_sub(out=WSL[:], in0=WS32[:],
                                              in1=WSH[:]), wait=(th,), inc="v")
            sh = st(w_ap(idx, 0), WSH[:], wait=(th,))
            sl = st(w_ap(idx, 1), WSL[:], wait=(tl_,))
            P.op("sync", lambda E: E.nop(), wait=(sh, sl))
        P.barrier()

    def emit_stats(src_ap, row0=0, sts=None):
        sts0, sts1 = (STS0, STS1) if sts is None else sts
        vt = [None] * NT
        stk = [None] * NT
        for t in range(NT):
            slot = t % 2
            w = []
            if t >= 2:
                w = [vt[t - 2], stk[t - 2]]
            l_ = ld(xin[:, slot, :],
                    src_ap[row0 + t * 128:row0 + (t + 1) * 128, :], wait=tuple(w))
            vt[t] = v_op(lambda E, sl=slot, tt=t, o=sts0: E.tensor_scalar(
                out=o[:], in0=xin[:, sl, :], scalar1=1.0, scalar2=0.0,
                op0=ALU.mult, op1=ALU.add,
                accum_out=sums_c[:, tt:tt + 1]), wait=(l_,), inc="v")
            stk[t] = s_op(lambda E, sl=slot, tt=t, o=sts1: E.activation(
                out=o[:], in_=xin[:, sl, :], func=AF.Square,
                accum_out=sumsq_c[:, tt:tt + 1]), wait=(l_,), inc="s")
        tv1 = vts(m_c[:], sums_c[:], 1.0 / D, ALU.mult,
                  wait=(vt[NT - 1], stk[NT - 1]), inc="v")
        tv2 = vtt(var_c[:], m_c[:], m_c[:], ALU.mult, wait=(tv1,), inc="v")
        tv3 = vstt(var_c[:], sumsq_c[:], 1.0 / D, var_c[:], ALU.mult,
                   ALU.subtract, wait=(tv2,), inc="v")
        tv4 = vts(var_c[:], var_c[:], EPS, ALU.add, wait=(tv3,), inc="v")
        s1 = sact(r_c[:], var_c[:], AF.Ln, wait=(tv4,), inc="s")
        s2 = sact(r_c[:], r_c[:], AF.Exp, scale=-0.5, wait=(s1,), inc="s")
        return s2

    def emit_ln_fm(src_ap, row0, ntiles, tile0, dst_h, dst_l, col0, wtok,
                   scratch=None):
        fs0, fs1, lnh, lnl = ((FS[0], FS[1], LNh, LNl) if scratch is None
                              else scratch)
        fsb = (fs0, fs1)
        ev_h = {}
        ev_l = {}
        vtok = {}
        for t in range(ntiles):
            slot = t % 2
            buf = t % 2
            gt = tile0 + t
            w = [wtok]
            if t >= 2:
                w.append(vtok[t - 2])
            l_ = ld(xin[:, slot, :],
                    src_ap[row0 + t * 128:row0 + (t + 1) * 128, :], wait=tuple(w))
            wv = [l_]
            if t >= 2:
                wv += [ev_h[t - 2], ev_l[t - 2]]
            vts(fsb[buf][:], xin[:, slot, :], m_c[:, gt:gt + 1], ALU.subtract,
                wait=tuple(wv))
            th = vts(lnh[:, buf, :], fsb[buf][:], r_c[:, gt:gt + 1], ALU.mult,
                     inc="v")
            tl_ = vstt(lnl[:, buf, :], fsb[buf][:], r_c[:, gt:gt + 1],
                       lnh[:, buf, :], ALU.mult, ALU.subtract, inc="v")
            vtok[t] = tl_
            col = col0 + t * 128
            for (pe, lnbuf, wt, evd, dst) in ((PEA, lnh, th, ev_h, dst_h),
                                              (PEB, lnl, tl_, ev_l, dst_l)):
                tt = None
                for kt in range(KT):
                    w2 = [wt]
                    if t >= 1:
                        w2.append(evd.get(t - 1))
                    tt = mm(pe[:, kt * 128:(kt + 1) * 128],
                            lnbuf[:, buf, kt * 128:(kt + 1) * 128], I_b[:],
                            kt == 0, kt == KT - 1, wait=tuple(x for x in w2 if x),
                            inc="t" if kt == KT - 1 else None, tr=True)
                dstap = dst[:, :, col:col + 128]
                srcap = pe[:, 0:1024].rearrange("p (a b) -> p a b", a=KT)
                if pe is PEA:
                    evd[t] = scopy(dstap, srcap, wait=(tt,), inc="s")
                else:
                    evd[t] = v_op(lambda E, d=dstap, s_=srcap:
                                  E.tensor_copy(out=d, in_=s_), wait=(tt,),
                                  inc="v")
        return (ev_h[ntiles - 1], ev_l[ntiles - 1])

    def proj_fm(wtoks, mov_h, mov_l, ncols, dst_fn, bname, col0=0, wbufs=None):
        wbh, wbl = (wa_h, wa_l) if wbufs is None else wbufs
        evt = {}
        nunit = 0
        for ntc in range(ncols // 512):
            for mt in range(KT):
                bank = (PSB, PSC)[nunit % 2]
                w0 = list(wtoks)
                if nunit >= 2:
                    w0 += list(evt[nunit - 2])
                tokmm = None
                for p, (wsel, msel) in enumerate(((wbh, mov_h), (wbl, mov_h),
                                                  (wbh, mov_l))):
                    for kt in range(KT):
                        last = p == 2 and kt == KT - 1
                        tokmm = mm(bank[:, 0:512],
                                   wsel[:, kt, mt * 128:(mt + 1) * 128],
                                   msel[:, kt, ntc * 512:(ntc + 1) * 512],
                                   p == 0 and kt == 0, last,
                                   wait=tuple(w0) if (p == 0 and kt == 0) else (),
                                   inc="t" if last else None)
                hi_ap, lo_ap = dst_fn(mt, col0 + ntc * 512, 512)
                te1 = sact(hi_ap, bank[:, 0:512], AF.Identity,
                           bias=bcols[bname][:, mt:mt + 1], wait=(tokmm,),
                           inc="s")
                te2 = vstt(lo_ap, bank[:, 0:512], bcols[bname][:, mt:mt + 1],
                           hi_ap, ALU.add, ALU.subtract, wait=(tokmm, te1),
                           inc="v")
                evt[nunit] = (te1, te2)
                nunit += 1
        return evt[nunit - 1]

    def proj_fm_pairs(wtoks, mov_h, mov_l, ncols, dst_fn, bname):
        evt = {}
        nunit = 0
        psel = ((wa_h, mov_h), (wa_l, mov_h), (wa_h, mov_l))
        for pair in range(ncols // 1024):
            for mt in range(KT):
                g = nunit % 2
                b0 = PSA[:, g * 1024:g * 1024 + 512]
                b1 = PSA[:, g * 1024 + 512:g * 1024 + 1024]
                w0 = list(wtoks)
                if nunit >= 2:
                    w0 += list(evt[nunit - 2])
                tokmm = None
                for p, (wsel, msel) in enumerate(psel):
                    for kt in range(KT):
                        lhs = wsel[:, kt, mt * 128:(mt + 1) * 128]
                        first = p == 0 and kt == 0
                        last = p == 2 and kt == KT - 1
                        mm(b0, lhs,
                           msel[:, kt, pair * 1024:pair * 1024 + 512],
                           first, last, wait=tuple(w0) if first else ())
                        tokmm = mm(b1, lhs,
                                   msel[:, kt, pair * 1024 + 512:
                                        pair * 1024 + 1024],
                                   first, last,
                                   inc="t" if last else None)
                toks = []
                for j, bk in enumerate((b0, b1)):
                    c0 = pair * 1024 + j * 512
                    hi_ap, lo_ap = dst_fn(mt, c0, 512)
                    te1 = sact(hi_ap, bk, AF.Identity,
                               bias=bcols[bname][:, mt:mt + 1], wait=(tokmm,),
                               inc="s")
                    te2 = vstt(lo_ap, bk, bcols[bname][:, mt:mt + 1],
                               hi_ap, ALU.add, ALU.subtract,
                               wait=(tokmm, te1), inc="v")
                    toks += [te1, te2]
                evt[nunit] = tuple(toks)
                nunit += 1
        return evt

    def proj_tm(first_w, act_fn, wmov, ntiles, brow, evac, passes=3,
                pre_cb=None, tile_w=None):
        evt = {}
        for tt in range(ntiles):
            if pre_cb is not None:
                pre_cb(tt)
            g = tt % 2
            b0 = PSA[:, g * 1024:g * 1024 + 512]
            b1 = PSA[:, g * 1024 + 512:g * 1024 + 1024]
            w0 = list(first_w)
            if tile_w is not None:
                w0 += list(tile_w(tt))
            if tt >= 2:
                w0 += list(evt[tt - 2])
            psel = (((0, 0), (1, 0), (0, 1)) if passes == 3 else ((0, 0),))
            first = True
            for (ah, wl_) in psel:
                for kt in range(KT):
                    lhs = act_fn(ah, kt, tt)
                    mm(b0, lhs, wmov[wl_][:, kt, 0:512], first, False,
                       wait=tuple(w0) if first else ())
                    mm(b1, lhs, wmov[wl_][:, kt, 512:1024], first, False)
                    first = False
            mm(b0, ones_row[:, 0:128], brow[0:1, 0:512], False, True)
            tokmm = mm(b1, ones_row[:, 0:128], brow[0:1, 512:1024], False,
                       True, inc="t")
            evt[tt] = tuple(evac(tt, 0, b0, tokmm)) +                 tuple(evac(tt, 1, b1, tokmm))
        return evt

    def emit_attn_a(src_ap, dst_ap):
        stok = emit_stats(src_ap)
        P.barrier()
        emit_ln_fm(src_ap, 0, NT, 0, LTh, LTl, 0, stok)
        P.barrier()
        wv = load_weight("wv_a")
        if debug_a:
            st(dbg["a_lth"].rearrange("(kt p) t -> p kt t", p=128), LTh[:])
            st(dbg["a_ltl"].rearrange("(kt p) t -> p kt t", p=128), LTl[:])
            P.barrier()

        spill_toks = {}

        def evac_v(tt, ntc, bank, tokmm):
            wv2 = []
            if tt >= 2:
                wv2 = [spill_toks[tt - 2]]
            te = scopy(xout[:, tt % 2, ntc * 512:(ntc + 1) * 512], bank,
                       wait=(tokmm,) + tuple(wv2), inc="s")
            if ntc == 1:
                spill_toks[tt] = st(scr_v[tt * 128:(tt + 1) * 128, :],
                                    xout[:, tt % 2, :], wait=(te,))
            return (te,)

        ev_v = proj_tm(wv, lambda h, kt, tt: (LTh if h == 0 else LTl)[
            :, kt, tt * 128:(tt + 1) * 128],
            (wa_h, wa_l), NT, brows["bv_a"], evac_v, passes=3)
        P.barrier()

        wk = load_weight("wk_a")
        ev_k = proj_fm_pairs(wk, LTh, LTl, S,
                             lambda mt, c0, w: (Kh[:, mt, c0:c0 + w],
                                                Kl[:, mt, c0:c0 + w]), "bk_a")
        P.barrier()
        if debug_a:
            st(dbg["a_kh"].rearrange("(kt p) t -> p kt t", p=128), Kh[:])
            st(dbg["a_kl"].rearrange("(kt p) t -> p kt t", p=128), Kl[:])
            P.barrier()

        wq = load_weight("wq_a")

        def qdst(mt, c0, w):
            if c0 < 512:
                return (QSh[:, mt, c0:c0 + w], QSl[:, mt, c0:c0 + w])
            return (LTh[:, mt, c0 - 512:c0 - 512 + w],
                    LTl[:, mt, c0 - 512:c0 - 512 + w])

        proj_fm(wq, LTh, LTl, S, qdst, "bq_a")
        P.barrier()
        if debug_a:
            st(dbg["a_qh"][:, 0:512].rearrange("(kt p) t -> p kt t", p=128),
               QSh[:])
            st(dbg["a_ql"][:, 0:512].rearrange("(kt p) t -> p kt t", p=128),
               QSl[:])
            st(dbg["a_qh"][:, 512:S].rearrange("(kt p) t -> p kt t", p=128),
               LTh[:, :, 0:S - 512])
            st(dbg["a_ql"][:, 512:S].rearrange("(kt p) t -> p kt t", p=128),
               LTl[:, :, 0:S - 512])
            P.barrier()

        def q_ap(hilo, kt, c0):
            if c0 < 512:
                t_ = (QSh, QSl)[hilo]
                return t_[:, kt, c0:c0 + 128]
            t_ = (LTh, LTl)[hilo]
            return t_[:, kt, c0 - 512:c0 - 512 + 128]

        at_ev = {}
        dg_tok = {}
        exp_tok = {}
        sm_done = {}
        sc_last = {}

        def p2_scores(idx):
            n, qt = idx // 2, idx % 2
            E = Ebuf[idx % 2]
            par = idx % 2
            c0 = n * 256 + qt * 128
            tsum = None
            for mp in range(4):
                tokmm = None
                for p, hl in enumerate(((0, 0), (1, 0), (0, 1))):
                    qh, kh = hl
                    for kt in range(KT):
                        lhs = q_ap(qh, kt, c0)
                        for m in (2 * mp, 2 * mp + 1):
                            start = p == 0 and kt == 0 and m == 2 * mp
                            stop = p == 2 and kt == KT - 1 and m == 2 * mp + 1
                            w0 = ()
                            if start and idx > 0:
                                w0 = tuple(exp_tok[(idx - 1, mp)])
                            tokmm = mm(
                                PSA[:, m * 256:(m + 1) * 256], lhs,
                                (Kh, Kl)[kh][:, kt, m * 256:(m + 1) * 256],
                                start, stop, wait=w0,
                                inc="t" if stop else None)
                tmx = v_op(lambda E_, mp_=mp: E_.reduce_max(
                    out=negM8[:, mp_:mp_ + 1],
                    in_=PSA[:, mp_ * 512:(mp_ + 1) * 512], axis=AX.X,
                    negate=True), wait=(tokmm,))
                tb_ = v_op(lambda E_, mp_=mp: E_.tensor_single_scalar(
                    out=bias8[:, mp_:mp_ + 1], in_=negM8[:, mp_:mp_ + 1],
                    scalar=SCALE_A, op=ALU.mult), inc="v")
                wexp = [tb_]
                if idx >= 2 and (idx - 2, 7) in dg_tok:
                    wexp.append(dg_tok[(idx - 2, 7)])
                texp = sact(E[:, mp * 512:(mp + 1) * 512],
                            PSA[:, mp * 512:(mp + 1) * 512], AF.Exp,
                            bias=bias8[:, mp:mp + 1], scale=SCALE_A,
                            wait=tuple(wexp), inc="s")
                exp_tok[(idx, mp)] = (tb_, texp)
                tsum = v_op(lambda E_, Ex=E, mp_=mp, pr=par: E_.reduce_sum(
                    out=sums16[:, pr * 8 + 2 * mp_:pr * 8 + 2 * mp_ + 2],
                    in_=Ex[:, mp_ * 512:(mp_ + 1) * 512].rearrange(
                        "p (a b) -> p a b", a=2),
                    axis=AX.X), wait=(texp,), inc="v")
            sc_last[idx] = tokmm
            tln = sact(tmp16[:, par * 8:par * 8 + 8],
                       sums16[:, par * 8:par * 8 + 8], AF.Ln, wait=(tsum,),
                       inc="s")
            trv = sact(rinv16[:, par * 8:par * 8 + 8],
                       tmp16[:, par * 8:par * 8 + 8], AF.Exp, scale=-1.0,
                       wait=(tln,), inc="s")
            trv2 = sact(rinv16x2[:, par * 8:par * 8 + 8],
                        tmp16[:, par * 8:par * 8 + 8], AF.Exp, scale=-1.0,
                        bias=ln2c[:, 0:1], wait=(tln,), inc="s")
            sm_done[idx] = (trv, trv2)

        def p2_diag(idx):
            n, qt = idx // 2, idx % 2
            E = Ebuf[idx % 2]
            par = idx % 2
            for m in range(8):
                wv2 = list(sm_done[idx])
                if m >= 2:
                    wv2.append(dg_tok[(idx, m - 2)])
                elif idx > 0:
                    wv2.append(dg_tok[(idx - 1, m + 6)])
                rsrc = rinv16x2 if m == n else rinv16
                tD = vts(Dbuf[m % 2][:], I_f[:],
                         rsrc[:, par * 8 + m:par * 8 + m + 1], ALU.mult,
                         wait=tuple(wv2), inc="v")
                wmm = [tD]
                if m == 0 and idx > 0:
                    wmm += list(at_ev[idx - 1])
                mm(PSB[:, 0:128], E[:, m * 256:m * 256 + 128], Dbuf[m % 2][:],
                   m == 0, m == 7, wait=tuple(wmm))
                dg_tok[(idx, m)] = mm(PSC[:, 0:128],
                                      E[:, m * 256 + 128:m * 256 + 256],
                                      Dbuf[m % 2][:], m == 0, m == 7, inc="t")
            ta1 = scopy(ATa[:, n * 2, qt * 128:qt * 128 + 128], PSB[:, 0:128],
                        wait=(dg_tok[(idx, 7)],), inc="s")
            ta2 = scopy(ATa[:, n * 2 + 1, qt * 128:qt * 128 + 128],
                        PSC[:, 0:128], wait=(dg_tok[(idx, 7)],), inc="s")
            at_ev[idx] = (ta1, ta2)

        prev_idx = None
        pre_vload = {}
        for idx in range(16):
            p2_scores(idx)
            if idx == 3:
                pre_vload[0] = (
                    ld(xin[:, 0, :], scr_v[0:128, :], wait=(sc_last[3],)),
                    ld(xin[:, 1, :], scr_v[128:256, :], wait=(sc_last[3],)))
                wsp = (sc_last[3], spill_toks[NT - 2], spill_toks[NT - 1])
                pre_vload[1] = (
                    ld(xout[:, 0, :], scr_v[256:384, :], wait=wsp),
                    ld(xout[:, 1, :], scr_v[384:512, :], wait=wsp))
            if prev_idx is not None:
                p2_diag(prev_idx)
            prev_idx = idx
        p2_diag(prev_idx)
        P.barrier()
        if debug_a:
            st(dbg["a_at"].rearrange("(a p) q -> p a q", p=128), ATa[:])
            st(dbg["a_e0"][:], Ebuf[0][:])
            st(dbg["a_e1"][:], Ebuf[1][:])
            st(dbg["a_r8"][0:1, :].rearrange("one (p w) -> (one p) w", w=8),
               rinv8[:])
            st(dbg["a_r8"][1:2, :].rearrange("one (p w) -> (one p) w", w=8),
               sums8[:])
            P.barrier()

        vload = {}
        avtok = {}
        evtok = {}
        nunit = 0
        for n in range(8):
            vrb = (xin, xout)[n % 2]
            if n in pre_vload:
                vload[n] = pre_vload[n]
            else:
                w0 = [avtok[n - 2]] if n >= 2 else []
                vload[n] = (
                    ld(vrb[:, 0, :], scr_v[n * 256:n * 256 + 128, :],
                       wait=tuple(w0)),
                    ld(vrb[:, 1, :], scr_v[n * 256 + 128:n * 256 + 256, :],
                       wait=tuple(w0)))
            for et in range(KT):
                bank = (PSB, PSC)[nunit % 2]
                w1 = list(vload[n])
                if nunit >= 2:
                    w1 += list(evtok[nunit - 2])
                mm(bank[:, 0:256], vrb[:, 0, et * 128:(et + 1) * 128],
                   ATa[:, n * 2, 0:256], True, False, wait=tuple(w1))
                tokav = mm(bank[:, 0:256], vrb[:, 1, et * 128:(et + 1) * 128],
                           ATa[:, n * 2 + 1, 0:256], False, True, inc="t")
                if et == KT - 1:
                    avtok[n] = tokav
                th = sact(CBh[:, et, n * 256:(n + 1) * 256], bank[:, 0:256],
                          AF.Identity, wait=(tokav,), inc="s")
                tl_ = vtt(CBl[:, et, n * 256:(n + 1) * 256], bank[:, 0:256],
                          CBh[:, et, n * 256:(n + 1) * 256], ALU.subtract,
                          wait=(tokav, th), inc="v")
                evtok[nunit] = (th, tl_)
                nunit += 1
        P.barrier()
        if debug_a:
            st(dbg["a_cbh"].rearrange("(kt p) t -> p kt t", p=128), CBh[:])
            st(dbg["a_cbl"].rearrange("(kt p) t -> p kt t", p=128), CBl[:])
            P.barrier()

        wo = load_weight("wo_a")
        emit_out_proj(wo, lambda h, kt, tt: (CBh if h == 0 else CBl)[
            :, kt, tt * 128:(tt + 1) * 128], (wa_h, wa_l), NT, 0,
            brows["bo_a"], src_ap, 0, dst_ap, 0, passes=3, ffn_scale=False)
        P.barrier()

    def emit_out_proj(first_w, act_fn, wmov, ntiles, tile0, brow,
                      res_ap, res_row0, dst_ap, dst_row0, passes, ffn_scale,
                      pre_cb=None, tile_w=None):
        res_ld = {}
        st_tok = {}

        def evac(tt, ntc, bank, tokmm):
            if ntc == 0 and tt not in res_ld:
                w = []
                if tt >= 2:
                    w = [st_tok[tt - 2]]
                res_ld[tt] = ld(
                    xin[:, tt % 2, :],
                    res_ap[res_row0 + (tile0 + tt) * 128:
                           res_row0 + (tile0 + tt + 1) * 128, :], wait=tuple(w))
            sl = tt % 2
            wst = (st_tok.get(tt - 2),) if tt >= 2 else ()
            if ffn_scale:
                te = vstt(xout[:, sl, ntc * 512:(ntc + 1) * 512], bank,
                          s_bc[:, 0:1], xin[:, sl, ntc * 512:(ntc + 1) * 512],
                          ALU.mult, ALU.add,
                          wait=(tokmm, res_ld[tt]) + wst, inc="v")
            else:
                te = vtt(xout[:, sl, ntc * 512:(ntc + 1) * 512], bank,
                         xin[:, sl, ntc * 512:(ntc + 1) * 512], ALU.add,
                         wait=(tokmm, res_ld[tt]) + wst, inc="v")
            if ntc == 1:
                st_tok[tt] = st(
                    dst_ap[dst_row0 + (tile0 + tt) * 128:
                           dst_row0 + (tile0 + tt + 1) * 128, :],
                    xout[:, sl, :], wait=(te,))
            return (te,)

        return proj_tm(first_w, act_fn, wmov, ntiles, brow, evac, passes=passes,
                       pre_cb=pre_cb, tile_w=tile_w)

    def emit_ffn(src_ap, dst_ap):
        stok = emit_stats(src_ap)
        twt = ld(wt_f[:], scr_wt.rearrange("(kt p) d -> p kt d", p=128))
        stok = vts(rs_c[:], r_c[:], srec_bc[:, 0:1], ALU.mult, wait=(stok,),
                   inc="v")
        P.barrier()
        pe_tok = {}
        ev_tok = {}

        def emit_quant_tile(t):
            slot = t % 2
            w = [stok]
            if t >= 2:
                w.append(pe_tok[t - 2])
            l_ = ld(xfq[:, slot, :], src_ap[t * 128:(t + 1) * 128, :],
                    wait=tuple(w))
            wv = [l_]
            if t >= 1:
                wv.append(pe_tok[t - 1])
            vts(FS[0][:], xfq[:, slot, :], m_c[:, t:t + 1], ALU.subtract,
                wait=tuple(wv))
            vts(FS[1][:], FS[0][:], rs_c[:, t:t + 1], ALU.mult)
            v_op(lambda E: E.tensor_scalar(out=FS[2][:], in0=FS[1][:],
                                           scalar1=M32C, scalar2=-M32C,
                                           op0=ALU.add, op1=ALU.add))
            v_op(lambda E: E.tensor_scalar(out=FS[3][:], in0=FS[1][:],
                                           scalar1=M64C, scalar2=-M64C,
                                           op0=ALU.add, op1=ALU.add))
            vts(FS[0][:], FS[2][:], qm1[:, 0:1], ALU.mult)
            vtt(FS[1][:], FS[0][:], FS[3][:], ALU.add)
            tq = v_op(lambda E: E.tensor_scalar(out=xqb[:],
                                                in0=FS[1][:], scalar1=127.0,
                                                scalar2=-127.0, op0=ALU.min,
                                                op1=ALU.max), inc="v")
            pe = (PEA, PEB)[t % 2]
            tt_ = None
            for kt in range(KT):
                w2 = [tq]
                if t >= 2:
                    w2.append(ev_tok[t - 2])
                tt_ = mm(pe[:, kt * 128:(kt + 1) * 128],
                         xqb[:, kt * 128:(kt + 1) * 128], I_b[:],
                         kt == 0, kt == KT - 1, wait=tuple(w2),
                         inc="t" if kt == KT - 1 else None, tr=True)
            pe_tok[t] = tt_
            ev_tok[t] = scopy(XIT[:, :, t * 128:(t + 1) * 128],
                              pe[:, 0:1024].rearrange("p (a b) -> p a b", a=KT),
                              wait=(tt_,), inc="s")

        emit_quant_tile(0)
        emit_quant_tile(1)
        if debug_ffn:
            st(dbg["xit"].rearrange("(kt p) t -> p kt t", p=128), XIT[:])
            st(dbg["mc"][:], m_c[:])
            st(dbg["rc"][:], r_c[:])
            st(dbg["sc"][0:1, :].rearrange("one (p w) -> (one p) w", w=1),
               s_bc[:])
            st(dbg["sc"][1:2, :].rearrange("one (p w) -> (one p) w", w=1),
               srec_bc[:])
            st(dbg["wt"].rearrange("(kt p) t -> p kt t", p=128), wt_f[:])
            st(dbg["fbs"][:], brows["fbs"][:])
            for i_ in range(4):
                st(dbg["fs"][i_ * 128:(i_ + 1) * 128, :], FS[i_][:])
            st(dbg["xqb"][:], xqb[:])
            st(dbg["qf"][:], qflag[:])
            st(dbg["qm1"][:], qm1[:])
            st(dbg["cur"][0:1, :].rearrange("one (p w) -> (one p) w", w=1),
               cur["m"][:])
            st(dbg["cur"][1:2, :].rearrange("one (p w) -> (one p) w", w=1),
               cur["r"][:])
            st(dbg["cur"][2:3, :].rearrange("one (p w) -> (one p) w", w=1),
               cur["rs"][:])
            st(dbg["xin1"][:], xin[:, 1, :])
            P.barrier()
        emit_out_proj((twt,),
                      lambda h, kt, tt: XIT[:, kt, tt * 128:(tt + 1) * 128],
                      (wt_f, wt_f), NT, 0, brows["fbs"], src_ap, 0, dst_ap, 0,
                      passes=1, ffn_scale=True,
                      pre_cb=lambda tt: (emit_quant_tile(tt + 2)
                                         if tt + 2 < NT else None),
                      tile_w=lambda tt: (ev_tok[tt],))
        P.barrier()

    def emit_attn_b(src_ap, dst_ap):
        stok = emit_stats(src_ap)
        P.barrier()
        for quarter in range(4):
            row0 = quarter * 512
            emit_ln_fm(src_ap, row0, 4, quarter * 4, LTbh, LTbl, 0, stok)
            P.barrier()

            wv = load_weight("wv_b")
            vp_tok = {}

            def evac_vb(tt, ntc, bank, tokmm):
                dst = vb[0][:, tt, :] if tt < 2 else vb[1][:, tt - 2, :]
                te = scopy(dst[:, ntc * 512:(ntc + 1) * 512], bank,
                           wait=(tokmm,), inc="s")
                if ntc == 1 and tt < 3:
                    src_t = vb[0] if tt < 2 else vb[1]
                    si = tt if tt < 2 else tt - 2
                    vp_tok[tt + 1] = st(vprev[0:1, tt + 1, :],
                                        src_t[127:128, si, :], wait=(te,))
                return (te,)

            proj_tm(wv, lambda h, kt, tt: (LTbh if h == 0 else LTbl)[
                :, kt, tt * 128:(tt + 1) * 128], (wa_h, wa_l), 4,
                brows["bv_b"], evac_vb, passes=3)
            if quarter == 0:
                vp_tok[0] = v_op(lambda E: E.memset(vprev[0:1, 0, :], 0.0),
                                 inc="v")
            else:
                vp_tok[0] = v_op(lambda E: E.tensor_copy(out=vprev[0:1, 0, :],
                                                         in_=vbound[0:1, :]),
                                 inc="v")
            P.barrier()

            wk = load_weight("wk_b")
            proj_fm(wk, LTbh, LTbl, 512,
                    lambda mt, c0, w: (kbh[:, mt, c0:c0 + w],
                                       kbl[:, mt, c0:c0 + w]), "bk_b")
            P.barrier()

            wq = load_weight("wq_b")
            proj_fm(wq, LTbh, LTbl, 512,
                    lambda mt, c0, w: (QSbh[:, mt, c0:c0 + w],
                                       QSbl[:, mt, c0:c0 + w]), "bq_b")
            P.barrier()

            def qb_ap(hilo, h, c):
                t_ = (QSbh, QSbl)[hilo]
                p0 = (h % 2) * 64
                return t_[p0:p0 + 64, h // 2, c * 128:(c + 1) * 128]

            def kb_ap(hilo, h, c0, c1):
                t_ = (kbh, kbl)[hilo]
                p0 = (h % 2) * 64
                return t_[p0:p0 + 64, h // 2, c0:c1]

            exp_tok = {}
            av_ev = {}
            tr_ev_all = {}
            a_tok = {}
            sd_tok = {}

            def rb_pass1(h):
                tp = (64, 0) if h % 2 == 1 else None
                par = h % 2
                wsd = [exp_tok.get((h - 1, 3))] if h > 0 else []
                tsd = vcopy(cur["sd"][:], sdf[:, h:h + 1], wait=tuple(wsd),
                            inc="v")
                sd_tok[h] = tsd
                tsum = None
                for c in range(4):
                    bank = (PSB, PSC)[c % 2]
                    w0 = [tsd]
                    if (h, c - 2) in exp_tok:
                        w0.append(exp_tok[(h, c - 2)])
                    elif h > 0 and c < 2:
                        w0.append(exp_tok[(h - 1, c + 2)])
                    first = True
                    for p, hl in enumerate(((0, 0), (1, 0), (0, 1))):
                        qh, kh = hl
                        if c == 0 and quarter == 0:
                            kprev = zcol_b[(h % 2) * 64:(h % 2) * 64 + 64, 0:1]
                        elif c == 0:
                            kprev = (kbound_h, kbound_l)[kh][
                                (h % 2) * 64:(h % 2) * 64 + 64,
                                h // 2:h // 2 + 1]
                        else:
                            kprev = kb_ap(kh, h, c * 128 - 1, c * 128)
                        mm(bank[:, 0:1], qb_ap(qh, h, c), kprev,
                           first, False, wait=tuple(w0) if first else (),
                           tp=tp)
                        first = False
                    tokmm = None
                    for p, hl in enumerate(((0, 0), (1, 0), (0, 1))):
                        qh, kh = hl
                        tokmm = mm(bank[:, 1:129], qb_ap(qh, h, c),
                                   kb_ap(kh, h, c * 128, (c + 1) * 128),
                                   False, p == 2, tp=tp,
                                   inc="t" if p == 2 else None)
                    sl8 = par * 4 + c
                    v_op(lambda E, b=bank, s_=sl8: E.reduce_max(
                        out=negM8[:, s_:s_ + 1], in_=b[:, 0:129], axis=AX.X,
                        negate=True), wait=(tokmm,))
                    tb_ = v_op(lambda E, s_=sl8: E.tensor_single_scalar(
                        out=bias8[:, s_:s_ + 1], in_=negM8[:, s_:s_ + 1],
                        scalar=cur["sd"][:, 0:1], op=ALU.mult), inc="v")
                    wexp = [tb_]
                    if h >= 2 and (h - 2, 3) in a_tok:
                        wexp.append(a_tok[(h - 2, 3)])
                    texp = sact(Eall[:, par * 4 + c, 0:129], bank[:, 0:129],
                                AF.Exp, bias=bias8[:, sl8:sl8 + 1],
                                scale=cur["sd"][:, 0:1], wait=tuple(wexp),
                                inc="s")
                    exp_tok[(h, c)] = texp
                    tsum = v_op(lambda E, c_=c, pr=par: E.reduce_sum(
                        out=sums16[:, pr * 8 + c_:pr * 8 + c_ + 1],
                        in_=Eall[:, pr * 4 + c_, 0:129],
                        axis=AX.X), wait=(texp,), inc="v")
                tln = sact(tmp16[:, par * 8:par * 8 + 4],
                           sums16[:, par * 8:par * 8 + 4], AF.Ln,
                           wait=(tsum,), inc="s")
                sact(rinv16[:, par * 8:par * 8 + 4],
                     tmp16[:, par * 8:par * 8 + 4], AF.Exp, scale=-1.0,
                     wait=(tln,), inc="s")
                exp_tok[(h, "rv")] = P.op(
                    "scalar", lambda E: E.nop(), inc="s")

            def rb_pass2(h):
                par = h % 2
                trv = exp_tok[(h, "rv")]
                tr_tok = {}
                for c in range(4):
                    w0 = [trv]
                    if c >= 1:
                        w0.append(tr_tok[c - 1])
                    tA = vts(Abuf[:, 0:129], Eall[:, par * 4 + c, 0:129],
                             rinv16[:, par * 8 + c:par * 8 + c + 1], ALU.mult,
                             wait=tuple(w0), inc="v")
                    a_tok[(h, c)] = tA
                    sl = (c % 2) * 512
                    w1 = [tA]
                    if (h, c - 2) in tr_ev_all:
                        w1 += list(tr_ev_all[(h, c - 2)])
                    elif h > 0 and (h - 1, c + 2) in tr_ev_all:
                        w1 += list(tr_ev_all[(h - 1, c + 2)])
                    mm(PSA[:, sl:sl + 128], Abuf[:, 1:129], I_f[:], True, True,
                       wait=tuple(x for x in w1 if x), tr=True)
                    trt = mm(PSA[0:1, sl + 128:sl + 256], Abuf[:, 0:1],
                             I_f[:], True, True, inc="t", tr=True)
                    tr_tok[c] = trt
                    ta1 = scopy(ATb[:, 0:128], PSA[:, sl:sl + 128],
                                wait=(trt,), inc="s")
                    ta2 = scopy(ATp[0:1, 0:128], PSA[0:1, sl + 128:sl + 256],
                                wait=(trt,), inc="s")
                    tr_ev_all[(h, c)] = (ta1, ta2)
                    sa = 1024 + (c % 2) * 512
                    vt_ = vb[0] if c < 2 else vb[1]
                    vi = c if c < 2 else c - 2
                    w2 = [ta1, ta2, vp_tok[c]]
                    if (h, c - 2) in av_ev:
                        w2 += list(av_ev[(h, c - 2)])
                    elif h > 0 and (h - 1, c + 2) in av_ev:
                        w2 += list(av_ev[(h - 1, c + 2)])
                    mm(PSA[0:64, sa:sa + 128],
                       vt_[:, vi, h * 64:h * 64 + 64],
                       ATb[:, 0:128], True, False, wait=tuple(w2))
                    tokav = mm(PSA[0:64, sa:sa + 128],
                               vprev[0:1, c, h * 64:h * 64 + 64],
                               ATp[0:1, 0:128], False, True, inc="t")
                    p0 = (h % 2) * 64
                    th_ = s_op(lambda E, p_=p0, h_=h, c_=c, sa_=sa:
                               E.activation(
                                   out=cbh_b[p_:p_ + 64, h_ // 2,
                                             c_ * 128:(c_ + 1) * 128],
                                   in_=PSA[0:64, sa_:sa_ + 128],
                                   func=AF.Identity), wait=(tokav,), inc="s")
                    tl_ = v_op(lambda E, p_=p0, h_=h, c_=c, sa_=sa:
                               E.tensor_tensor(
                                   out=cbl_b[p_:p_ + 64, h_ // 2,
                                             c_ * 128:(c_ + 1) * 128],
                                   in0=PSA[0:64, sa_:sa_ + 128],
                                   in1=cbh_b[p_:p_ + 64, h_ // 2,
                                             c_ * 128:(c_ + 1) * 128],
                                   op=ALU.subtract), wait=(tokav, th_),
                               inc="v")
                    av_ev[(h, c)] = (th_, tl_)

            prev_h = None
            for h in range(16):
                rb_pass1(h)
                if prev_h is not None:
                    rb_pass2(prev_h)
                prev_h = h
            rb_pass2(prev_h)
            P.barrier()

            if quarter < 3:
                vcopy(kbound_h[:], kbh[:, :, 511:512].rearrange(
                    "p a b -> p (a b)"), inc="v")
                vcopy(kbound_l[:], kbl[:, :, 511:512].rearrange(
                    "p a b -> p (a b)"), inc="v")
                st(vbound[0:1, :], vb[1][127:128, 1, :])
                P.barrier()

            wo = load_weight("wo_b")
            emit_out_proj(
                wo, lambda hl, kt, tt: (cbh_b if hl == 0 else cbl_b)[
                    :, kt, tt * 128:(tt + 1) * 128], (wa_h, wa_l), 4,
                quarter * 4, brows["bo_b"], src_ap, 0, dst_ap, 0, passes=3,
                ffn_scale=False)
            P.barrier()

    emit_init()
    for s_i in range(n_seq):
        xsrc = x_d[s_i * S:(s_i + 1) * S, :]
        chain = []
        cur_src = xsrc
        for j, sl in enumerate(sublayers):
            is_last = j == len(sublayers) - 1
            dst = y_d[s_i * S:(s_i + 1) * S, :] if is_last else scr_x[j % 3][:, :]
            chain.append((sl, cur_src, dst))
            cur_src = dst
        for sl, src, dst in chain:
            if sl == "a":
                emit_attn_a(src, dst)
            elif sl == "b":
                emit_attn_b(src, dst)
            else:
                emit_ffn(src, dst)
    P.barrier()

    with contextlib.ExitStack() as stack:
        sems = {}
        for name in ("dIn", "dOut", "dW", "t", "v", "s", "g",
                     "m_sync", "m_tensor", "m_vector", "m_scalar", "m_gpsimd"):
            sems[name] = stack.enter_context(nc.semaphore(name))
        block = stack.enter_context(nc.Block())
        P.replay(block, sems)
    assert nc.sbuf_bytes_remaining >= 229344 - 49120, nc.sbuf_bytes_remaining
    return nc


_NC_CACHE = {}


def _get_program():
    key = (SUBLAYERS, N_SEQ)
    if key not in _NC_CACHE:
        _NC_CACHE[key] = build_core_program()
    return _NC_CACHE[key]


def kernel(L0_attn_a_x, L0_attn_a_Wq, L0_attn_a_bq, L0_attn_a_Wk, L0_attn_a_bk,
           L0_attn_a_Wv, L0_attn_a_bv, L0_attn_a_Wo, L0_attn_a_bo,
           L0_attn_b_Wq, L0_attn_b_bq, L0_attn_b_Wk, L0_attn_b_bk,
           L0_attn_b_Wv, L0_attn_b_bv, L0_attn_b_Wo, L0_attn_b_bo,
           L0_attn_b_decay, L0_ffn_ternary_weight, L0_ffn_bias,
           L0_ffn_act_scale):
    nc = _get_program()
    x = np.ascontiguousarray(np.asarray(L0_attn_a_x, dtype=np.float32))
    common = {
        "wq_a": np.ascontiguousarray(L0_attn_a_Wq),
        "wk_a": np.ascontiguousarray(L0_attn_a_Wk),
        "wv_a": np.ascontiguousarray(L0_attn_a_Wv),
        "wo_a": np.ascontiguousarray(L0_attn_a_Wo),
        "wq_b": np.ascontiguousarray(L0_attn_b_Wq),
        "wk_b": np.ascontiguousarray(L0_attn_b_Wk),
        "wv_b": np.ascontiguousarray(L0_attn_b_Wv),
        "wo_b": np.ascontiguousarray(L0_attn_b_Wo),
        "bq_a": np.ascontiguousarray(L0_attn_a_bq.reshape(1, D)),
        "bk_a": np.ascontiguousarray(L0_attn_a_bk.reshape(1, D)),
        "bv_a": np.ascontiguousarray(L0_attn_a_bv.reshape(1, D)),
        "bo_a": np.ascontiguousarray(L0_attn_a_bo.reshape(1, D)),
        "bq_b": np.ascontiguousarray(L0_attn_b_bq.reshape(1, D)),
        "bk_b": np.ascontiguousarray(L0_attn_b_bk.reshape(1, D)),
        "bv_b": np.ascontiguousarray(L0_attn_b_bv.reshape(1, D)),
        "bo_b": np.ascontiguousarray(L0_attn_b_bo.reshape(1, D)),
        "fb": np.ascontiguousarray(L0_ffn_bias.reshape(1, D)),
        "tw": np.ascontiguousarray(L0_ffn_ternary_weight),
        "ascale": np.ascontiguousarray(
            np.asarray(L0_ffn_act_scale, dtype=np.float32).reshape(1, 1)),
        "decay": np.ascontiguousarray(L0_attn_b_decay.reshape(1, 16)),
    }
    in_maps = []
    for c in range(N_CORES):
        m = dict(common)
        m["x"] = np.ascontiguousarray(
            x[c * N_SEQ:(c + 1) * N_SEQ].reshape(N_SEQ * S, D))
        in_maps.append(m)
    res = run_bass_kernel_spmd(nc, in_maps, core_ids=list(range(N_CORES)))
    out = np.concatenate(
        [r["y"].reshape(N_SEQ, S, D) for r in res.results], axis=0)
    return out


# version 11
# speedup vs baseline: 1.0366x; 1.0087x over previous
import contextlib
import numpy as np

import concourse.bass as bass
import concourse.mybir as mybir
import concourse.bass_utils as _bass_utils
from concourse.bass_utils import run_bass_kernel_spmd

_orig_run_command = _bass_utils.run_command


def _run_command_ldwopt(argv, **kwargs):
    argv = [a.replace("--enable-ldw-opt=false", "--enable-ldw-opt=true")
            if isinstance(a, str) else a for a in argv]
    return _orig_run_command(argv, **kwargs)


_bass_utils.run_command = _run_command_ldwopt

F32 = mybir.dt.float32
BF16 = mybir.dt.bfloat16
I32 = mybir.dt.int32
AF = mybir.ActivationFunctionType
ALU = mybir.AluOpType
AX = mybir.AxisListType

N_CORES = 8
N_SEQ = 4
S = 2048
D = 1024
NT = 16
KT = 8
M32C = 12582912.0
M64C = 1.5 * 2.0 ** 52
EPS = 1e-5
SCALE_A = 0.03125
SCALE_B = 0.125
SUBLAYERS = ("a", "f1", "b", "f2")

ENGINES = ("sync", "tensor", "vector", "scalar", "gpsimd")


class Prog:

    def __init__(self):
        self.q = {e: [] for e in ENGINES}
        self.cnt = {}

    def op(self, eng, fn, wait=(), inc=None, amt=1):
        tok = None
        incs = ()
        if inc is not None:
            self.cnt[inc] = self.cnt.get(inc, 0) + amt
            tok = (inc, self.cnt[inc])
            incs = ((inc, self.cnt[inc], amt),)
        self.q[eng].append((tuple(w for w in wait if w is not None), fn, incs))
        return tok

    def barrier(self):
        self.op("sync", lambda E: E.nop(),
                wait=(("dIn", self.cnt.get("dIn", 0)),))
        self.op("gpsimd", lambda E: E.nop(),
                wait=(("dOut", self.cnt.get("dOut", 0)),))
        toks = {}
        for e in ENGINES:
            toks[e] = self.op(e, lambda E: E.nop(), inc="m_" + e)
        for e in ENGINES:
            waits = tuple(toks[o] for o in ENGINES if o != e)
            self.op(e, lambda E: E.nop(), wait=waits)

    def replay(self, block, sems):
        for eng_name in ENGINES:
            items = self.q[eng_name]

            def body(eng, items=items):
                waited = {}
                for waits, fn, incs in items:
                    for s_, v in waits:
                        if waited.get(s_, -1) < v:
                            eng.wait_ge(sems[s_], v)
                            waited[s_] = v
                    inst = fn(eng)
                    for s_, _v, amt in incs:
                        inst.then_inc(sems[s_], amt)
            getattr(block, eng_name)(body)


def build_core_program(sublayers=SUBLAYERS, n_seq=N_SEQ, debug_ffn=False, debug_a=False):
    nc = bass.Bass("TRN2", target_bir_lowering=False,
                   detect_race_conditions=False)
    P = Prog()

    x_d = nc.dram_tensor("x", [n_seq * S, D], F32, kind="ExternalInput").ap()
    y_d = nc.dram_tensor("y", [n_seq * S, D], F32, kind="ExternalOutput").ap()
    W_NAMES = ("wq_a", "wk_a", "wv_a", "wo_a", "wq_b", "wk_b", "wv_b", "wo_b")
    W_IDX = {w: i for i, w in enumerate(W_NAMES)}
    win = {w: nc.dram_tensor(w, [D, D], F32, kind="ExternalInput").ap()
           for w in W_NAMES}
    B_NAMES = ("bq_a", "bk_a", "bv_a", "bo_a", "bq_b", "bk_b", "bv_b", "bo_b", "fb")
    bin_ = {b: nc.dram_tensor(b, [1, D], F32, kind="ExternalInput").ap()
            for b in B_NAMES}
    tw_d = nc.dram_tensor("tw", [D, D], I32, kind="ExternalInput").ap()
    scal_d = nc.dram_tensor("ascale", [1, 1], F32, kind="ExternalInput").ap()
    decay_d = nc.dram_tensor("decay", [1, 16], F32, kind="ExternalInput").ap()

    dbg = {}
    if debug_ffn:
        dbg["xit"] = nc.dram_tensor("dbg_xit", [D, S], BF16,
                                    kind="ExternalOutput").ap()
        dbg["mc"] = nc.dram_tensor("dbg_mc", [128, NT], F32,
                                   kind="ExternalOutput").ap()
        dbg["rc"] = nc.dram_tensor("dbg_rc", [128, NT], F32,
                                   kind="ExternalOutput").ap()
        dbg["sc"] = nc.dram_tensor("dbg_sc", [2, 128], F32,
                                   kind="ExternalOutput").ap()
        dbg["wt"] = nc.dram_tensor("dbg_wt", [D, D], BF16,
                                   kind="ExternalOutput").ap()
        dbg["fbs"] = nc.dram_tensor("dbg_fbs", [1, D], F32,
                                    kind="ExternalOutput").ap()
        dbg["fs"] = nc.dram_tensor("dbg_fs", [4 * 128, D], F32,
                                   kind="ExternalOutput").ap()
        dbg["xqb"] = nc.dram_tensor("dbg_xqb", [128, D], BF16,
                                    kind="ExternalOutput").ap()
        dbg["qf"] = nc.dram_tensor("dbg_qf", [128, 1], F32,
                                   kind="ExternalOutput").ap()
        dbg["cur"] = nc.dram_tensor("dbg_cur", [3, 128], F32,
                                    kind="ExternalOutput").ap()
        dbg["qm1"] = nc.dram_tensor("dbg_qm1", [128, 1], F32,
                                    kind="ExternalOutput").ap()
        dbg["xin1"] = nc.dram_tensor("dbg_xin1", [128, D], F32,
                                     kind="ExternalOutput").ap()
    if debug_a:
        for nm, shp, dt_ in (("a_lth", [D, S], BF16), ("a_ltl", [D, S], BF16),
                             ("a_kh", [D, S], BF16), ("a_kl", [D, S], BF16),
                             ("a_qh", [D, S], BF16), ("a_ql", [D, S], BF16),
                             ("a_at", [2048, 256], F32),
                             ("a_e0", [128, S], F32), ("a_e1", [128, S], F32),
                             ("a_r8", [2, 1024], F32),
                             ("a_cbh", [D, S], BF16), ("a_cbl", [D, S], BF16)):
            dbg[nm] = nc.dram_tensor("dbg_" + nm, shp, dt_,
                                     kind="ExternalOutput").ap()
    scr_w = nc.dram_tensor("scr_w", [16 * D, D], BF16).ap()
    scr_wt = nc.dram_tensor("scr_wt", [D, D], BF16).ap()
    scr_v = nc.dram_tensor("scr_v", [S, D], F32).ap()
    scr_x = [nc.dram_tensor(f"scr_x{i}", [S, D], F32).ap() for i in range(3)]

    def sb(name, shape, dtype, off):
        return nc.alloc_sbuf_tensor_at(name, shape, dtype, offset=off)

    def sbm(name, shape, dtype):
        return nc.alloc_sbuf_tensor(name, shape, dtype)

    ones_row = sbm("ones_row", [1, 128], F32)
    I_f = sbm("I_f", [128, 128], F32)
    I_b = sbm("I_b", [128, 128], BF16)
    iocol = sbm("iocol", [128, 1], F32)
    iof = sbm("iof", [128, 128], F32)
    qflag = sbm("qflag", [128, 1], F32)
    qm1 = sbm("qm1", [128, 1], F32)
    zcol = sbm("zcol", [128, 8], F32)
    zcol_b = sbm("zcol_b", [128, 8], BF16)
    sdf = sbm("sdf", [128, 16], F32)
    bcols = {b: sbm("c_" + b, [128, KT], F32)
             for b in ("bq_a", "bk_a", "bq_b", "bk_b")}
    brows = {b: sbm("r_" + b, [1, D], F32)
             for b in ("bv_a", "bo_a", "bv_b", "bo_b", "fb")}
    brows["fbs"] = brows["fb"]
    s_bc = sbm("s_bc", [128, 1], F32)
    srec_bc = sbm("srec_bc", [128, 1], F32)
    sums_c = sbm("sums_c", [128, NT], F32)
    sumsq_c = sbm("sumsq_c", [128, NT], F32)
    m_c = sbm("m_c", [128, NT], F32)
    r_c = sbm("r_c", [128, NT], F32)
    rs_c = sbm("rs_c", [128, NT], F32)
    var_c = sbm("var_c", [128, NT], F32)
    CURS = ("m", "r", "rs", "bias", "negM", "rv", "sd", "b")
    cur = {k: sbm("cur_" + k, [128, 1], F32) for k in CURS}
    sums8 = sbm("sums8", [128, 8], F32)
    tmp8 = sbm("tmp8", [128, 8], F32)
    rinv8 = sbm("rinv8", [128, 8], F32)
    rinv8x2 = sbm("rinv8x2", [128, 8], F32)
    ln2c = sbm("ln2c", [128, 1], F32)
    sums16 = sbm("sums16", [128, 16], F32)
    tmp16 = sbm("tmp16", [128, 16], F32)
    rinv16 = sbm("rinv16", [128, 16], F32)
    rinv16x2 = sbm("rinv16x2", [128, 16], F32)
    negM8 = sbm("negM8", [128, 8], F32)
    bias8 = sbm("bias8", [128, 8], F32)
    sums8b = sbm("sums8b", [128, 8], F32)
    rinv8b = sbm("rinv8b", [128, 8], F32)
    Dbuf = [sbm(f"Dbuf{i}", [128, 128], F32) for i in range(2)]
    Abuf = sbm("Abuf", [128, 132], F32)
    ATb = sbm("ATb", [128, 132], F32)
    ATp = sbm("ATp", [1, 132], F32)
    kbound_h = sbm("kbound_h", [128, KT], BF16)
    kbound_l = sbm("kbound_l", [128, KT], BF16)
    vbound = sbm("vbound", [1, D], F32)
    lnd_bc = sbm("lnd_bc", [128, 16], F32)
    assert nc.SBUF_PARTITION_SIZE_BYTES - nc.sbuf_bytes_remaining // 1 >= 0

    OFF_XIN = 49120
    OFF_XOUT = 57312
    OFF_WA = 65504
    OFF_BIG1 = 98272
    OFF_BIG2 = 163808
    assert OFF_BIG2 + 64 * 1024 <= 229344

    xin = sb("xin", [128, 2, D], F32, OFF_XIN)
    xout = sb("xout", [128, 2, D], F32, OFF_XOUT)
    STS0 = sb("STS0", [128, D], F32, OFF_WA)
    STS1 = sb("STS1", [128, D], F32, OFF_WA + 4 * 1024)
    FS = [sb(f"FS{i}", [128, D], F32, OFF_WA + i * 4 * 1024) for i in range(4)]
    LNh = sb("LNh", [128, 2, D], BF16, OFF_WA + 8 * 1024)
    LNl = sb("LNl", [128, 2, D], BF16, OFF_WA + 12 * 1024)
    xqb = sb("xqb", [128, D], BF16, OFF_WA)
    wa_h = sb("wa_h", [128, KT, D], BF16, OFF_WA)
    wa_l = sb("wa_l", [128, KT, D], BF16, OFF_WA + 16 * 1024)
    wt_f = sb("wt_f", [128, KT, D], BF16, OFF_WA + 16 * 1024)
    Ebuf = [sb("Ebuf0", [128, S], F32, OFF_WA),
            sb("Ebuf1", [128, S], F32, OFF_WA + 8 * 1024)]
    ATa = sb("ATa", [128, 16, 256], F32, OFF_WA + 16 * 1024)

    LTh = sb("LTh", [128, KT, S], BF16, OFF_BIG1)
    LTl = sb("LTl", [128, KT, S], BF16, OFF_BIG1 + 32 * 1024)
    QSh = sb("QSh", [128, KT, 512], BF16, OFF_XIN)
    QSl = sb("QSl", [128, KT, 512], BF16, OFF_XOUT)
    XIT = sb("XIT", [128, KT, S], BF16, OFF_BIG1)
    LTbh = sb("LTbh", [128, KT, 512], BF16, OFF_BIG1)
    LTbl = sb("LTbl", [128, KT, 512], BF16, OFF_BIG1 + 8 * 1024)
    QSbh = sb("QSbh", [128, KT, 512], BF16, OFF_BIG1 + 16 * 1024)
    QSbl = sb("QSbl", [128, KT, 512], BF16, OFF_BIG1 + 24 * 1024)
    Eall = sb("Eall", [128, 8, 132], F32, OFF_BIG1 + 32 * 1024)
    vprev = sb("vprev", [1, 4, D], F32, OFF_BIG1 + 37 * 1024)

    Kh = sb("Kh", [128, KT, S], BF16, OFF_BIG2)
    Kl = sb("Kl", [128, KT, S], BF16, OFF_BIG2 + 32 * 1024)
    CBh = sb("CBh", [128, KT, S], BF16, OFF_BIG2)
    CBl = sb("CBl", [128, KT, S], BF16, OFF_BIG2 + 32 * 1024)
    kbh = sb("kbh", [128, KT, 512], BF16, OFF_BIG2)
    kbl = sb("kbl", [128, KT, 512], BF16, OFF_BIG2 + 8 * 1024)
    cbh_b = sb("cbh_b", [128, KT, 512], BF16, OFF_BIG2 + 16 * 1024)
    cbl_b = sb("cbl_b", [128, KT, 512], BF16, OFF_BIG2 + 24 * 1024)
    wb1_h = sb("wb1_h", [128, KT, D], BF16, OFF_BIG1)
    wb1_l = sb("wb1_l", [128, KT, D], BF16, OFF_BIG1 + 16 * 1024)
    wb2_h = sb("wb2_h", [128, KT, D], BF16, OFF_BIG2 + 32 * 1024)
    wb2_l = sb("wb2_l", [128, KT, D], BF16, OFF_BIG2 + 48 * 1024)
    FSa = [sb(f"FSa{i}", [128, D], F32, OFF_BIG2 + i * 4 * 1024)
           for i in range(2)]
    LNah = sb("LNah", [128, 2, D], BF16, OFF_BIG2 + 8 * 1024)
    LNal = sb("LNal", [128, 2, D], BF16, OFF_BIG2 + 12 * 1024)
    FSq = [sb(f"FSq{i}", [128, D], F32, OFF_BIG1 + 16 * 1024 + i * 4 * 1024)
           for i in range(2)]
    LNqh = sb("LNqh", [128, 2, D], BF16, OFF_BIG1 + 24 * 1024)
    LNql = sb("LNql", [128, 2, D], BF16, OFF_BIG1 + 28 * 1024)
    xfq = sb("xfq", [128, 2, D], F32, OFF_BIG2)
    vb = [sb("vb0", [128, 2, D], F32, OFF_XIN),
          sb("vb1", [128, 2, D], F32, OFF_XOUT)]
    WS32 = sb("WS32", [128, KT, D], F32, OFF_BIG1)
    WSI = sb("WSI", [128, KT, D], I32, OFF_BIG1 + 32 * 1024)
    WSH = sb("WSH", [128, KT, D], BF16, OFF_BIG2)
    WSL = sb("WSL", [128, KT, D], BF16, OFF_BIG2 + 16 * 1024)

    PSA = nc.alloc_psum_tensor("PSA", [128, 2048], F32)
    PSB = nc.alloc_psum_tensor("PSB", [128, 512], F32)
    PSC = nc.alloc_psum_tensor("PSC", [128, 512], F32)
    PEA = nc.alloc_psum_tensor("PEA", [128, 1024], BF16)
    PEB = nc.alloc_psum_tensor("PEB", [128, 1024], BF16)

    def ld(out_ap, in_ap, wait=(), sem="dIn"):
        return P.op("sync", lambda E, o=out_ap, i=in_ap:
                    E.dma_start(out=o, in_=i), wait, inc=sem, amt=16)

    def st(out_ap, in_ap, wait=()):
        return P.op("gpsimd", lambda E, o=out_ap, i=in_ap:
                    E.dma_start(out=o, in_=i), wait, inc="dOut", amt=16)

    def mm(ps, lhsT, rhs, start, stop, wait=(), inc=None, tp=None, tr=False):
        def f(E, o=ps, l_=lhsT, r_=rhs, st_=start, sp=stop, tp_=tp, t_=tr):
            if t_:
                return E.matmul(out=o, lhsT=l_, rhs=r_, start=st_, stop=sp,
                                tile_position=tp_, is_transpose=True)
            return E.matmul(out=o, lhsT=l_, rhs=r_, start=st_, stop=sp,
                            tile_position=tp_)
        return P.op("tensor", f, wait, inc)

    def v_op(fn, wait=(), inc=None):
        return P.op("vector", fn, wait, inc)

    def s_op(fn, wait=(), inc=None):
        return P.op("scalar", fn, wait, inc)

    def vcopy(dst, src, wait=(), inc=None):
        return v_op(lambda E, d=dst, s_=src: E.tensor_copy(out=d, in_=s_),
                    wait, inc)

    def vts(dst, src, scalar, opc, wait=(), inc=None):
        return v_op(lambda E, d=dst, s_=src, sc=scalar, o=opc:
                    E.tensor_single_scalar(out=d, in_=s_, scalar=sc, op=o),
                    wait, inc)

    def vtt(dst, a, b, opc, wait=(), inc=None):
        return v_op(lambda E, d=dst, a_=a, b_=b, o=opc:
                    E.tensor_tensor(out=d, in0=a_, in1=b_, op=o), wait, inc)

    def vstt(dst, in0, scalar, in1, op0, op1, wait=(), inc=None):
        return v_op(lambda E, d=dst, i0=in0, sc=scalar, i1=in1, o0=op0, o1=op1:
                    E.scalar_tensor_tensor(out=d, in0=i0, scalar=sc, in1=i1,
                                           op0=o0, op1=o1), wait, inc)

    def scopy(dst, src, wait=(), inc=None):
        return s_op(lambda E, d=dst, s_=src: E.copy(out=d, in_=s_), wait, inc)

    def sact(dst, src, func, bias=0.0, scale=1.0, accum=None, wait=(), inc=None):
        return s_op(lambda E, d=dst, s_=src, f=func, b=bias, sc=scale, a=accum:
                    E.activation(out=d, in_=s_, func=f, bias=b, scale=sc,
                                 accum_out=a), wait, inc)

    def w_ap(idx, hilo):
        return scr_w[(2 * idx + hilo) * D:(2 * idx + hilo + 1) * D, :].rearrange(
            "(kt p) d -> p kt d", p=128)

    def load_weight(wname, bh=None, bl=None, wait=()):
        idx = W_IDX[wname]
        bh = wa_h if bh is None else bh
        bl = wa_l if bl is None else bl
        t1 = ld(bh[:], w_ap(idx, 0), wait=wait, sem="dW")
        t2 = ld(bl[:], w_ap(idx, 1), wait=wait, sem="dW")
        return (t1, t2)

    def emit_init():
        P.op("gpsimd", lambda E: E.iota(out=iocol[:], pattern=[[0, 1]], base=0,
                                        channel_multiplier=1,
                                        allow_small_or_imprecise_dtypes=True))
        tg = P.op("gpsimd", lambda E: E.iota(out=iof[:], pattern=[[1, 128]],
                                             base=0, channel_multiplier=0,
                                             allow_small_or_imprecise_dtypes=True),
                  inc="g")
        v_op(lambda E: E.memset(ones_row[:], 1.0))
        v_op(lambda E: E.memset(zcol[:], 0.0))
        tz = v_op(lambda E: E.memset(zcol_b[:], 0.0), inc="v")
        vts(qflag[:], zcol[:, 0:1], 1.0, ALU.add, wait=(tz,))
        vts(ln2c[:], zcol[:, 0:1], 0.6931471805599453, ALU.add)
        v_op(lambda E: E.tensor_scalar(out=qflag[:], in0=qflag[:], scalar1=M64C,
                                       scalar2=-M64C, op0=ALU.add, op1=ALU.add))
        vts(I_f[:], iof[:], iocol[:, 0:1], ALU.is_equal, wait=(tg,))
        vcopy(I_b[:], I_f[:], inc="v")
        v_op(lambda E: E.tensor_scalar(out=qm1[:], in0=qflag[:], scalar1=-1.0,
                                       scalar2=1.0, op0=ALU.mult, op1=ALU.add))

        tb = []
        for b in ("bv_a", "bo_a", "bv_b", "bo_b", "fb"):
            tb.append(ld(brows[b][:], bin_[b][:]))
        prev_ev = None
        for b in bcols:
            trow = ld(vbound[0:1, :], bin_[b][:],
                      wait=(prev_ev,) if prev_ev else ())
            tg2 = None
            for mt in range(KT):
                tg2 = mm(PSC[:, mt:mt + 1], vbound[0:1, mt * 128:(mt + 1) * 128],
                         ones_row[0:1, 0:1], mt == 0, mt == KT - 1,
                         wait=(trow,) if mt == 0 else (),
                         inc="t" if mt == KT - 1 else None)
            prev_ev = scopy(bcols[b][:], PSC[:, 0:KT], wait=(tg2,), inc="s")
        tb.append(prev_ev)
        t_sc = ld(cur["m"][0:1, 0:1], scal_d[:])
        t_dec = ld(lnd_bc[0:1, 0:16], decay_d[:])

        tv = vts(cur["m"][0:1, 0:1], cur["m"][0:1, 0:1], 1e-5, ALU.max,
                 wait=(t_sc,), inc="v")
        tmm = mm(PSB[:, 0:1], ones_row[:, 0:128], cur["m"][0:1, 0:1], True, True,
                 wait=(tv,), inc="t")
        ts1 = scopy(s_bc[:], PSB[:, 0:1], wait=(tmm,), inc="s")
        ts1b = sact(srec_bc[:], s_bc[:], AF.Ln, wait=(ts1,), inc="s")
        ts2 = sact(srec_bc[:], srec_bc[:], AF.Exp, scale=-1.0, wait=(ts1b,),
                   inc="s")
        vts(brows["fb"][:], brows["fb"][:], srec_bc[0:1, 0:1], ALU.mult,
            wait=tuple(tb) + (ts2,), inc="v")

        tl = sact(lnd_bc[0:1, 0:16], lnd_bc[0:1, 0:16], AF.Ln, wait=(t_dec,),
                  inc="s")
        tmm2 = mm(PSB[:, 8:24], ones_row[:, 0:128], lnd_bc[0:1, 0:16], True, True,
                  wait=(tl, ts1), inc="t")
        tl2 = scopy(lnd_bc[:], PSB[:, 8:24], wait=(tmm2,), inc="s")
        tl3 = sact(sdf[:], lnd_bc[:], AF.Exp, scale=iocol[:, 0:1], wait=(tl2,),
                   inc="s")
        vts(sdf[:], sdf[:], SCALE_B, ALU.mult, wait=(tl3,), inc="v")
        P.barrier()

        tld = ld(WSI[:], tw_d.rearrange("(kt p) d -> p kt d", p=128))
        tv1 = vcopy(WS32[:], WSI[:], wait=(tld,), inc="v")
        tv2 = vts(WSH[:], WS32[:], -1.0, ALU.add, wait=(tv1,), inc="v")
        ev_toks = []
        pe_ev = {}
        for okt in range(KT):
            for g in range(2):
                pe = PEA if g == 0 else PEB
                key = (okt * 2 + g) % 2
                tts = None
                for j in range(4):
                    ikt = g * 4 + j
                    w = (tv2,)
                    if okt * 2 + g >= 2:
                        w = (tv2, pe_ev.get(okt * 2 + g - 2))
                    tts = mm(pe[:, j * 128:(j + 1) * 128],
                             WSH[:, okt, ikt * 128:(ikt + 1) * 128], I_b[:],
                             j == 0, j == 3, wait=w,
                             inc="t" if j == 3 else None, tr=True)
                dst = WSL[:, g * 4:(g + 1) * 4, okt * 128:okt * 128 + 128]
                src = pe[:, 0:512].rearrange("p (a b) -> p a b", a=4)
                ev = v_op(lambda E, d=dst, s_=src:
                          E.tensor_copy(out=d, in_=s_), wait=(tts,), inc="v")
                pe_ev[okt * 2 + g] = ev
                ev_toks.append(ev)
        st(scr_wt.rearrange("(kt p) d -> p kt d", p=128), WSL[:],
           wait=tuple(ev_toks))
        P.barrier()

        for w, idx in W_IDX.items():
            tldw = ld(WS32[:], win[w].rearrange("(kt p) d -> p kt d", p=128))
            th = vcopy(WSH[:], WS32[:], wait=(tldw,), inc="v")
            tl_ = v_op(lambda E: E.tensor_sub(out=WSL[:], in0=WS32[:],
                                              in1=WSH[:]), wait=(th,), inc="v")
            sh = st(w_ap(idx, 0), WSH[:], wait=(th,))
            sl = st(w_ap(idx, 1), WSL[:], wait=(tl_,))
            P.op("sync", lambda E: E.nop(), wait=(sh, sl))
        P.barrier()

    def emit_stats(src_ap, row0=0, sts=None):
        sts0, sts1 = (STS0, STS1) if sts is None else sts
        vt = [None] * NT
        stk = [None] * NT
        for t in range(NT):
            slot = t % 2
            w = []
            if t >= 2:
                w = [vt[t - 2], stk[t - 2]]
            l_ = ld(xin[:, slot, :],
                    src_ap[row0 + t * 128:row0 + (t + 1) * 128, :], wait=tuple(w))
            vt[t] = v_op(lambda E, sl=slot, tt=t, o=sts0: E.tensor_scalar(
                out=o[:], in0=xin[:, sl, :], scalar1=1.0, scalar2=0.0,
                op0=ALU.mult, op1=ALU.add,
                accum_out=sums_c[:, tt:tt + 1]), wait=(l_,), inc="v")
            stk[t] = s_op(lambda E, sl=slot, tt=t, o=sts1: E.activation(
                out=o[:], in_=xin[:, sl, :], func=AF.Square,
                accum_out=sumsq_c[:, tt:tt + 1]), wait=(l_,), inc="s")
        tv1 = vts(m_c[:], sums_c[:], 1.0 / D, ALU.mult,
                  wait=(vt[NT - 1], stk[NT - 1]), inc="v")
        tv2 = vtt(var_c[:], m_c[:], m_c[:], ALU.mult, wait=(tv1,), inc="v")
        tv3 = vstt(var_c[:], sumsq_c[:], 1.0 / D, var_c[:], ALU.mult,
                   ALU.subtract, wait=(tv2,), inc="v")
        tv4 = vts(var_c[:], var_c[:], EPS, ALU.add, wait=(tv3,), inc="v")
        s1 = sact(r_c[:], var_c[:], AF.Ln, wait=(tv4,), inc="s")
        s2 = sact(r_c[:], r_c[:], AF.Exp, scale=-0.5, wait=(s1,), inc="s")
        return s2

    def emit_ln_fm(src_ap, row0, ntiles, tile0, dst_h, dst_l, col0, wtok,
                   scratch=None):
        fs0, fs1, lnh, lnl = ((FS[0], FS[1], LNh, LNl) if scratch is None
                              else scratch)
        fsb = (fs0, fs1)
        ev_h = {}
        ev_l = {}
        vtok = {}
        for t in range(ntiles):
            slot = t % 2
            buf = t % 2
            gt = tile0 + t
            w = [wtok]
            if t >= 2:
                w.append(vtok[t - 2])
            l_ = ld(xin[:, slot, :],
                    src_ap[row0 + t * 128:row0 + (t + 1) * 128, :], wait=tuple(w))
            wv = [l_]
            if t >= 2:
                wv += [ev_h[t - 2], ev_l[t - 2]]
            vts(fsb[buf][:], xin[:, slot, :], m_c[:, gt:gt + 1], ALU.subtract,
                wait=tuple(wv))
            th = vts(lnh[:, buf, :], fsb[buf][:], r_c[:, gt:gt + 1], ALU.mult,
                     inc="v")
            tl_ = vstt(lnl[:, buf, :], fsb[buf][:], r_c[:, gt:gt + 1],
                       lnh[:, buf, :], ALU.mult, ALU.subtract, inc="v")
            vtok[t] = tl_
            col = col0 + t * 128
            for (pe, lnbuf, wt, evd, dst) in ((PEA, lnh, th, ev_h, dst_h),
                                              (PEB, lnl, tl_, ev_l, dst_l)):
                tt = None
                for kt in range(KT):
                    w2 = [wt]
                    if t >= 1:
                        w2.append(evd.get(t - 1))
                    tt = mm(pe[:, kt * 128:(kt + 1) * 128],
                            lnbuf[:, buf, kt * 128:(kt + 1) * 128], I_b[:],
                            kt == 0, kt == KT - 1, wait=tuple(x for x in w2 if x),
                            inc="t" if kt == KT - 1 else None, tr=True)
                dstap = dst[:, :, col:col + 128]
                srcap = pe[:, 0:1024].rearrange("p (a b) -> p a b", a=KT)
                if pe is PEA:
                    evd[t] = scopy(dstap, srcap, wait=(tt,), inc="s")
                else:
                    evd[t] = v_op(lambda E, d=dstap, s_=srcap:
                                  E.tensor_copy(out=d, in_=s_), wait=(tt,),
                                  inc="v")
        return (ev_h[ntiles - 1], ev_l[ntiles - 1])

    def proj_fm(wtoks, mov_h, mov_l, ncols, dst_fn, bname, col0=0, wbufs=None):
        wbh, wbl = (wa_h, wa_l) if wbufs is None else wbufs
        evt = {}
        nunit = 0
        for ntc in range(ncols // 512):
            for mt in range(KT):
                bank = (PSB, PSC)[nunit % 2]
                w0 = list(wtoks)
                if nunit >= 2:
                    w0 += list(evt[nunit - 2])
                tokmm = None
                for p, (wsel, msel) in enumerate(((wbh, mov_h), (wbl, mov_h),
                                                  (wbh, mov_l))):
                    for kt in range(KT):
                        last = p == 2 and kt == KT - 1
                        tokmm = mm(bank[:, 0:512],
                                   wsel[:, kt, mt * 128:(mt + 1) * 128],
                                   msel[:, kt, ntc * 512:(ntc + 1) * 512],
                                   p == 0 and kt == 0, last,
                                   wait=tuple(w0) if (p == 0 and kt == 0) else (),
                                   inc="t" if last else None)
                hi_ap, lo_ap = dst_fn(mt, col0 + ntc * 512, 512)
                te1 = sact(hi_ap, bank[:, 0:512], AF.Identity,
                           bias=bcols[bname][:, mt:mt + 1], wait=(tokmm,),
                           inc="s")
                te2 = vstt(lo_ap, bank[:, 0:512], bcols[bname][:, mt:mt + 1],
                           hi_ap, ALU.add, ALU.subtract, wait=(tokmm, te1),
                           inc="v")
                evt[nunit] = (te1, te2)
                nunit += 1
        return evt[nunit - 1]

    def proj_fm_pairs(wtoks, mov_h, mov_l, ncols, dst_fn, bname):
        evt = {}
        nunit = 0
        psel = ((wa_h, mov_h), (wa_l, mov_h), (wa_h, mov_l))
        for pair in range(ncols // 1024):
            for mt in range(KT):
                g = nunit % 2
                b0 = PSA[:, g * 1024:g * 1024 + 512]
                b1 = PSA[:, g * 1024 + 512:g * 1024 + 1024]
                w0 = list(wtoks)
                if nunit >= 2:
                    w0 += list(evt[nunit - 2])
                tokmm = None
                for p, (wsel, msel) in enumerate(psel):
                    for kt in range(KT):
                        lhs = wsel[:, kt, mt * 128:(mt + 1) * 128]
                        first = p == 0 and kt == 0
                        last = p == 2 and kt == KT - 1
                        mm(b0, lhs,
                           msel[:, kt, pair * 1024:pair * 1024 + 512],
                           first, last, wait=tuple(w0) if first else ())
                        tokmm = mm(b1, lhs,
                                   msel[:, kt, pair * 1024 + 512:
                                        pair * 1024 + 1024],
                                   first, last,
                                   inc="t" if last else None)
                toks = []
                for j, bk in enumerate((b0, b1)):
                    c0 = pair * 1024 + j * 512
                    hi_ap, lo_ap = dst_fn(mt, c0, 512)
                    te1 = sact(hi_ap, bk, AF.Identity,
                               bias=bcols[bname][:, mt:mt + 1], wait=(tokmm,),
                               inc="s")
                    te2 = vstt(lo_ap, bk, bcols[bname][:, mt:mt + 1],
                               hi_ap, ALU.add, ALU.subtract,
                               wait=(tokmm, te1), inc="v")
                    toks += [te1, te2]
                evt[nunit] = tuple(toks)
                nunit += 1
        return evt

    def proj_tm(first_w, act_fn, wmov, ntiles, brow, evac, passes=3,
                pre_cb=None, tile_w=None):
        evt = {}
        for tt in range(ntiles):
            if pre_cb is not None:
                pre_cb(tt)
            g = tt % 2
            b0 = PSA[:, g * 1024:g * 1024 + 512]
            b1 = PSA[:, g * 1024 + 512:g * 1024 + 1024]
            w0 = list(first_w)
            if tile_w is not None:
                w0 += list(tile_w(tt))
            if tt >= 2:
                w0 += list(evt[tt - 2])
            psel = (((0, 0), (1, 0), (0, 1)) if passes == 3 else ((0, 0),))
            first = True
            for (ah, wl_) in psel:
                for kt in range(KT):
                    lhs = act_fn(ah, kt, tt)
                    mm(b0, lhs, wmov[wl_][:, kt, 0:512], first, False,
                       wait=tuple(w0) if first else ())
                    mm(b1, lhs, wmov[wl_][:, kt, 512:1024], first, False)
                    first = False
            mm(b0, ones_row[:, 0:128], brow[0:1, 0:512], False, True)
            tokmm = mm(b1, ones_row[:, 0:128], brow[0:1, 512:1024], False,
                       True, inc="t")
            evt[tt] = tuple(evac(tt, 0, b0, tokmm)) +                 tuple(evac(tt, 1, b1, tokmm))
        return evt

    def emit_attn_a(src_ap, dst_ap):
        stok = emit_stats(src_ap)
        P.barrier()
        emit_ln_fm(src_ap, 0, NT, 0, LTh, LTl, 0, stok)
        P.barrier()
        wv = load_weight("wv_a")
        if debug_a:
            st(dbg["a_lth"].rearrange("(kt p) t -> p kt t", p=128), LTh[:])
            st(dbg["a_ltl"].rearrange("(kt p) t -> p kt t", p=128), LTl[:])
            P.barrier()

        spill_toks = {}

        def evac_v(tt, ntc, bank, tokmm):
            wv2 = []
            if tt >= 2:
                wv2 = [spill_toks[tt - 2]]
            te = scopy(xout[:, tt % 2, ntc * 512:(ntc + 1) * 512], bank,
                       wait=(tokmm,) + tuple(wv2), inc="s")
            if ntc == 1:
                spill_toks[tt] = st(scr_v[tt * 128:(tt + 1) * 128, :],
                                    xout[:, tt % 2, :], wait=(te,))
            return (te,)

        ev_v = proj_tm(wv, lambda h, kt, tt: (LTh if h == 0 else LTl)[
            :, kt, tt * 128:(tt + 1) * 128],
            (wa_h, wa_l), NT, brows["bv_a"], evac_v, passes=3)
        P.barrier()

        wk = load_weight("wk_a")
        ev_k = proj_fm_pairs(wk, LTh, LTl, S,
                             lambda mt, c0, w: (Kh[:, mt, c0:c0 + w],
                                                Kl[:, mt, c0:c0 + w]), "bk_a")
        P.barrier()
        if debug_a:
            st(dbg["a_kh"].rearrange("(kt p) t -> p kt t", p=128), Kh[:])
            st(dbg["a_kl"].rearrange("(kt p) t -> p kt t", p=128), Kl[:])
            P.barrier()

        wq = load_weight("wq_a")

        def qdst(mt, c0, w):
            if c0 < 512:
                return (QSh[:, mt, c0:c0 + w], QSl[:, mt, c0:c0 + w])
            return (LTh[:, mt, c0 - 512:c0 - 512 + w],
                    LTl[:, mt, c0 - 512:c0 - 512 + w])

        proj_fm(wq, LTh, LTl, S, qdst, "bq_a")
        P.barrier()
        if debug_a:
            st(dbg["a_qh"][:, 0:512].rearrange("(kt p) t -> p kt t", p=128),
               QSh[:])
            st(dbg["a_ql"][:, 0:512].rearrange("(kt p) t -> p kt t", p=128),
               QSl[:])
            st(dbg["a_qh"][:, 512:S].rearrange("(kt p) t -> p kt t", p=128),
               LTh[:, :, 0:S - 512])
            st(dbg["a_ql"][:, 512:S].rearrange("(kt p) t -> p kt t", p=128),
               LTl[:, :, 0:S - 512])
            P.barrier()

        def q_ap(hilo, kt, c0):
            if c0 < 512:
                t_ = (QSh, QSl)[hilo]
                return t_[:, kt, c0:c0 + 128]
            t_ = (LTh, LTl)[hilo]
            return t_[:, kt, c0 - 512:c0 - 512 + 128]

        at_ev = {}
        dg_tok = {}
        exp_tok = {}
        sm_done = {}
        sc_last = {}

        def p2_scores(idx):
            n, qt = idx // 2, idx % 2
            E = Ebuf[idx % 2]
            par = idx % 2
            c0 = n * 256 + qt * 128
            tsum = None
            for mp in range(4):
                tokmm = None
                for p, hl in enumerate(((0, 0), (1, 0), (0, 1))):
                    qh, kh = hl
                    for kt in range(KT):
                        lhs = q_ap(qh, kt, c0)
                        for m in (2 * mp, 2 * mp + 1):
                            start = p == 0 and kt == 0 and m == 2 * mp
                            stop = p == 2 and kt == KT - 1 and m == 2 * mp + 1
                            w0 = ()
                            if start and idx > 0:
                                w0 = tuple(exp_tok[(idx - 1, mp)])
                            tokmm = mm(
                                PSA[:, m * 256:(m + 1) * 256], lhs,
                                (Kh, Kl)[kh][:, kt, m * 256:(m + 1) * 256],
                                start, stop, wait=w0,
                                inc="t" if stop else None)
                tmx = v_op(lambda E_, mp_=mp: E_.reduce_max(
                    out=negM8[:, mp_:mp_ + 1],
                    in_=PSA[:, mp_ * 512:(mp_ + 1) * 512], axis=AX.X,
                    negate=True), wait=(tokmm,))
                tb_ = v_op(lambda E_, mp_=mp: E_.tensor_single_scalar(
                    out=bias8[:, mp_:mp_ + 1], in_=negM8[:, mp_:mp_ + 1],
                    scalar=SCALE_A, op=ALU.mult), inc="v")
                wexp = [tb_]
                if idx >= 2 and (idx - 2, 7) in dg_tok:
                    wexp.append(dg_tok[(idx - 2, 7)])
                texp = sact(E[:, mp * 512:(mp + 1) * 512],
                            PSA[:, mp * 512:(mp + 1) * 512], AF.Exp,
                            bias=bias8[:, mp:mp + 1], scale=SCALE_A,
                            wait=tuple(wexp), inc="s")
                exp_tok[(idx, mp)] = (tb_, texp)
                tsum = v_op(lambda E_, Ex=E, mp_=mp, pr=par: E_.reduce_sum(
                    out=sums16[:, pr * 8 + 2 * mp_:pr * 8 + 2 * mp_ + 2],
                    in_=Ex[:, mp_ * 512:(mp_ + 1) * 512].rearrange(
                        "p (a b) -> p a b", a=2),
                    axis=AX.X), wait=(texp,), inc="v")
            sc_last[idx] = tokmm
            tln = sact(tmp16[:, par * 8:par * 8 + 8],
                       sums16[:, par * 8:par * 8 + 8], AF.Ln, wait=(tsum,),
                       inc="s")
            trv = sact(rinv16[:, par * 8:par * 8 + 8],
                       tmp16[:, par * 8:par * 8 + 8], AF.Exp, scale=-1.0,
                       wait=(tln,), inc="s")
            trv2 = sact(rinv16x2[:, par * 8:par * 8 + 8],
                        tmp16[:, par * 8:par * 8 + 8], AF.Exp, scale=-1.0,
                        bias=ln2c[:, 0:1], wait=(tln,), inc="s")
            sm_done[idx] = (trv, trv2)

        def p2_diag(idx):
            n, qt = idx // 2, idx % 2
            E = Ebuf[idx % 2]
            par = idx % 2
            for m in range(8):
                wv2 = list(sm_done[idx])
                if m >= 2:
                    wv2.append(dg_tok[(idx, m - 2)])
                elif idx > 0:
                    wv2.append(dg_tok[(idx - 1, m + 6)])
                rsrc = rinv16x2 if m == n else rinv16
                tD = vts(Dbuf[m % 2][:], I_f[:],
                         rsrc[:, par * 8 + m:par * 8 + m + 1], ALU.mult,
                         wait=tuple(wv2), inc="v")
                wmm = [tD]
                if m == 0 and idx > 0:
                    wmm += list(at_ev[idx - 1])
                mm(PSB[:, 0:128], E[:, m * 256:m * 256 + 128], Dbuf[m % 2][:],
                   m == 0, m == 7, wait=tuple(wmm))
                dg_tok[(idx, m)] = mm(PSC[:, 0:128],
                                      E[:, m * 256 + 128:m * 256 + 256],
                                      Dbuf[m % 2][:], m == 0, m == 7, inc="t")
            ta1 = scopy(ATa[:, n * 2, qt * 128:qt * 128 + 128], PSB[:, 0:128],
                        wait=(dg_tok[(idx, 7)],), inc="s")
            ta2 = scopy(ATa[:, n * 2 + 1, qt * 128:qt * 128 + 128],
                        PSC[:, 0:128], wait=(dg_tok[(idx, 7)],), inc="s")
            at_ev[idx] = (ta1, ta2)

        prev_idx = None
        pre_vload = {}
        for idx in range(16):
            p2_scores(idx)
            if idx == 3:
                pre_vload[0] = (
                    ld(xin[:, 0, :], scr_v[0:128, :], wait=(sc_last[3],)),
                    ld(xin[:, 1, :], scr_v[128:256, :], wait=(sc_last[3],)))
                wsp = (sc_last[3], spill_toks[NT - 2], spill_toks[NT - 1])
                pre_vload[1] = (
                    ld(xout[:, 0, :], scr_v[256:384, :], wait=wsp),
                    ld(xout[:, 1, :], scr_v[384:512, :], wait=wsp))
            if prev_idx is not None:
                p2_diag(prev_idx)
            prev_idx = idx
        p2_diag(prev_idx)
        P.barrier()
        if debug_a:
            st(dbg["a_at"].rearrange("(a p) q -> p a q", p=128), ATa[:])
            st(dbg["a_e0"][:], Ebuf[0][:])
            st(dbg["a_e1"][:], Ebuf[1][:])
            st(dbg["a_r8"][0:1, :].rearrange("one (p w) -> (one p) w", w=8),
               rinv8[:])
            st(dbg["a_r8"][1:2, :].rearrange("one (p w) -> (one p) w", w=8),
               sums8[:])
            P.barrier()

        vload = {}
        avtok = {}
        evtok = {}
        nunit = 0
        for n in range(8):
            vrb = (xin, xout)[n % 2]
            if n in pre_vload:
                vload[n] = pre_vload[n]
            else:
                w0 = [avtok[n - 2]] if n >= 2 else []
                vload[n] = (
                    ld(vrb[:, 0, :], scr_v[n * 256:n * 256 + 128, :],
                       wait=tuple(w0)),
                    ld(vrb[:, 1, :], scr_v[n * 256 + 128:n * 256 + 256, :],
                       wait=tuple(w0)))
            for et in range(KT):
                bank = (PSB, PSC)[nunit % 2]
                w1 = list(vload[n])
                if nunit >= 2:
                    w1 += list(evtok[nunit - 2])
                mm(bank[:, 0:256], vrb[:, 0, et * 128:(et + 1) * 128],
                   ATa[:, n * 2, 0:256], True, False, wait=tuple(w1))
                tokav = mm(bank[:, 0:256], vrb[:, 1, et * 128:(et + 1) * 128],
                           ATa[:, n * 2 + 1, 0:256], False, True, inc="t")
                if et == KT - 1:
                    avtok[n] = tokav
                th = sact(CBh[:, et, n * 256:(n + 1) * 256], bank[:, 0:256],
                          AF.Identity, wait=(tokav,), inc="s")
                tl_ = vtt(CBl[:, et, n * 256:(n + 1) * 256], bank[:, 0:256],
                          CBh[:, et, n * 256:(n + 1) * 256], ALU.subtract,
                          wait=(tokav, th), inc="v")
                evtok[nunit] = (th, tl_)
                nunit += 1
        P.barrier()
        if debug_a:
            st(dbg["a_cbh"].rearrange("(kt p) t -> p kt t", p=128), CBh[:])
            st(dbg["a_cbl"].rearrange("(kt p) t -> p kt t", p=128), CBl[:])
            P.barrier()

        wo = load_weight("wo_a")
        emit_out_proj(wo, lambda h, kt, tt: (CBh if h == 0 else CBl)[
            :, kt, tt * 128:(tt + 1) * 128], (wa_h, wa_l), NT, 0,
            brows["bo_a"], src_ap, 0, dst_ap, 0, passes=3, ffn_scale=False)
        P.barrier()

    def emit_out_proj(first_w, act_fn, wmov, ntiles, tile0, brow,
                      res_ap, res_row0, dst_ap, dst_row0, passes, ffn_scale,
                      pre_cb=None, tile_w=None):
        res_ld = {}
        st_tok = {}

        def evac(tt, ntc, bank, tokmm):
            if ntc == 0 and tt not in res_ld:
                w = []
                if tt >= 2:
                    w = [st_tok[tt - 2]]
                res_ld[tt] = ld(
                    xin[:, tt % 2, :],
                    res_ap[res_row0 + (tile0 + tt) * 128:
                           res_row0 + (tile0 + tt + 1) * 128, :], wait=tuple(w))
            sl = tt % 2
            wst = (st_tok.get(tt - 2),) if tt >= 2 else ()
            if ffn_scale:
                te = vstt(xout[:, sl, ntc * 512:(ntc + 1) * 512], bank,
                          s_bc[:, 0:1], xin[:, sl, ntc * 512:(ntc + 1) * 512],
                          ALU.mult, ALU.add,
                          wait=(tokmm, res_ld[tt]) + wst, inc="v")
            else:
                te = vtt(xout[:, sl, ntc * 512:(ntc + 1) * 512], bank,
                         xin[:, sl, ntc * 512:(ntc + 1) * 512], ALU.add,
                         wait=(tokmm, res_ld[tt]) + wst, inc="v")
            if ntc == 1:
                st_tok[tt] = st(
                    dst_ap[dst_row0 + (tile0 + tt) * 128:
                           dst_row0 + (tile0 + tt + 1) * 128, :],
                    xout[:, sl, :], wait=(te,))
            return (te,)

        return proj_tm(first_w, act_fn, wmov, ntiles, brow, evac, passes=passes,
                       pre_cb=pre_cb, tile_w=tile_w)

    def emit_ffn(src_ap, dst_ap):
        stok = emit_stats(src_ap)
        twt = ld(wt_f[:], scr_wt.rearrange("(kt p) d -> p kt d", p=128))
        stok = vts(rs_c[:], r_c[:], srec_bc[:, 0:1], ALU.mult, wait=(stok,),
                   inc="v")
        P.barrier()
        pe_tok = {}
        ev_tok = {}

        def emit_quant_tile(t):
            slot = t % 2
            w = [stok]
            if t >= 2:
                w.append(pe_tok[t - 2])
            l_ = ld(xfq[:, slot, :], src_ap[t * 128:(t + 1) * 128, :],
                    wait=tuple(w))
            wv = [l_]
            if t >= 1:
                wv.append(pe_tok[t - 1])
            vts(FS[0][:], xfq[:, slot, :], m_c[:, t:t + 1], ALU.subtract,
                wait=tuple(wv))
            vts(FS[1][:], FS[0][:], rs_c[:, t:t + 1], ALU.mult)
            v_op(lambda E: E.tensor_scalar(out=FS[2][:], in0=FS[1][:],
                                           scalar1=M32C, scalar2=-M32C,
                                           op0=ALU.add, op1=ALU.add))
            v_op(lambda E: E.tensor_scalar(out=FS[3][:], in0=FS[1][:],
                                           scalar1=M64C, scalar2=-M64C,
                                           op0=ALU.add, op1=ALU.add))
            vts(FS[0][:], FS[2][:], qm1[:, 0:1], ALU.mult)
            vtt(FS[1][:], FS[0][:], FS[3][:], ALU.add)
            tq = v_op(lambda E: E.tensor_scalar(out=xqb[:],
                                                in0=FS[1][:], scalar1=127.0,
                                                scalar2=-127.0, op0=ALU.min,
                                                op1=ALU.max), inc="v")
            pe = (PEA, PEB)[t % 2]
            tt_ = None
            for kt in range(KT):
                w2 = [tq]
                if t >= 2:
                    w2.append(ev_tok[t - 2])
                tt_ = mm(pe[:, kt * 128:(kt + 1) * 128],
                         xqb[:, kt * 128:(kt + 1) * 128], I_b[:],
                         kt == 0, kt == KT - 1, wait=tuple(w2),
                         inc="t" if kt == KT - 1 else None, tr=True)
            pe_tok[t] = tt_
            ev_tok[t] = scopy(XIT[:, :, t * 128:(t + 1) * 128],
                              pe[:, 0:1024].rearrange("p (a b) -> p a b", a=KT),
                              wait=(tt_,), inc="s")

        for _t in range(NT):
            emit_quant_tile(_t)
        P.barrier()
        if debug_ffn:
            st(dbg["xit"].rearrange("(kt p) t -> p kt t", p=128), XIT[:])
            st(dbg["mc"][:], m_c[:])
            st(dbg["rc"][:], r_c[:])
            st(dbg["sc"][0:1, :].rearrange("one (p w) -> (one p) w", w=1),
               s_bc[:])
            st(dbg["sc"][1:2, :].rearrange("one (p w) -> (one p) w", w=1),
               srec_bc[:])
            st(dbg["wt"].rearrange("(kt p) t -> p kt t", p=128), wt_f[:])
            st(dbg["fbs"][:], brows["fbs"][:])
            for i_ in range(4):
                st(dbg["fs"][i_ * 128:(i_ + 1) * 128, :], FS[i_][:])
            st(dbg["xqb"][:], xqb[:])
            st(dbg["qf"][:], qflag[:])
            st(dbg["qm1"][:], qm1[:])
            st(dbg["cur"][0:1, :].rearrange("one (p w) -> (one p) w", w=1),
               cur["m"][:])
            st(dbg["cur"][1:2, :].rearrange("one (p w) -> (one p) w", w=1),
               cur["r"][:])
            st(dbg["cur"][2:3, :].rearrange("one (p w) -> (one p) w", w=1),
               cur["rs"][:])
            st(dbg["xin1"][:], xin[:, 1, :])
            P.barrier()
        emit_out_proj((twt, ev_tok[NT - 1]),
                      lambda h, kt, tt: XIT[:, kt, tt * 128:(tt + 1) * 128],
                      (wt_f, wt_f), NT, 0, brows["fbs"], src_ap, 0, dst_ap, 0,
                      passes=1, ffn_scale=True)
        P.barrier()

    def emit_attn_b(src_ap, dst_ap):
        stok = emit_stats(src_ap)
        P.barrier()
        for quarter in range(4):
            row0 = quarter * 512
            emit_ln_fm(src_ap, row0, 4, quarter * 4, LTbh, LTbl, 0, stok)
            P.barrier()

            wv = load_weight("wv_b")
            vp_tok = {}

            def evac_vb(tt, ntc, bank, tokmm):
                dst = vb[0][:, tt, :] if tt < 2 else vb[1][:, tt - 2, :]
                te = scopy(dst[:, ntc * 512:(ntc + 1) * 512], bank,
                           wait=(tokmm,), inc="s")
                if ntc == 1 and tt < 3:
                    src_t = vb[0] if tt < 2 else vb[1]
                    si = tt if tt < 2 else tt - 2
                    vp_tok[tt + 1] = st(vprev[0:1, tt + 1, :],
                                        src_t[127:128, si, :], wait=(te,))
                return (te,)

            proj_tm(wv, lambda h, kt, tt: (LTbh if h == 0 else LTbl)[
                :, kt, tt * 128:(tt + 1) * 128], (wa_h, wa_l), 4,
                brows["bv_b"], evac_vb, passes=3)
            if quarter == 0:
                vp_tok[0] = v_op(lambda E: E.memset(vprev[0:1, 0, :], 0.0),
                                 inc="v")
            else:
                vp_tok[0] = v_op(lambda E: E.tensor_copy(out=vprev[0:1, 0, :],
                                                         in_=vbound[0:1, :]),
                                 inc="v")
            P.barrier()

            wk = load_weight("wk_b")
            proj_fm(wk, LTbh, LTbl, 512,
                    lambda mt, c0, w: (kbh[:, mt, c0:c0 + w],
                                       kbl[:, mt, c0:c0 + w]), "bk_b")
            P.barrier()

            wq = load_weight("wq_b")
            proj_fm(wq, LTbh, LTbl, 512,
                    lambda mt, c0, w: (QSbh[:, mt, c0:c0 + w],
                                       QSbl[:, mt, c0:c0 + w]), "bq_b")
            P.barrier()

            def qb_ap(hilo, h, c):
                t_ = (QSbh, QSbl)[hilo]
                p0 = (h % 2) * 64
                return t_[p0:p0 + 64, h // 2, c * 128:(c + 1) * 128]

            def kb_ap(hilo, h, c0, c1):
                t_ = (kbh, kbl)[hilo]
                p0 = (h % 2) * 64
                return t_[p0:p0 + 64, h // 2, c0:c1]

            exp_tok = {}
            av_ev = {}
            tr_ev_all = {}
            a_tok = {}
            sd_tok = {}

            def rb_pass1(h):
                tp = (64, 0) if h % 2 == 1 else None
                par = h % 2
                wsd = [exp_tok.get((h - 1, 3))] if h > 0 else []
                tsd = vcopy(cur["sd"][:], sdf[:, h:h + 1], wait=tuple(wsd),
                            inc="v")
                sd_tok[h] = tsd
                tsum = None
                for c in range(4):
                    bank = (PSB, PSC)[c % 2]
                    w0 = [tsd]
                    if (h, c - 2) in exp_tok:
                        w0.append(exp_tok[(h, c - 2)])
                    elif h > 0 and c < 2:
                        w0.append(exp_tok[(h - 1, c + 2)])
                    first = True
                    for p, hl in enumerate(((0, 0), (1, 0), (0, 1))):
                        qh, kh = hl
                        if c == 0 and quarter == 0:
                            kprev = zcol_b[(h % 2) * 64:(h % 2) * 64 + 64, 0:1]
                        elif c == 0:
                            kprev = (kbound_h, kbound_l)[kh][
                                (h % 2) * 64:(h % 2) * 64 + 64,
                                h // 2:h // 2 + 1]
                        else:
                            kprev = kb_ap(kh, h, c * 128 - 1, c * 128)
                        mm(bank[:, 0:1], qb_ap(qh, h, c), kprev,
                           first, False, wait=tuple(w0) if first else (),
                           tp=tp)
                        first = False
                    tokmm = None
                    for p, hl in enumerate(((0, 0), (1, 0), (0, 1))):
                        qh, kh = hl
                        tokmm = mm(bank[:, 1:129], qb_ap(qh, h, c),
                                   kb_ap(kh, h, c * 128, (c + 1) * 128),
                                   False, p == 2, tp=tp,
                                   inc="t" if p == 2 else None)
                    sl8 = par * 4 + c
                    v_op(lambda E, b=bank, s_=sl8: E.reduce_max(
                        out=negM8[:, s_:s_ + 1], in_=b[:, 0:129], axis=AX.X,
                        negate=True), wait=(tokmm,))
                    tb_ = v_op(lambda E, s_=sl8: E.tensor_single_scalar(
                        out=bias8[:, s_:s_ + 1], in_=negM8[:, s_:s_ + 1],
                        scalar=cur["sd"][:, 0:1], op=ALU.mult), inc="v")
                    wexp = [tb_]
                    if h >= 2 and (h - 2, 3) in a_tok:
                        wexp.append(a_tok[(h - 2, 3)])
                    texp = sact(Eall[:, par * 4 + c, 0:129], bank[:, 0:129],
                                AF.Exp, bias=bias8[:, sl8:sl8 + 1],
                                scale=cur["sd"][:, 0:1], wait=tuple(wexp),
                                inc="s")
                    exp_tok[(h, c)] = texp
                    tsum = v_op(lambda E, c_=c, pr=par: E.reduce_sum(
                        out=sums16[:, pr * 8 + c_:pr * 8 + c_ + 1],
                        in_=Eall[:, pr * 4 + c_, 0:129],
                        axis=AX.X), wait=(texp,), inc="v")
                tln = sact(tmp16[:, par * 8:par * 8 + 4],
                           sums16[:, par * 8:par * 8 + 4], AF.Ln,
                           wait=(tsum,), inc="s")
                sact(rinv16[:, par * 8:par * 8 + 4],
                     tmp16[:, par * 8:par * 8 + 4], AF.Exp, scale=-1.0,
                     wait=(tln,), inc="s")
                exp_tok[(h, "rv")] = P.op(
                    "scalar", lambda E: E.nop(), inc="s")

            def rb_pass2(h):
                par = h % 2
                trv = exp_tok[(h, "rv")]
                tr_tok = {}
                for c in range(4):
                    w0 = [trv]
                    if c >= 1:
                        w0.append(tr_tok[c - 1])
                    tA = vts(Abuf[:, 0:129], Eall[:, par * 4 + c, 0:129],
                             rinv16[:, par * 8 + c:par * 8 + c + 1], ALU.mult,
                             wait=tuple(w0), inc="v")
                    a_tok[(h, c)] = tA
                    sl = (c % 2) * 512
                    w1 = [tA]
                    if (h, c - 2) in tr_ev_all:
                        w1 += list(tr_ev_all[(h, c - 2)])
                    elif h > 0 and (h - 1, c + 2) in tr_ev_all:
                        w1 += list(tr_ev_all[(h - 1, c + 2)])
                    mm(PSA[:, sl:sl + 128], Abuf[:, 1:129], I_f[:], True, True,
                       wait=tuple(x for x in w1 if x), tr=True)
                    trt = mm(PSA[0:1, sl + 128:sl + 256], Abuf[:, 0:1],
                             I_f[:], True, True, inc="t", tr=True)
                    tr_tok[c] = trt
                    ta1 = scopy(ATb[:, 0:128], PSA[:, sl:sl + 128],
                                wait=(trt,), inc="s")
                    ta2 = scopy(ATp[0:1, 0:128], PSA[0:1, sl + 128:sl + 256],
                                wait=(trt,), inc="s")
                    tr_ev_all[(h, c)] = (ta1, ta2)
                    sa = 1024 + (c % 2) * 512
                    vt_ = vb[0] if c < 2 else vb[1]
                    vi = c if c < 2 else c - 2
                    w2 = [ta1, ta2, vp_tok[c]]
                    if (h, c - 2) in av_ev:
                        w2 += list(av_ev[(h, c - 2)])
                    elif h > 0 and (h - 1, c + 2) in av_ev:
                        w2 += list(av_ev[(h - 1, c + 2)])
                    mm(PSA[0:64, sa:sa + 128],
                       vt_[:, vi, h * 64:h * 64 + 64],
                       ATb[:, 0:128], True, False, wait=tuple(w2))
                    tokav = mm(PSA[0:64, sa:sa + 128],
                               vprev[0:1, c, h * 64:h * 64 + 64],
                               ATp[0:1, 0:128], False, True, inc="t")
                    p0 = (h % 2) * 64
                    th_ = s_op(lambda E, p_=p0, h_=h, c_=c, sa_=sa:
                               E.activation(
                                   out=cbh_b[p_:p_ + 64, h_ // 2,
                                             c_ * 128:(c_ + 1) * 128],
                                   in_=PSA[0:64, sa_:sa_ + 128],
                                   func=AF.Identity), wait=(tokav,), inc="s")
                    tl_ = v_op(lambda E, p_=p0, h_=h, c_=c, sa_=sa:
                               E.tensor_tensor(
                                   out=cbl_b[p_:p_ + 64, h_ // 2,
                                             c_ * 128:(c_ + 1) * 128],
                                   in0=PSA[0:64, sa_:sa_ + 128],
                                   in1=cbh_b[p_:p_ + 64, h_ // 2,
                                             c_ * 128:(c_ + 1) * 128],
                                   op=ALU.subtract), wait=(tokav, th_),
                               inc="v")
                    av_ev[(h, c)] = (th_, tl_)

            prev_h = None
            for h in range(16):
                rb_pass1(h)
                if prev_h is not None:
                    rb_pass2(prev_h)
                prev_h = h
            rb_pass2(prev_h)
            P.barrier()

            if quarter < 3:
                vcopy(kbound_h[:], kbh[:, :, 511:512].rearrange(
                    "p a b -> p (a b)"), inc="v")
                vcopy(kbound_l[:], kbl[:, :, 511:512].rearrange(
                    "p a b -> p (a b)"), inc="v")
                st(vbound[0:1, :], vb[1][127:128, 1, :])
                P.barrier()

            wo = load_weight("wo_b")
            emit_out_proj(
                wo, lambda hl, kt, tt: (cbh_b if hl == 0 else cbl_b)[
                    :, kt, tt * 128:(tt + 1) * 128], (wa_h, wa_l), 4,
                quarter * 4, brows["bo_b"], src_ap, 0, dst_ap, 0, passes=3,
                ffn_scale=False)
            P.barrier()

    emit_init()
    for s_i in range(n_seq):
        xsrc = x_d[s_i * S:(s_i + 1) * S, :]
        chain = []
        cur_src = xsrc
        for j, sl in enumerate(sublayers):
            is_last = j == len(sublayers) - 1
            dst = y_d[s_i * S:(s_i + 1) * S, :] if is_last else scr_x[j % 3][:, :]
            chain.append((sl, cur_src, dst))
            cur_src = dst
        for sl, src, dst in chain:
            if sl == "a":
                emit_attn_a(src, dst)
            elif sl == "b":
                emit_attn_b(src, dst)
            else:
                emit_ffn(src, dst)
    P.barrier()

    with contextlib.ExitStack() as stack:
        sems = {}
        for name in ("dIn", "dOut", "dW", "t", "v", "s", "g",
                     "m_sync", "m_tensor", "m_vector", "m_scalar", "m_gpsimd"):
            sems[name] = stack.enter_context(nc.semaphore(name))
        block = stack.enter_context(nc.Block())
        P.replay(block, sems)
    assert nc.sbuf_bytes_remaining >= 229344 - 49120, nc.sbuf_bytes_remaining
    return nc


_NC_CACHE = {}


def _get_program():
    key = (SUBLAYERS, N_SEQ)
    if key not in _NC_CACHE:
        _NC_CACHE[key] = build_core_program()
    return _NC_CACHE[key]


def kernel(L0_attn_a_x, L0_attn_a_Wq, L0_attn_a_bq, L0_attn_a_Wk, L0_attn_a_bk,
           L0_attn_a_Wv, L0_attn_a_bv, L0_attn_a_Wo, L0_attn_a_bo,
           L0_attn_b_Wq, L0_attn_b_bq, L0_attn_b_Wk, L0_attn_b_bk,
           L0_attn_b_Wv, L0_attn_b_bv, L0_attn_b_Wo, L0_attn_b_bo,
           L0_attn_b_decay, L0_ffn_ternary_weight, L0_ffn_bias,
           L0_ffn_act_scale):
    nc = _get_program()
    x = np.ascontiguousarray(np.asarray(L0_attn_a_x, dtype=np.float32))
    common = {
        "wq_a": np.ascontiguousarray(L0_attn_a_Wq),
        "wk_a": np.ascontiguousarray(L0_attn_a_Wk),
        "wv_a": np.ascontiguousarray(L0_attn_a_Wv),
        "wo_a": np.ascontiguousarray(L0_attn_a_Wo),
        "wq_b": np.ascontiguousarray(L0_attn_b_Wq),
        "wk_b": np.ascontiguousarray(L0_attn_b_Wk),
        "wv_b": np.ascontiguousarray(L0_attn_b_Wv),
        "wo_b": np.ascontiguousarray(L0_attn_b_Wo),
        "bq_a": np.ascontiguousarray(L0_attn_a_bq.reshape(1, D)),
        "bk_a": np.ascontiguousarray(L0_attn_a_bk.reshape(1, D)),
        "bv_a": np.ascontiguousarray(L0_attn_a_bv.reshape(1, D)),
        "bo_a": np.ascontiguousarray(L0_attn_a_bo.reshape(1, D)),
        "bq_b": np.ascontiguousarray(L0_attn_b_bq.reshape(1, D)),
        "bk_b": np.ascontiguousarray(L0_attn_b_bk.reshape(1, D)),
        "bv_b": np.ascontiguousarray(L0_attn_b_bv.reshape(1, D)),
        "bo_b": np.ascontiguousarray(L0_attn_b_bo.reshape(1, D)),
        "fb": np.ascontiguousarray(L0_ffn_bias.reshape(1, D)),
        "tw": np.ascontiguousarray(L0_ffn_ternary_weight),
        "ascale": np.ascontiguousarray(
            np.asarray(L0_ffn_act_scale, dtype=np.float32).reshape(1, 1)),
        "decay": np.ascontiguousarray(L0_attn_b_decay.reshape(1, 16)),
    }
    in_maps = []
    for c in range(N_CORES):
        m = dict(common)
        m["x"] = np.ascontiguousarray(
            x[c * N_SEQ:(c + 1) * N_SEQ].reshape(N_SEQ * S, D))
        in_maps.append(m)
    res = run_bass_kernel_spmd(nc, in_maps, core_ids=list(range(N_CORES)))
    out = np.concatenate(
        [r["y"].reshape(N_SEQ, S, D) for r in res.results], axis=0)
    return out


# version 12
# speedup vs baseline: 1.0617x; 1.0243x over previous
import contextlib
import numpy as np

import concourse.bass as bass
import concourse.mybir as mybir
import concourse.bass_utils as _bass_utils
from concourse.bass_utils import run_bass_kernel_spmd

_orig_run_command = _bass_utils.run_command


def _run_command_ldwopt(argv, **kwargs):
    argv = [a.replace("--enable-ldw-opt=false", "--enable-ldw-opt=true")
            if isinstance(a, str) else a for a in argv]
    return _orig_run_command(argv, **kwargs)


_bass_utils.run_command = _run_command_ldwopt

F32 = mybir.dt.float32
BF16 = mybir.dt.bfloat16
I32 = mybir.dt.int32
AF = mybir.ActivationFunctionType
ALU = mybir.AluOpType
AX = mybir.AxisListType

N_CORES = 8
N_SEQ = 4
S = 2048
D = 1024
NT = 16
KT = 8
M32C = 12582912.0
M64C = 1.5 * 2.0 ** 52
EPS = 1e-5
SCALE_A = 0.03125
SCALE_B = 0.125
SUBLAYERS = ("a", "f1", "b", "f2")

ENGINES = ("sync", "tensor", "vector", "scalar", "gpsimd")


class Prog:

    def __init__(self):
        self.q = {e: [] for e in ENGINES}
        self.cnt = {}

    def op(self, eng, fn, wait=(), inc=None, amt=1):
        tok = None
        incs = ()
        if inc is not None:
            self.cnt[inc] = self.cnt.get(inc, 0) + amt
            tok = (inc, self.cnt[inc])
            incs = ((inc, self.cnt[inc], amt),)
        self.q[eng].append((tuple(w for w in wait if w is not None), fn, incs))
        return tok

    def barrier(self):
        self.op("sync", lambda E: E.nop(),
                wait=(("dIn", self.cnt.get("dIn", 0)),))
        self.op("gpsimd", lambda E: E.nop(),
                wait=(("dOut", self.cnt.get("dOut", 0)),))
        toks = {}
        for e in ENGINES:
            toks[e] = self.op(e, lambda E: E.nop(), inc="m_" + e)
        for e in ENGINES:
            waits = tuple(toks[o] for o in ENGINES if o != e)
            self.op(e, lambda E: E.nop(), wait=waits)

    def replay(self, block, sems):
        for eng_name in ENGINES:
            items = self.q[eng_name]

            def body(eng, items=items):
                waited = {}
                for waits, fn, incs in items:
                    for s_, v in waits:
                        if waited.get(s_, -1) < v:
                            eng.wait_ge(sems[s_], v)
                            waited[s_] = v
                    inst = fn(eng)
                    for s_, _v, amt in incs:
                        inst.then_inc(sems[s_], amt)
            getattr(block, eng_name)(body)


def build_core_program(sublayers=SUBLAYERS, n_seq=N_SEQ, debug_ffn=False, debug_a=False):
    nc = bass.Bass("TRN2", target_bir_lowering=False,
                   detect_race_conditions=False)
    P = Prog()

    x_d = nc.dram_tensor("x", [n_seq * S, D], F32, kind="ExternalInput").ap()
    y_d = nc.dram_tensor("y", [n_seq * S, D], F32, kind="ExternalOutput").ap()
    W_NAMES = ("wq_a", "wk_a", "wv_a", "wo_a", "wq_b", "wk_b", "wv_b", "wo_b")
    W_IDX = {w: i for i, w in enumerate(W_NAMES)}
    win = {w: nc.dram_tensor(w, [D, D], F32, kind="ExternalInput").ap()
           for w in W_NAMES}
    B_NAMES = ("bq_a", "bk_a", "bv_a", "bo_a", "bq_b", "bk_b", "bv_b", "bo_b", "fb")
    bin_ = {b: nc.dram_tensor(b, [1, D], F32, kind="ExternalInput").ap()
            for b in B_NAMES}
    tw_d = nc.dram_tensor("tw", [D, D], I32, kind="ExternalInput").ap()
    scal_d = nc.dram_tensor("ascale", [1, 1], F32, kind="ExternalInput").ap()
    decay_d = nc.dram_tensor("decay", [1, 16], F32, kind="ExternalInput").ap()

    dbg = {}
    if debug_ffn:
        dbg["xit"] = nc.dram_tensor("dbg_xit", [D, S], BF16,
                                    kind="ExternalOutput").ap()
        dbg["mc"] = nc.dram_tensor("dbg_mc", [128, NT], F32,
                                   kind="ExternalOutput").ap()
        dbg["rc"] = nc.dram_tensor("dbg_rc", [128, NT], F32,
                                   kind="ExternalOutput").ap()
        dbg["sc"] = nc.dram_tensor("dbg_sc", [2, 128], F32,
                                   kind="ExternalOutput").ap()
        dbg["wt"] = nc.dram_tensor("dbg_wt", [D, D], BF16,
                                   kind="ExternalOutput").ap()
        dbg["fbs"] = nc.dram_tensor("dbg_fbs", [1, D], F32,
                                    kind="ExternalOutput").ap()
        dbg["fs"] = nc.dram_tensor("dbg_fs", [4 * 128, D], F32,
                                   kind="ExternalOutput").ap()
        dbg["xqb"] = nc.dram_tensor("dbg_xqb", [128, D], BF16,
                                    kind="ExternalOutput").ap()
        dbg["qf"] = nc.dram_tensor("dbg_qf", [128, 1], F32,
                                   kind="ExternalOutput").ap()
        dbg["cur"] = nc.dram_tensor("dbg_cur", [3, 128], F32,
                                    kind="ExternalOutput").ap()
        dbg["qm1"] = nc.dram_tensor("dbg_qm1", [128, 1], F32,
                                    kind="ExternalOutput").ap()
        dbg["xin1"] = nc.dram_tensor("dbg_xin1", [128, D], F32,
                                     kind="ExternalOutput").ap()
    if debug_a:
        for nm, shp, dt_ in (("a_lth", [D, S], BF16), ("a_ltl", [D, S], BF16),
                             ("a_kh", [D, S], BF16), ("a_kl", [D, S], BF16),
                             ("a_qh", [D, S], BF16), ("a_ql", [D, S], BF16),
                             ("a_at", [2048, 256], F32),
                             ("a_e0", [128, S], F32), ("a_e1", [128, S], F32),
                             ("a_r8", [2, 1024], F32),
                             ("a_cbh", [D, S], BF16), ("a_cbl", [D, S], BF16)):
            dbg[nm] = nc.dram_tensor("dbg_" + nm, shp, dt_,
                                     kind="ExternalOutput").ap()
    scr_w = nc.dram_tensor("scr_w", [16 * D, D], BF16).ap()
    scr_wt = nc.dram_tensor("scr_wt", [D, D], BF16).ap()
    scr_v = nc.dram_tensor("scr_v", [S, D], F32).ap()
    scr_x = [nc.dram_tensor(f"scr_x{i}", [S, D], F32).ap() for i in range(3)]

    def sb(name, shape, dtype, off):
        return nc.alloc_sbuf_tensor_at(name, shape, dtype, offset=off)

    def sbm(name, shape, dtype):
        return nc.alloc_sbuf_tensor(name, shape, dtype)

    ones_row = sbm("ones_row", [1, 128], F32)
    I_f = sbm("I_f", [128, 128], F32)
    I_b = sbm("I_b", [128, 128], BF16)
    iocol = sbm("iocol", [128, 1], F32)
    iof = sbm("iof", [128, 128], F32)
    qflag = sbm("qflag", [128, 1], F32)
    qm1 = sbm("qm1", [128, 1], F32)
    zcol = sbm("zcol", [128, 8], F32)
    zcol_b = sbm("zcol_b", [128, 8], BF16)
    sdf = sbm("sdf", [128, 16], F32)
    bcols = {b: sbm("c_" + b, [128, KT], F32)
             for b in ("bq_a", "bk_a", "bq_b", "bk_b")}
    brows = {b: sbm("r_" + b, [1, D], F32)
             for b in ("bv_a", "bo_a", "bv_b", "bo_b", "fb")}
    brows["fbs"] = brows["fb"]
    s_bc = sbm("s_bc", [128, 1], F32)
    srec_bc = sbm("srec_bc", [128, 1], F32)
    sums_c = sbm("sums_c", [128, NT], F32)
    sumsq_c = sbm("sumsq_c", [128, NT], F32)
    m_c = sbm("m_c", [128, NT], F32)
    r_c = sbm("r_c", [128, NT], F32)
    rs_c = sbm("rs_c", [128, NT], F32)
    var_c = sbm("var_c", [128, NT], F32)
    CURS = ("m", "r", "rs", "bias", "negM", "rv", "sd", "b")
    cur = {k: sbm("cur_" + k, [128, 1], F32) for k in CURS}
    sums8 = sbm("sums8", [128, 8], F32)
    tmp8 = sbm("tmp8", [128, 8], F32)
    rinv8 = sbm("rinv8", [128, 8], F32)
    rinv8x2 = sbm("rinv8x2", [128, 8], F32)
    ln2c = sbm("ln2c", [128, 1], F32)
    sums16 = sbm("sums16", [128, 16], F32)
    tmp16 = sbm("tmp16", [128, 16], F32)
    rinv16 = sbm("rinv16", [128, 16], F32)
    rinv16x2 = sbm("rinv16x2", [128, 16], F32)
    negM8 = sbm("negM8", [128, 8], F32)
    bias8 = sbm("bias8", [128, 8], F32)
    sums8b = sbm("sums8b", [128, 8], F32)
    rinv8b = sbm("rinv8b", [128, 8], F32)
    Dbuf = [sbm(f"Dbuf{i}", [128, 128], F32) for i in range(2)]
    Abuf = sbm("Abuf", [128, 132], F32)
    ATb = sbm("ATb", [128, 132], F32)
    ATp = sbm("ATp", [1, 132], F32)
    kbound_h = sbm("kbound_h", [128, KT], BF16)
    kbound_l = sbm("kbound_l", [128, KT], BF16)
    vbound = sbm("vbound", [1, D], F32)
    lnd_bc = sbm("lnd_bc", [128, 16], F32)
    assert nc.SBUF_PARTITION_SIZE_BYTES - nc.sbuf_bytes_remaining // 1 >= 0

    OFF_XIN = 49120
    OFF_XOUT = 57312
    OFF_WA = 65504
    OFF_BIG1 = 98272
    OFF_BIG2 = 163808
    assert OFF_BIG2 + 64 * 1024 <= 229344

    xin = sb("xin", [128, 2, D], F32, OFF_XIN)
    xout = sb("xout", [128, 2, D], F32, OFF_XOUT)
    STS0 = sb("STS0", [128, D], F32, OFF_WA)
    STS1 = sb("STS1", [128, D], F32, OFF_WA + 4 * 1024)
    FS = [sb(f"FS{i}", [128, D], F32, OFF_WA + i * 4 * 1024) for i in range(4)]
    LNh = sb("LNh", [128, 2, D], BF16, OFF_WA + 8 * 1024)
    LNl = sb("LNl", [128, 2, D], BF16, OFF_WA + 12 * 1024)
    xqb = sb("xqb", [128, D], BF16, OFF_WA)
    wa_h = sb("wa_h", [128, KT, D], BF16, OFF_WA)
    wa_l = sb("wa_l", [128, KT, D], BF16, OFF_WA + 16 * 1024)
    wt_f = sb("wt_f", [128, KT, D], BF16, OFF_WA + 16 * 1024)
    Ebuf = [sb("Ebuf0", [128, S], F32, OFF_WA),
            sb("Ebuf1", [128, S], F32, OFF_WA + 8 * 1024)]
    ATa = sb("ATa", [128, 16, 256], F32, OFF_WA + 16 * 1024)

    LTh = sb("LTh", [128, KT, S], BF16, OFF_BIG1)
    LTl = sb("LTl", [128, KT, S], BF16, OFF_BIG1 + 32 * 1024)
    QSh = sb("QSh", [128, KT, 512], BF16, OFF_XIN)
    QSl = sb("QSl", [128, KT, 512], BF16, OFF_XOUT)
    XIT = sb("XIT", [128, KT, S], BF16, OFF_BIG1)
    LTbh = sb("LTbh", [128, KT, 512], BF16, OFF_BIG1)
    LTbl = sb("LTbl", [128, KT, 512], BF16, OFF_BIG1 + 8 * 1024)
    QSbh = sb("QSbh", [128, KT, 512], BF16, OFF_BIG1 + 16 * 1024)
    QSbl = sb("QSbl", [128, KT, 512], BF16, OFF_BIG1 + 24 * 1024)
    Eall = sb("Eall", [128, 8, 132], F32, OFF_BIG1 + 32 * 1024)
    vprev = sb("vprev", [1, 4, D], F32, OFF_BIG1 + 37 * 1024)

    Kh = sb("Kh", [128, KT, S], BF16, OFF_BIG2)
    Kl = sb("Kl", [128, KT, S], BF16, OFF_BIG2 + 32 * 1024)
    CBh = sb("CBh", [128, KT, S], BF16, OFF_BIG2)
    CBl = sb("CBl", [128, KT, S], BF16, OFF_BIG2 + 32 * 1024)
    kbh = sb("kbh", [128, KT, 512], BF16, OFF_BIG2)
    kbl = sb("kbl", [128, KT, 512], BF16, OFF_BIG2 + 8 * 1024)
    cbh_b = sb("cbh_b", [128, KT, 512], BF16, OFF_BIG2 + 16 * 1024)
    cbl_b = sb("cbl_b", [128, KT, 512], BF16, OFF_BIG2 + 24 * 1024)
    vb = [sb("vb0", [128, 2, D], F32, OFF_XIN),
          sb("vb1", [128, 2, D], F32, OFF_XOUT)]
    WS32 = sb("WS32", [128, KT, D], F32, OFF_BIG1)
    WSI = sb("WSI", [128, KT, D], I32, OFF_BIG1 + 32 * 1024)
    WSH = sb("WSH", [128, KT, D], BF16, OFF_BIG2)
    WSL = sb("WSL", [128, KT, D], BF16, OFF_BIG2 + 16 * 1024)

    PSA = nc.alloc_psum_tensor("PSA", [128, 2048], F32)
    PSB = nc.alloc_psum_tensor("PSB", [128, 512], F32)
    PSC = nc.alloc_psum_tensor("PSC", [128, 512], F32)
    PEA = nc.alloc_psum_tensor("PEA", [128, 1024], BF16)
    PEB = nc.alloc_psum_tensor("PEB", [128, 1024], BF16)

    def ld(out_ap, in_ap, wait=()):
        return P.op("sync", lambda E, o=out_ap, i=in_ap:
                    E.dma_start(out=o, in_=i), wait, inc="dIn", amt=16)

    def st(out_ap, in_ap, wait=()):
        return P.op("gpsimd", lambda E, o=out_ap, i=in_ap:
                    E.dma_start(out=o, in_=i), wait, inc="dOut", amt=16)

    def mm(ps, lhsT, rhs, start, stop, wait=(), inc=None, tp=None, tr=False):
        def f(E, o=ps, l_=lhsT, r_=rhs, st_=start, sp=stop, tp_=tp, t_=tr):
            if t_:
                return E.matmul(out=o, lhsT=l_, rhs=r_, start=st_, stop=sp,
                                tile_position=tp_, is_transpose=True)
            return E.matmul(out=o, lhsT=l_, rhs=r_, start=st_, stop=sp,
                            tile_position=tp_)
        return P.op("tensor", f, wait, inc)

    def v_op(fn, wait=(), inc=None):
        return P.op("vector", fn, wait, inc)

    def s_op(fn, wait=(), inc=None):
        return P.op("scalar", fn, wait, inc)

    def vcopy(dst, src, wait=(), inc=None):
        return v_op(lambda E, d=dst, s_=src: E.tensor_copy(out=d, in_=s_),
                    wait, inc)

    def vts(dst, src, scalar, opc, wait=(), inc=None):
        return v_op(lambda E, d=dst, s_=src, sc=scalar, o=opc:
                    E.tensor_single_scalar(out=d, in_=s_, scalar=sc, op=o),
                    wait, inc)

    def vtt(dst, a, b, opc, wait=(), inc=None):
        return v_op(lambda E, d=dst, a_=a, b_=b, o=opc:
                    E.tensor_tensor(out=d, in0=a_, in1=b_, op=o), wait, inc)

    def vstt(dst, in0, scalar, in1, op0, op1, wait=(), inc=None):
        return v_op(lambda E, d=dst, i0=in0, sc=scalar, i1=in1, o0=op0, o1=op1:
                    E.scalar_tensor_tensor(out=d, in0=i0, scalar=sc, in1=i1,
                                           op0=o0, op1=o1), wait, inc)

    def scopy(dst, src, wait=(), inc=None):
        return s_op(lambda E, d=dst, s_=src: E.copy(out=d, in_=s_), wait, inc)

    def sact(dst, src, func, bias=0.0, scale=1.0, accum=None, wait=(), inc=None):
        return s_op(lambda E, d=dst, s_=src, f=func, b=bias, sc=scale, a=accum:
                    E.activation(out=d, in_=s_, func=f, bias=b, scale=sc,
                                 accum_out=a), wait, inc)

    def w_ap(idx, hilo):
        return scr_w[(2 * idx + hilo) * D:(2 * idx + hilo + 1) * D, :].rearrange(
            "(kt p) d -> p kt d", p=128)

    def load_weight(wname):
        idx = W_IDX[wname]
        t1 = ld(wa_h[:], w_ap(idx, 0))
        t2 = ld(wa_l[:], w_ap(idx, 1))
        return (t1, t2)

    def emit_init():
        P.op("gpsimd", lambda E: E.iota(out=iocol[:], pattern=[[0, 1]], base=0,
                                        channel_multiplier=1,
                                        allow_small_or_imprecise_dtypes=True))
        tg = P.op("gpsimd", lambda E: E.iota(out=iof[:], pattern=[[1, 128]],
                                             base=0, channel_multiplier=0,
                                             allow_small_or_imprecise_dtypes=True),
                  inc="g")
        v_op(lambda E: E.memset(ones_row[:], 1.0))
        v_op(lambda E: E.memset(zcol[:], 0.0))
        tz = v_op(lambda E: E.memset(zcol_b[:], 0.0), inc="v")
        vts(qflag[:], zcol[:, 0:1], 1.0, ALU.add, wait=(tz,))
        vts(ln2c[:], zcol[:, 0:1], 0.6931471805599453, ALU.add)
        v_op(lambda E: E.tensor_scalar(out=qflag[:], in0=qflag[:], scalar1=M64C,
                                       scalar2=-M64C, op0=ALU.add, op1=ALU.add))
        vts(I_f[:], iof[:], iocol[:, 0:1], ALU.is_equal, wait=(tg,))
        vcopy(I_b[:], I_f[:], inc="v")
        v_op(lambda E: E.tensor_scalar(out=qm1[:], in0=qflag[:], scalar1=-1.0,
                                       scalar2=1.0, op0=ALU.mult, op1=ALU.add))

        tb = []
        for b in ("bv_a", "bo_a", "bv_b", "bo_b", "fb"):
            tb.append(ld(brows[b][:], bin_[b][:]))
        prev_ev = None
        for b in bcols:
            trow = ld(vbound[0:1, :], bin_[b][:],
                      wait=(prev_ev,) if prev_ev else ())
            tg2 = None
            for mt in range(KT):
                tg2 = mm(PSC[:, mt:mt + 1], vbound[0:1, mt * 128:(mt + 1) * 128],
                         ones_row[0:1, 0:1], mt == 0, mt == KT - 1,
                         wait=(trow,) if mt == 0 else (),
                         inc="t" if mt == KT - 1 else None)
            prev_ev = scopy(bcols[b][:], PSC[:, 0:KT], wait=(tg2,), inc="s")
        tb.append(prev_ev)
        t_sc = ld(cur["m"][0:1, 0:1], scal_d[:])
        t_dec = ld(lnd_bc[0:1, 0:16], decay_d[:])

        tv = vts(cur["m"][0:1, 0:1], cur["m"][0:1, 0:1], 1e-5, ALU.max,
                 wait=(t_sc,), inc="v")
        tmm = mm(PSB[:, 0:1], ones_row[:, 0:128], cur["m"][0:1, 0:1], True, True,
                 wait=(tv,), inc="t")
        ts1 = scopy(s_bc[:], PSB[:, 0:1], wait=(tmm,), inc="s")
        ts1b = sact(srec_bc[:], s_bc[:], AF.Ln, wait=(ts1,), inc="s")
        ts2 = sact(srec_bc[:], srec_bc[:], AF.Exp, scale=-1.0, wait=(ts1b,),
                   inc="s")
        vts(brows["fb"][:], brows["fb"][:], srec_bc[0:1, 0:1], ALU.mult,
            wait=tuple(tb) + (ts2,), inc="v")

        tl = sact(lnd_bc[0:1, 0:16], lnd_bc[0:1, 0:16], AF.Ln, wait=(t_dec,),
                  inc="s")
        tmm2 = mm(PSB[:, 8:24], ones_row[:, 0:128], lnd_bc[0:1, 0:16], True, True,
                  wait=(tl, ts1), inc="t")
        tl2 = scopy(lnd_bc[:], PSB[:, 8:24], wait=(tmm2,), inc="s")
        tl3 = sact(sdf[:], lnd_bc[:], AF.Exp, scale=iocol[:, 0:1], wait=(tl2,),
                   inc="s")
        vts(sdf[:], sdf[:], SCALE_B, ALU.mult, wait=(tl3,), inc="v")
        P.barrier()

        tld = ld(WSI[:], tw_d.rearrange("(kt p) d -> p kt d", p=128))
        tv1 = vcopy(WS32[:], WSI[:], wait=(tld,), inc="v")
        tv2 = vts(WSH[:], WS32[:], -1.0, ALU.add, wait=(tv1,), inc="v")
        ev_toks = []
        pe_ev = {}
        for okt in range(KT):
            for g in range(2):
                pe = PEA if g == 0 else PEB
                key = (okt * 2 + g) % 2
                tts = None
                for j in range(4):
                    ikt = g * 4 + j
                    w = (tv2,)
                    if okt * 2 + g >= 2:
                        w = (tv2, pe_ev.get(okt * 2 + g - 2))
                    tts = mm(pe[:, j * 128:(j + 1) * 128],
                             WSH[:, okt, ikt * 128:(ikt + 1) * 128], I_b[:],
                             j == 0, j == 3, wait=w,
                             inc="t" if j == 3 else None, tr=True)
                dst = WSL[:, g * 4:(g + 1) * 4, okt * 128:okt * 128 + 128]
                src = pe[:, 0:512].rearrange("p (a b) -> p a b", a=4)
                ev = v_op(lambda E, d=dst, s_=src:
                          E.tensor_copy(out=d, in_=s_), wait=(tts,), inc="v")
                pe_ev[okt * 2 + g] = ev
                ev_toks.append(ev)
        st(scr_wt.rearrange("(kt p) d -> p kt d", p=128), WSL[:],
           wait=tuple(ev_toks))
        P.barrier()

        for w, idx in W_IDX.items():
            tldw = ld(WS32[:], win[w].rearrange("(kt p) d -> p kt d", p=128))
            th = vcopy(WSH[:], WS32[:], wait=(tldw,), inc="v")
            tl_ = v_op(lambda E: E.tensor_sub(out=WSL[:], in0=WS32[:],
                                              in1=WSH[:]), wait=(th,), inc="v")
            sh = st(w_ap(idx, 0), WSH[:], wait=(th,))
            sl = st(w_ap(idx, 1), WSL[:], wait=(tl_,))
            P.op("sync", lambda E: E.nop(), wait=(sh, sl))
        P.barrier()

    def emit_stats(src_ap, row0=0):
        vt = [None] * NT
        stk = [None] * NT
        for t in range(NT):
            slot = t % 2
            w = []
            if t >= 2:
                w = [vt[t - 2], stk[t - 2]]
            l_ = ld(xin[:, slot, :],
                    src_ap[row0 + t * 128:row0 + (t + 1) * 128, :], wait=tuple(w))
            vt[t] = v_op(lambda E, sl=slot, tt=t: E.tensor_scalar(
                out=STS0[:], in0=xin[:, sl, :], scalar1=1.0, scalar2=0.0,
                op0=ALU.mult, op1=ALU.add,
                accum_out=sums_c[:, tt:tt + 1]), wait=(l_,), inc="v")
            stk[t] = s_op(lambda E, sl=slot, tt=t: E.activation(
                out=STS1[:], in_=xin[:, sl, :], func=AF.Square,
                accum_out=sumsq_c[:, tt:tt + 1]), wait=(l_,), inc="s")
        tv1 = vts(m_c[:], sums_c[:], 1.0 / D, ALU.mult,
                  wait=(vt[NT - 1], stk[NT - 1]), inc="v")
        tv2 = vtt(var_c[:], m_c[:], m_c[:], ALU.mult, wait=(tv1,), inc="v")
        tv3 = vstt(var_c[:], sumsq_c[:], 1.0 / D, var_c[:], ALU.mult,
                   ALU.subtract, wait=(tv2,), inc="v")
        tv4 = vts(var_c[:], var_c[:], EPS, ALU.add, wait=(tv3,), inc="v")
        s1 = sact(r_c[:], var_c[:], AF.Ln, wait=(tv4,), inc="s")
        s2 = sact(r_c[:], r_c[:], AF.Exp, scale=-0.5, wait=(s1,), inc="s")
        return s2

    def emit_ln_fm(src_ap, row0, ntiles, tile0, dst_h, dst_l, col0, wtok):
        ev_h = {}
        ev_l = {}
        vtok = {}
        for t in range(ntiles):
            slot = t % 2
            buf = t % 2
            gt = tile0 + t
            w = [wtok]
            if t >= 2:
                w.append(vtok[t - 2])
            l_ = ld(xin[:, slot, :],
                    src_ap[row0 + t * 128:row0 + (t + 1) * 128, :], wait=tuple(w))
            wv = [l_]
            if t >= 2:
                wv += [ev_h[t - 2], ev_l[t - 2]]
            vts(FS[buf][:], xin[:, slot, :], m_c[:, gt:gt + 1], ALU.subtract,
                wait=tuple(wv))
            th = vts(LNh[:, buf, :], FS[buf][:], r_c[:, gt:gt + 1], ALU.mult,
                     inc="v")
            tl_ = vstt(LNl[:, buf, :], FS[buf][:], r_c[:, gt:gt + 1],
                       LNh[:, buf, :], ALU.mult, ALU.subtract, inc="v")
            vtok[t] = tl_
            col = col0 + t * 128
            for (pe, lnbuf, wt, evd, dst) in ((PEA, LNh, th, ev_h, dst_h),
                                              (PEB, LNl, tl_, ev_l, dst_l)):
                tt = None
                for kt in range(KT):
                    w2 = [wt]
                    if t >= 1:
                        w2.append(evd.get(t - 1))
                    tt = mm(pe[:, kt * 128:(kt + 1) * 128],
                            lnbuf[:, buf, kt * 128:(kt + 1) * 128], I_b[:],
                            kt == 0, kt == KT - 1, wait=tuple(x for x in w2 if x),
                            inc="t" if kt == KT - 1 else None, tr=True)
                dstap = dst[:, :, col:col + 128]
                srcap = pe[:, 0:1024].rearrange("p (a b) -> p a b", a=KT)
                if pe is PEA:
                    evd[t] = scopy(dstap, srcap, wait=(tt,), inc="s")
                else:
                    evd[t] = v_op(lambda E, d=dstap, s_=srcap:
                                  E.tensor_copy(out=d, in_=s_), wait=(tt,),
                                  inc="v")
        return (ev_h[ntiles - 1], ev_l[ntiles - 1])

    def proj_fm(wtoks, mov_h, mov_l, ncols, dst_fn, bname, col0=0):
        evt = {}
        nunit = 0
        for ntc in range(ncols // 512):
            for mt in range(KT):
                bank = (PSB, PSC)[nunit % 2]
                w0 = list(wtoks)
                if nunit >= 2:
                    w0 += list(evt[nunit - 2])
                tokmm = None
                for p, (wsel, msel) in enumerate(((wa_h, mov_h), (wa_l, mov_h),
                                                  (wa_h, mov_l))):
                    for kt in range(KT):
                        last = p == 2 and kt == KT - 1
                        tokmm = mm(bank[:, 0:512],
                                   wsel[:, kt, mt * 128:(mt + 1) * 128],
                                   msel[:, kt, ntc * 512:(ntc + 1) * 512],
                                   p == 0 and kt == 0, last,
                                   wait=tuple(w0) if (p == 0 and kt == 0) else (),
                                   inc="t" if last else None)
                hi_ap, lo_ap = dst_fn(mt, col0 + ntc * 512, 512)
                te1 = sact(hi_ap, bank[:, 0:512], AF.Identity,
                           bias=bcols[bname][:, mt:mt + 1], wait=(tokmm,),
                           inc="s")
                te2 = vstt(lo_ap, bank[:, 0:512], bcols[bname][:, mt:mt + 1],
                           hi_ap, ALU.add, ALU.subtract, wait=(tokmm, te1),
                           inc="v")
                evt[nunit] = (te1, te2)
                nunit += 1
        return evt[nunit - 1]

    def proj_fm_pairs(wtoks, mov_h, mov_l, ncols, dst_fn, bname):
        evt = {}
        nunit = 0
        psel = ((wa_h, mov_h), (wa_l, mov_h), (wa_h, mov_l))
        for pair in range(ncols // 1024):
            for mt in range(KT):
                g = nunit % 2
                b0 = PSA[:, g * 1024:g * 1024 + 512]
                b1 = PSA[:, g * 1024 + 512:g * 1024 + 1024]
                w0 = list(wtoks)
                if nunit >= 2:
                    w0 += list(evt[nunit - 2])
                tokmm = None
                for p, (wsel, msel) in enumerate(psel):
                    for kt in range(KT):
                        lhs = wsel[:, kt, mt * 128:(mt + 1) * 128]
                        first = p == 0 and kt == 0
                        last = p == 2 and kt == KT - 1
                        mm(b0, lhs,
                           msel[:, kt, pair * 1024:pair * 1024 + 512],
                           first, last, wait=tuple(w0) if first else ())
                        tokmm = mm(b1, lhs,
                                   msel[:, kt, pair * 1024 + 512:
                                        pair * 1024 + 1024],
                                   first, last,
                                   inc="t" if last else None)
                toks = []
                for j, bk in enumerate((b0, b1)):
                    c0 = pair * 1024 + j * 512
                    hi_ap, lo_ap = dst_fn(mt, c0, 512)
                    te1 = sact(hi_ap, bk, AF.Identity,
                               bias=bcols[bname][:, mt:mt + 1], wait=(tokmm,),
                               inc="s")
                    te2 = vstt(lo_ap, bk, bcols[bname][:, mt:mt + 1],
                               hi_ap, ALU.add, ALU.subtract,
                               wait=(tokmm, te1), inc="v")
                    toks += [te1, te2]
                evt[nunit] = tuple(toks)
                nunit += 1
        return evt

    def proj_tm(first_w, act_fn, wmov, ntiles, brow, evac, passes=3):
        evt = {}
        for tt in range(ntiles):
            g = tt % 2
            b0 = PSA[:, g * 1024:g * 1024 + 512]
            b1 = PSA[:, g * 1024 + 512:g * 1024 + 1024]
            w0 = list(first_w)
            if tt >= 2:
                w0 += list(evt[tt - 2])
            psel = (((0, 0), (1, 0), (0, 1)) if passes == 3 else ((0, 0),))
            first = True
            for (ah, wl_) in psel:
                for kt in range(KT):
                    lhs = act_fn(ah, kt, tt)
                    mm(b0, lhs, wmov[wl_][:, kt, 0:512], first, False,
                       wait=tuple(w0) if first else ())
                    mm(b1, lhs, wmov[wl_][:, kt, 512:1024], first, False)
                    first = False
            mm(b0, ones_row[:, 0:128], brow[0:1, 0:512], False, True)
            tokmm = mm(b1, ones_row[:, 0:128], brow[0:1, 512:1024], False,
                       True, inc="t")
            evt[tt] = tuple(evac(tt, 0, b0, tokmm)) +                 tuple(evac(tt, 1, b1, tokmm))
        return evt

    def emit_attn_a(src_ap, dst_ap):
        stok = emit_stats(src_ap)
        P.barrier()
        emit_ln_fm(src_ap, 0, NT, 0, LTh, LTl, 0, stok)
        P.barrier()
        if debug_a:
            st(dbg["a_lth"].rearrange("(kt p) t -> p kt t", p=128), LTh[:])
            st(dbg["a_ltl"].rearrange("(kt p) t -> p kt t", p=128), LTl[:])
            P.barrier()

        wv = load_weight("wv_a")
        spill_toks = {}

        def evac_v(tt, ntc, bank, tokmm):
            wv2 = []
            if tt >= 2:
                wv2 = [spill_toks[tt - 2]]
            te = scopy(xout[:, tt % 2, ntc * 512:(ntc + 1) * 512], bank,
                       wait=(tokmm,) + tuple(wv2), inc="s")
            if ntc == 1:
                spill_toks[tt] = st(scr_v[tt * 128:(tt + 1) * 128, :],
                                    xout[:, tt % 2, :], wait=(te,))
            return (te,)

        proj_tm(wv, lambda h, kt, tt: (LTh if h == 0 else LTl)[:, kt,
                                                              tt * 128:(tt + 1) * 128],
                (wa_h, wa_l), NT, brows["bv_a"], evac_v, passes=3)
        P.barrier()

        wk = load_weight("wk_a")
        proj_fm_pairs(wk, LTh, LTl, S,
                      lambda mt, c0, w: (Kh[:, mt, c0:c0 + w],
                                         Kl[:, mt, c0:c0 + w]), "bk_a")
        P.barrier()
        if debug_a:
            st(dbg["a_kh"].rearrange("(kt p) t -> p kt t", p=128), Kh[:])
            st(dbg["a_kl"].rearrange("(kt p) t -> p kt t", p=128), Kl[:])
            P.barrier()

        wq = load_weight("wq_a")

        def qdst(mt, c0, w):
            if c0 < 512:
                return (QSh[:, mt, c0:c0 + w], QSl[:, mt, c0:c0 + w])
            return (LTh[:, mt, c0 - 512:c0 - 512 + w],
                    LTl[:, mt, c0 - 512:c0 - 512 + w])

        proj_fm(wq, LTh, LTl, S, qdst, "bq_a")
        P.barrier()
        if debug_a:
            st(dbg["a_qh"][:, 0:512].rearrange("(kt p) t -> p kt t", p=128),
               QSh[:])
            st(dbg["a_ql"][:, 0:512].rearrange("(kt p) t -> p kt t", p=128),
               QSl[:])
            st(dbg["a_qh"][:, 512:S].rearrange("(kt p) t -> p kt t", p=128),
               LTh[:, :, 0:S - 512])
            st(dbg["a_ql"][:, 512:S].rearrange("(kt p) t -> p kt t", p=128),
               LTl[:, :, 0:S - 512])
            P.barrier()

        def q_ap(hilo, kt, c0):
            if c0 < 512:
                t_ = (QSh, QSl)[hilo]
                return t_[:, kt, c0:c0 + 128]
            t_ = (LTh, LTl)[hilo]
            return t_[:, kt, c0 - 512:c0 - 512 + 128]

        at_ev = {}
        dg_tok = {}
        exp_tok = {}
        sm_done = {}

        def p2_scores(idx):
            n, qt = idx // 2, idx % 2
            E = Ebuf[idx % 2]
            par = idx % 2
            c0 = n * 256 + qt * 128
            tsum = None
            for mp in range(4):
                tokmm = None
                for p, hl in enumerate(((0, 0), (1, 0), (0, 1))):
                    qh, kh = hl
                    for kt in range(KT):
                        lhs = q_ap(qh, kt, c0)
                        for m in (2 * mp, 2 * mp + 1):
                            start = p == 0 and kt == 0 and m == 2 * mp
                            stop = p == 2 and kt == KT - 1 and m == 2 * mp + 1
                            w0 = ()
                            if start and idx > 0:
                                w0 = tuple(exp_tok[(idx - 1, mp)])
                            tokmm = mm(
                                PSA[:, m * 256:(m + 1) * 256], lhs,
                                (Kh, Kl)[kh][:, kt, m * 256:(m + 1) * 256],
                                start, stop, wait=w0,
                                inc="t" if stop else None)
                tmx = v_op(lambda E_, mp_=mp: E_.reduce_max(
                    out=negM8[:, mp_:mp_ + 1],
                    in_=PSA[:, mp_ * 512:(mp_ + 1) * 512], axis=AX.X,
                    negate=True), wait=(tokmm,))
                tb_ = v_op(lambda E_, mp_=mp: E_.tensor_single_scalar(
                    out=bias8[:, mp_:mp_ + 1], in_=negM8[:, mp_:mp_ + 1],
                    scalar=SCALE_A, op=ALU.mult), inc="v")
                wexp = [tb_]
                if idx >= 2 and (idx - 2, 7) in dg_tok:
                    wexp.append(dg_tok[(idx - 2, 7)])
                texp = sact(E[:, mp * 512:(mp + 1) * 512],
                            PSA[:, mp * 512:(mp + 1) * 512], AF.Exp,
                            bias=bias8[:, mp:mp + 1], scale=SCALE_A,
                            wait=tuple(wexp), inc="s")
                exp_tok[(idx, mp)] = (tb_, texp)
                tsum = v_op(lambda E_, Ex=E, mp_=mp, pr=par: E_.reduce_sum(
                    out=sums16[:, pr * 8 + 2 * mp_:pr * 8 + 2 * mp_ + 2],
                    in_=Ex[:, mp_ * 512:(mp_ + 1) * 512].rearrange(
                        "p (a b) -> p a b", a=2),
                    axis=AX.X), wait=(texp,), inc="v")
            tln = sact(tmp16[:, par * 8:par * 8 + 8],
                       sums16[:, par * 8:par * 8 + 8], AF.Ln, wait=(tsum,),
                       inc="s")
            trv = sact(rinv16[:, par * 8:par * 8 + 8],
                       tmp16[:, par * 8:par * 8 + 8], AF.Exp, scale=-1.0,
                       wait=(tln,), inc="s")
            trv2 = sact(rinv16x2[:, par * 8:par * 8 + 8],
                        tmp16[:, par * 8:par * 8 + 8], AF.Exp, scale=-1.0,
                        bias=ln2c[:, 0:1], wait=(tln,), inc="s")
            sm_done[idx] = (trv, trv2)

        def p2_diag(idx):
            n, qt = idx // 2, idx % 2
            E = Ebuf[idx % 2]
            par = idx % 2
            for m in range(8):
                wv2 = list(sm_done[idx])
                if m >= 2:
                    wv2.append(dg_tok[(idx, m - 2)])
                elif idx > 0:
                    wv2.append(dg_tok[(idx - 1, m + 6)])
                rsrc = rinv16x2 if m == n else rinv16
                tD = vts(Dbuf[m % 2][:], I_f[:],
                         rsrc[:, par * 8 + m:par * 8 + m + 1], ALU.mult,
                         wait=tuple(wv2), inc="v")
                wmm = [tD]
                if m == 0 and idx > 0:
                    wmm += list(at_ev[idx - 1])
                mm(PSB[:, 0:128], E[:, m * 256:m * 256 + 128], Dbuf[m % 2][:],
                   m == 0, m == 7, wait=tuple(wmm))
                dg_tok[(idx, m)] = mm(PSC[:, 0:128],
                                      E[:, m * 256 + 128:m * 256 + 256],
                                      Dbuf[m % 2][:], m == 0, m == 7, inc="t")
            ta1 = scopy(ATa[:, n * 2, qt * 128:qt * 128 + 128], PSB[:, 0:128],
                        wait=(dg_tok[(idx, 7)],), inc="s")
            ta2 = scopy(ATa[:, n * 2 + 1, qt * 128:qt * 128 + 128],
                        PSC[:, 0:128], wait=(dg_tok[(idx, 7)],), inc="s")
            at_ev[idx] = (ta1, ta2)

        prev_idx = None
        for idx in range(16):
            p2_scores(idx)
            if prev_idx is not None:
                p2_diag(prev_idx)
            prev_idx = idx
        p2_diag(prev_idx)
        P.barrier()
        if debug_a:
            st(dbg["a_at"].rearrange("(a p) q -> p a q", p=128), ATa[:])
            st(dbg["a_e0"][:], Ebuf[0][:])
            st(dbg["a_e1"][:], Ebuf[1][:])
            st(dbg["a_r8"][0:1, :].rearrange("one (p w) -> (one p) w", w=8),
               rinv8[:])
            st(dbg["a_r8"][1:2, :].rearrange("one (p w) -> (one p) w", w=8),
               sums8[:])
            P.barrier()

        p3banks = (PSB[:, 0:256], PSC[:, 0:256], PSA[:, 0:256],
                   PSA[:, 512:768], PSA[:, 1024:1280], PSA[:, 1536:1792])
        vload = {}
        avtok = {}
        evtok = {}
        nunit = 0
        for n in range(8):
            vrb = (xin, xout)[n % 2]
            w0 = []
            if n >= 2:
                w0 = [avtok[n - 2]]
            vload[n] = (
                ld(vrb[:, 0, :], scr_v[n * 256:n * 256 + 128, :],
                   wait=tuple(w0)),
                ld(vrb[:, 1, :], scr_v[n * 256 + 128:n * 256 + 256, :],
                   wait=tuple(w0)))
            for et in range(KT):
                bank = p3banks[nunit % 6]
                w1 = list(vload[n])
                if nunit >= 6:
                    w1 += list(evtok[nunit - 6])
                mm(bank, vrb[:, 0, et * 128:(et + 1) * 128],
                   ATa[:, n * 2, 0:256], True, False, wait=tuple(w1))
                tokav = mm(bank, vrb[:, 1, et * 128:(et + 1) * 128],
                           ATa[:, n * 2 + 1, 0:256], False, True, inc="t")
                if et == KT - 1:
                    avtok[n] = tokav
                th = sact(CBh[:, et, n * 256:(n + 1) * 256], bank,
                          AF.Identity, wait=(tokav,), inc="s")
                tl_ = vtt(CBl[:, et, n * 256:(n + 1) * 256], bank,
                          CBh[:, et, n * 256:(n + 1) * 256], ALU.subtract,
                          wait=(tokav, th), inc="v")
                evtok[nunit] = (th, tl_)
                nunit += 1
        P.barrier()
        if debug_a:
            st(dbg["a_cbh"].rearrange("(kt p) t -> p kt t", p=128), CBh[:])
            st(dbg["a_cbl"].rearrange("(kt p) t -> p kt t", p=128), CBl[:])
            P.barrier()

        wo = load_weight("wo_a")
        emit_out_proj(wo, lambda h, kt, tt: (CBh if h == 0 else CBl)[
            :, kt, tt * 128:(tt + 1) * 128], (wa_h, wa_l), NT, 0,
            brows["bo_a"], src_ap, 0, dst_ap, 0, passes=3, ffn_scale=False)
        P.barrier()

    def emit_out_proj(first_w, act_fn, wmov, ntiles, tile0, brow,
                      res_ap, res_row0, dst_ap, dst_row0, passes, ffn_scale):
        res_ld = {}
        st_tok = {}

        def evac(tt, ntc, bank, tokmm):
            if ntc == 0 and tt not in res_ld:
                w = []
                if tt >= 2:
                    w = [st_tok[tt - 2]]
                res_ld[tt] = ld(
                    xin[:, tt % 2, :],
                    res_ap[res_row0 + (tile0 + tt) * 128:
                           res_row0 + (tile0 + tt + 1) * 128, :], wait=tuple(w))
            sl = tt % 2
            wst = (st_tok.get(tt - 2),) if tt >= 2 else ()
            if ffn_scale:
                te = vstt(xout[:, sl, ntc * 512:(ntc + 1) * 512], bank,
                          s_bc[:, 0:1], xin[:, sl, ntc * 512:(ntc + 1) * 512],
                          ALU.mult, ALU.add,
                          wait=(tokmm, res_ld[tt]) + wst, inc="v")
            else:
                te = vtt(xout[:, sl, ntc * 512:(ntc + 1) * 512], bank,
                         xin[:, sl, ntc * 512:(ntc + 1) * 512], ALU.add,
                         wait=(tokmm, res_ld[tt]) + wst, inc="v")
            if ntc == 1:
                st_tok[tt] = st(
                    dst_ap[dst_row0 + (tile0 + tt) * 128:
                           dst_row0 + (tile0 + tt + 1) * 128, :],
                    xout[:, sl, :], wait=(te,))
            return (te,)

        proj_tm(first_w, act_fn, wmov, ntiles, brow, evac, passes=passes)

    def emit_ffn(src_ap, dst_ap):
        stok = emit_stats(src_ap)
        twt = ld(wt_f[:], scr_wt.rearrange("(kt p) d -> p kt d", p=128))
        stok = vts(rs_c[:], r_c[:], srec_bc[:, 0:1], ALU.mult, wait=(stok,),
                   inc="v")
        P.barrier()
        pe_tok = {}
        ev_tok = {}
        for t in range(NT):
            slot = t % 2
            w = [stok]
            if t >= 2:
                w.append(pe_tok[t - 2])
            l_ = ld(xin[:, slot, :], src_ap[t * 128:(t + 1) * 128, :],
                    wait=tuple(w))
            wv = [l_]
            if t >= 1:
                wv.append(pe_tok[t - 1])
            vts(FS[0][:], xin[:, slot, :], m_c[:, t:t + 1], ALU.subtract,
                wait=tuple(wv))
            vts(FS[1][:], FS[0][:], rs_c[:, t:t + 1], ALU.mult)
            v_op(lambda E: E.tensor_scalar(out=FS[2][:], in0=FS[1][:],
                                           scalar1=M32C, scalar2=-M32C,
                                           op0=ALU.add, op1=ALU.add))
            v_op(lambda E: E.tensor_scalar(out=FS[3][:], in0=FS[1][:],
                                           scalar1=M64C, scalar2=-M64C,
                                           op0=ALU.add, op1=ALU.add))
            vts(FS[0][:], FS[2][:], qm1[:, 0:1], ALU.mult)
            vtt(FS[1][:], FS[0][:], FS[3][:], ALU.add)
            tq = v_op(lambda E: E.tensor_scalar(out=xqb[:],
                                                in0=FS[1][:], scalar1=127.0,
                                                scalar2=-127.0, op0=ALU.min,
                                                op1=ALU.max), inc="v")
            pe = (PEA, PEB)[t % 2]
            tt_ = None
            for kt in range(KT):
                w2 = [tq]
                if t >= 2:
                    w2.append(ev_tok[t - 2])
                tt_ = mm(pe[:, kt * 128:(kt + 1) * 128],
                         xqb[:, kt * 128:(kt + 1) * 128], I_b[:],
                         kt == 0, kt == KT - 1, wait=tuple(w2),
                         inc="t" if kt == KT - 1 else None, tr=True)
            pe_tok[t] = tt_
            ev_tok[t] = scopy(XIT[:, :, t * 128:(t + 1) * 128],
                              pe[:, 0:1024].rearrange("p (a b) -> p a b", a=KT),
                              wait=(tt_,), inc="s")
        P.barrier()
        if debug_ffn:
            st(dbg["xit"].rearrange("(kt p) t -> p kt t", p=128), XIT[:])
            st(dbg["mc"][:], m_c[:])
            st(dbg["rc"][:], r_c[:])
            st(dbg["sc"][0:1, :].rearrange("one (p w) -> (one p) w", w=1),
               s_bc[:])
            st(dbg["sc"][1:2, :].rearrange("one (p w) -> (one p) w", w=1),
               srec_bc[:])
            st(dbg["wt"].rearrange("(kt p) t -> p kt t", p=128), wt_f[:])
            st(dbg["fbs"][:], brows["fbs"][:])
            for i_ in range(4):
                st(dbg["fs"][i_ * 128:(i_ + 1) * 128, :], FS[i_][:])
            st(dbg["xqb"][:], xqb[:])
            st(dbg["qf"][:], qflag[:])
            st(dbg["qm1"][:], qm1[:])
            st(dbg["cur"][0:1, :].rearrange("one (p w) -> (one p) w", w=1),
               cur["m"][:])
            st(dbg["cur"][1:2, :].rearrange("one (p w) -> (one p) w", w=1),
               cur["r"][:])
            st(dbg["cur"][2:3, :].rearrange("one (p w) -> (one p) w", w=1),
               cur["rs"][:])
            st(dbg["xin1"][:], xin[:, 1, :])
            P.barrier()
        emit_out_proj((twt, ev_tok[NT - 1]),
                      lambda h, kt, tt: XIT[:, kt, tt * 128:(tt + 1) * 128],
                      (wt_f, wt_f), NT, 0, brows["fbs"], src_ap, 0, dst_ap, 0,
                      passes=1, ffn_scale=True)
        P.barrier()

    def emit_attn_b(src_ap, dst_ap):
        stok = emit_stats(src_ap)
        P.barrier()
        for quarter in range(4):
            row0 = quarter * 512
            emit_ln_fm(src_ap, row0, 4, quarter * 4, LTbh, LTbl, 0, stok)
            P.barrier()

            wv = load_weight("wv_b")
            vp_tok = {}

            def evac_vb(tt, ntc, bank, tokmm):
                dst = vb[0][:, tt, :] if tt < 2 else vb[1][:, tt - 2, :]
                te = scopy(dst[:, ntc * 512:(ntc + 1) * 512], bank,
                           wait=(tokmm,), inc="s")
                if ntc == 1 and tt < 3:
                    src_t = vb[0] if tt < 2 else vb[1]
                    si = tt if tt < 2 else tt - 2
                    vp_tok[tt + 1] = st(vprev[0:1, tt + 1, :],
                                        src_t[127:128, si, :], wait=(te,))
                return (te,)

            proj_tm(wv, lambda h, kt, tt: (LTbh if h == 0 else LTbl)[
                :, kt, tt * 128:(tt + 1) * 128], (wa_h, wa_l), 4,
                brows["bv_b"], evac_vb, passes=3)
            if quarter == 0:
                vp_tok[0] = v_op(lambda E: E.memset(vprev[0:1, 0, :], 0.0),
                                 inc="v")
            else:
                vp_tok[0] = v_op(lambda E: E.tensor_copy(out=vprev[0:1, 0, :],
                                                         in_=vbound[0:1, :]),
                                 inc="v")
            P.barrier()

            wk = load_weight("wk_b")
            proj_fm(wk, LTbh, LTbl, 512,
                    lambda mt, c0, w: (kbh[:, mt, c0:c0 + w],
                                       kbl[:, mt, c0:c0 + w]), "bk_b")
            P.barrier()

            wq = load_weight("wq_b")
            proj_fm(wq, LTbh, LTbl, 512,
                    lambda mt, c0, w: (QSbh[:, mt, c0:c0 + w],
                                       QSbl[:, mt, c0:c0 + w]), "bq_b")
            P.barrier()

            def qb_ap(hilo, h, c):
                t_ = (QSbh, QSbl)[hilo]
                p0 = (h % 2) * 64
                return t_[p0:p0 + 64, h // 2, c * 128:(c + 1) * 128]

            def kb_ap(hilo, h, c0, c1):
                t_ = (kbh, kbl)[hilo]
                p0 = (h % 2) * 64
                return t_[p0:p0 + 64, h // 2, c0:c1]

            exp_tok = {}
            av_ev = {}
            tr_ev_all = {}
            a_tok = {}
            sd_tok = {}

            def rb_pass1(h):
                tp = (64, 0) if h % 2 == 1 else None
                par = h % 2
                wsd = [exp_tok.get((h - 1, 3))] if h > 0 else []
                tsd = vcopy(cur["sd"][:], sdf[:, h:h + 1], wait=tuple(wsd),
                            inc="v")
                sd_tok[h] = tsd
                tsum = None
                for c in range(4):
                    bank = (PSB, PSC)[c % 2]
                    w0 = [tsd]
                    if (h, c - 2) in exp_tok:
                        w0.append(exp_tok[(h, c - 2)])
                    elif h > 0 and c < 2:
                        w0.append(exp_tok[(h - 1, c + 2)])
                    first = True
                    for p, hl in enumerate(((0, 0), (1, 0), (0, 1))):
                        qh, kh = hl
                        if c == 0 and quarter == 0:
                            kprev = zcol_b[(h % 2) * 64:(h % 2) * 64 + 64, 0:1]
                        elif c == 0:
                            kprev = (kbound_h, kbound_l)[kh][
                                (h % 2) * 64:(h % 2) * 64 + 64,
                                h // 2:h // 2 + 1]
                        else:
                            kprev = kb_ap(kh, h, c * 128 - 1, c * 128)
                        mm(bank[:, 0:1], qb_ap(qh, h, c), kprev,
                           first, False, wait=tuple(w0) if first else (),
                           tp=tp)
                        first = False
                    tokmm = None
                    for p, hl in enumerate(((0, 0), (1, 0), (0, 1))):
                        qh, kh = hl
                        tokmm = mm(bank[:, 1:129], qb_ap(qh, h, c),
                                   kb_ap(kh, h, c * 128, (c + 1) * 128),
                                   False, p == 2, tp=tp,
                                   inc="t" if p == 2 else None)
                    sl8 = par * 4 + c
                    v_op(lambda E, b=bank, s_=sl8: E.reduce_max(
                        out=negM8[:, s_:s_ + 1], in_=b[:, 0:129], axis=AX.X,
                        negate=True), wait=(tokmm,))
                    tb_ = v_op(lambda E, s_=sl8: E.tensor_single_scalar(
                        out=bias8[:, s_:s_ + 1], in_=negM8[:, s_:s_ + 1],
                        scalar=cur["sd"][:, 0:1], op=ALU.mult), inc="v")
                    wexp = [tb_]
                    if h >= 2 and (h - 2, 3) in a_tok:
                        wexp.append(a_tok[(h - 2, 3)])
                    texp = sact(Eall[:, par * 4 + c, 0:129], bank[:, 0:129],
                                AF.Exp, bias=bias8[:, sl8:sl8 + 1],
                                scale=cur["sd"][:, 0:1], wait=tuple(wexp),
                                inc="s")
                    exp_tok[(h, c)] = texp
                    tsum = v_op(lambda E, c_=c, pr=par: E.reduce_sum(
                        out=sums16[:, pr * 8 + c_:pr * 8 + c_ + 1],
                        in_=Eall[:, pr * 4 + c_, 0:129],
                        axis=AX.X), wait=(texp,), inc="v")
                tln = sact(tmp16[:, par * 8:par * 8 + 4],
                           sums16[:, par * 8:par * 8 + 4], AF.Ln,
                           wait=(tsum,), inc="s")
                sact(rinv16[:, par * 8:par * 8 + 4],
                     tmp16[:, par * 8:par * 8 + 4], AF.Exp, scale=-1.0,
                     wait=(tln,), inc="s")
                exp_tok[(h, "rv")] = P.op(
                    "scalar", lambda E: E.nop(), inc="s")

            def rb_pass2(h):
                par = h % 2
                trv = exp_tok[(h, "rv")]
                tr_tok = {}
                for c in range(4):
                    w0 = [trv]
                    if c >= 1:
                        w0.append(tr_tok[c - 1])
                    tA = vts(Abuf[:, 0:129], Eall[:, par * 4 + c, 0:129],
                             rinv16[:, par * 8 + c:par * 8 + c + 1], ALU.mult,
                             wait=tuple(w0), inc="v")
                    a_tok[(h, c)] = tA
                    sl = (c % 2) * 512
                    w1 = [tA]
                    if (h, c - 2) in tr_ev_all:
                        w1 += list(tr_ev_all[(h, c - 2)])
                    elif h > 0 and (h - 1, c + 2) in tr_ev_all:
                        w1 += list(tr_ev_all[(h - 1, c + 2)])
                    mm(PSA[:, sl:sl + 128], Abuf[:, 1:129], I_f[:], True, True,
                       wait=tuple(x for x in w1 if x), tr=True)
                    trt = mm(PSA[0:1, sl + 128:sl + 256], Abuf[:, 0:1],
                             I_f[:], True, True, inc="t", tr=True)
                    tr_tok[c] = trt
                    ta1 = scopy(ATb[:, 0:128], PSA[:, sl:sl + 128],
                                wait=(trt,), inc="s")
                    ta2 = scopy(ATp[0:1, 0:128], PSA[0:1, sl + 128:sl + 256],
                                wait=(trt,), inc="s")
                    tr_ev_all[(h, c)] = (ta1, ta2)
                    sa = 1024 + (c % 2) * 512
                    vt_ = vb[0] if c < 2 else vb[1]
                    vi = c if c < 2 else c - 2
                    w2 = [ta1, ta2, vp_tok[c]]
                    if (h, c - 2) in av_ev:
                        w2 += list(av_ev[(h, c - 2)])
                    elif h > 0 and (h - 1, c + 2) in av_ev:
                        w2 += list(av_ev[(h - 1, c + 2)])
                    mm(PSA[0:64, sa:sa + 128],
                       vt_[:, vi, h * 64:h * 64 + 64],
                       ATb[:, 0:128], True, False, wait=tuple(w2))
                    tokav = mm(PSA[0:64, sa:sa + 128],
                               vprev[0:1, c, h * 64:h * 64 + 64],
                               ATp[0:1, 0:128], False, True, inc="t")
                    p0 = (h % 2) * 64
                    th_ = s_op(lambda E, p_=p0, h_=h, c_=c, sa_=sa:
                               E.activation(
                                   out=cbh_b[p_:p_ + 64, h_ // 2,
                                             c_ * 128:(c_ + 1) * 128],
                                   in_=PSA[0:64, sa_:sa_ + 128],
                                   func=AF.Identity), wait=(tokav,), inc="s")
                    tl_ = v_op(lambda E, p_=p0, h_=h, c_=c, sa_=sa:
                               E.tensor_tensor(
                                   out=cbl_b[p_:p_ + 64, h_ // 2,
                                             c_ * 128:(c_ + 1) * 128],
                                   in0=PSA[0:64, sa_:sa_ + 128],
                                   in1=cbh_b[p_:p_ + 64, h_ // 2,
                                             c_ * 128:(c_ + 1) * 128],
                                   op=ALU.subtract), wait=(tokav, th_),
                               inc="v")
                    av_ev[(h, c)] = (th_, tl_)

            prev_h = None
            for h in range(16):
                rb_pass1(h)
                if prev_h is not None:
                    rb_pass2(prev_h)
                prev_h = h
            rb_pass2(prev_h)
            P.barrier()

            if quarter < 3:
                vcopy(kbound_h[:], kbh[:, :, 511:512].rearrange(
                    "p a b -> p (a b)"), inc="v")
                vcopy(kbound_l[:], kbl[:, :, 511:512].rearrange(
                    "p a b -> p (a b)"), inc="v")
                st(vbound[0:1, :], vb[1][127:128, 1, :])
                P.barrier()

            wo = load_weight("wo_b")
            emit_out_proj(wo, lambda hl, kt, tt: (cbh_b if hl == 0 else cbl_b)[
                :, kt, tt * 128:(tt + 1) * 128], (wa_h, wa_l), 4, quarter * 4,
                brows["bo_b"], src_ap, 0, dst_ap, 0, passes=3, ffn_scale=False)
            P.barrier()

    emit_init()
    for s_i in range(n_seq):
        xsrc = x_d[s_i * S:(s_i + 1) * S, :]
        chain = []
        cur_src = xsrc
        for j, sl in enumerate(sublayers):
            is_last = j == len(sublayers) - 1
            dst = y_d[s_i * S:(s_i + 1) * S, :] if is_last else scr_x[j % 3][:, :]
            chain.append((sl, cur_src, dst))
            cur_src = dst
        for sl, src, dst in chain:
            if sl == "a":
                emit_attn_a(src, dst)
            elif sl == "b":
                emit_attn_b(src, dst)
            else:
                emit_ffn(src, dst)
    P.barrier()

    with contextlib.ExitStack() as stack:
        sems = {}
        for name in ("dIn", "dOut", "t", "v", "s", "g",
                     "m_sync", "m_tensor", "m_vector", "m_scalar", "m_gpsimd"):
            sems[name] = stack.enter_context(nc.semaphore(name))
        block = stack.enter_context(nc.Block())
        P.replay(block, sems)
    assert nc.sbuf_bytes_remaining >= 229344 - 49120, nc.sbuf_bytes_remaining
    return nc


_NC_CACHE = {}


def _get_program():
    key = (SUBLAYERS, N_SEQ)
    if key not in _NC_CACHE:
        _NC_CACHE[key] = build_core_program()
    return _NC_CACHE[key]


def kernel(L0_attn_a_x, L0_attn_a_Wq, L0_attn_a_bq, L0_attn_a_Wk, L0_attn_a_bk,
           L0_attn_a_Wv, L0_attn_a_bv, L0_attn_a_Wo, L0_attn_a_bo,
           L0_attn_b_Wq, L0_attn_b_bq, L0_attn_b_Wk, L0_attn_b_bk,
           L0_attn_b_Wv, L0_attn_b_bv, L0_attn_b_Wo, L0_attn_b_bo,
           L0_attn_b_decay, L0_ffn_ternary_weight, L0_ffn_bias,
           L0_ffn_act_scale):
    nc = _get_program()
    x = np.ascontiguousarray(np.asarray(L0_attn_a_x, dtype=np.float32))
    common = {
        "wq_a": np.ascontiguousarray(L0_attn_a_Wq),
        "wk_a": np.ascontiguousarray(L0_attn_a_Wk),
        "wv_a": np.ascontiguousarray(L0_attn_a_Wv),
        "wo_a": np.ascontiguousarray(L0_attn_a_Wo),
        "wq_b": np.ascontiguousarray(L0_attn_b_Wq),
        "wk_b": np.ascontiguousarray(L0_attn_b_Wk),
        "wv_b": np.ascontiguousarray(L0_attn_b_Wv),
        "wo_b": np.ascontiguousarray(L0_attn_b_Wo),
        "bq_a": np.ascontiguousarray(L0_attn_a_bq.reshape(1, D)),
        "bk_a": np.ascontiguousarray(L0_attn_a_bk.reshape(1, D)),
        "bv_a": np.ascontiguousarray(L0_attn_a_bv.reshape(1, D)),
        "bo_a": np.ascontiguousarray(L0_attn_a_bo.reshape(1, D)),
        "bq_b": np.ascontiguousarray(L0_attn_b_bq.reshape(1, D)),
        "bk_b": np.ascontiguousarray(L0_attn_b_bk.reshape(1, D)),
        "bv_b": np.ascontiguousarray(L0_attn_b_bv.reshape(1, D)),
        "bo_b": np.ascontiguousarray(L0_attn_b_bo.reshape(1, D)),
        "fb": np.ascontiguousarray(L0_ffn_bias.reshape(1, D)),
        "tw": np.ascontiguousarray(L0_ffn_ternary_weight),
        "ascale": np.ascontiguousarray(
            np.asarray(L0_ffn_act_scale, dtype=np.float32).reshape(1, 1)),
        "decay": np.ascontiguousarray(L0_attn_b_decay.reshape(1, 16)),
    }
    in_maps = []
    for c in range(N_CORES):
        m = dict(common)
        m["x"] = np.ascontiguousarray(
            x[c * N_SEQ:(c + 1) * N_SEQ].reshape(N_SEQ * S, D))
        in_maps.append(m)
    res = run_bass_kernel_spmd(nc, in_maps, core_ids=list(range(N_CORES)))
    out = np.concatenate(
        [r["y"].reshape(N_SEQ, S, D) for r in res.results], axis=0)
    return out
